# version 07
# baseline (speedup 1.0000x reference)
import os
from collections import defaultdict

import numpy as np
import concourse.bass as bass
import concourse.mybir as mybir
from concourse.bass_utils import run_bass_kernel_spmd

F32 = mybir.dt.float32
AF = mybir.ActivationFunctionType
ALU = mybir.AluOpType
AX = mybir.AxisListType

HID = 2048
HK = 8
HV = 16
DK = 128
DV = 128
KCONV = 4
KEY_DIM = HK * DK
VAL_DIM = HV * DV
CONV_DIM = 2 * KEY_DIM + VAL_DIM
RMS_EPS = 1e-6
L2_EPS = 1e-6
LN_EPS = 1e-5
NEXP = 8
FF = 1408
FS = 2816
NKC = HID // 128

SPAN = 1024
PREF = 3072
PTILE = 512
NPT = PREF // PTILE
PCOLS = PTILE + 4
QTOK = 256
NQ = SPAN // QTOK
QCOLS = QTOK + 4
CH = 128

NCORES = 8



class Buf:
    __slots__ = ("writer", "readers", "name")
    def __init__(self, name=""):
        self.writer = None
        self.readers = []
        self.name = name


class PH:
    __slots__ = ("eng", "tok")
    def __init__(self, eng):
        self.eng = eng
        self.tok = None


class Sched:
    ENGINES = ("tensor", "vector", "scalar", "gpsimd", "sync")
    KSEM = 4

    def __init__(self):
        self.ops = {e: [] for e in self.ENGINES}
        self.counts = defaultdict(int)
        self.opidx = defaultdict(int)
        self.waiters = defaultdict(list)
        self.pending_ph = {e: [] for e in self.ENGINES}
        self._rr = 0

    def _force_token(self, eng):
        return self.op(eng, lambda e: e.nop(), reads=(), writes=(), dma=False,
                       extra=(), quiet=False, _force=True)

    def _resolve(self, d, eng):
        if not isinstance(d, PH):
            return d
        if d.tok is not None:
            return d.tok
        if d.eng == eng:
            return None
        return self._force_token(d.eng)

    def op(self, eng, fn, reads=(), writes=(), dma=False, extra=(), quiet=False,
           _force=False):
        deps = []
        for d in extra:
            if d is None:
                continue
            d = self._resolve(d, eng)
            if d is not None:
                deps.append(d)
        for b in reads:
            w = self._resolve(b.writer, eng) if b.writer is not None else None
            if w is not None:
                deps.append(w)
        for b in writes:
            w = self._resolve(b.writer, eng) if b.writer is not None else None
            if w is not None:
                deps.append(w)
            for r in list(b.readers):
                r = self._resolve(r, eng)
                if r is not None:
                    deps.append(r)
        if quiet and not writes and not dma:
            ph = PH(eng)
            self.ops[eng].append((fn, deps, None, dma))
            for (s, v) in deps:
                self.waiters[s].append((v, ph))
            for b in reads:
                b.readers.append(ph)
            self.pending_ph[eng].append(ph)
            return ph
        kind = "d" if dma else "c"
        idx = self.opidx[(eng, kind)]
        self.opidx[(eng, kind)] = idx + 1
        sem = f"{eng}_{kind}{idx % self.KSEM}"
        inc = 16 if dma else 1
        start_val = self.counts[sem]
        wl = self.waiters[sem]
        if wl:
            keep = []
            for (v, wtok) in wl:
                if v == start_val:
                    if isinstance(wtok, PH):
                        if wtok.tok is not None:
                            deps.append(wtok.tok)
                        elif wtok.eng != eng:
                            raise AssertionError("unresolved waiter placeholder")
                    else:
                        deps.append(wtok)
                elif v > start_val:
                    keep.append((v, wtok))
            self.waiters[sem] = keep
        self.counts[sem] = start_val + inc
        tok = (sem, start_val + inc)
        self.ops[eng].append((fn, deps, tok, dma))
        for (s, v) in deps:
            self.waiters[s].append((v, tok))
        for ph in self.pending_ph[eng]:
            ph.tok = tok
        self.pending_ph[eng].clear()
        for b in reads:
            b.readers.append(tok)
        for b in writes:
            b.writer = tok
            b.readers = []
        return tok

    def ev(self):
        self._rr += 1
        return "vector" if self._rr % 2 else "scalar"

    def emit(self, nc):
        sems = {}
        for e in self.ENGINES:
            for kind in ("c", "d"):
                for i in range(self.KSEM):
                    name = f"{e}_{kind}{i}"
                    sems[name] = nc.semaphore(f"S_{name}").__enter__()
        blk = nc.Block()
        block = blk.__enter__()

        def mk(eng_name):
            own = eng_name + "_c" if eng_name == "tensor" else "\0"
            def body(eng):
                watermark = defaultdict(int)
                for fn, deps, tok, dma in self.ops[eng_name]:
                    need = {}
                    for (sem, val) in deps:
                        if sem.startswith(own):
                            continue
                        if val > watermark[sem] and val > need.get(sem, 0):
                            need[sem] = val
                    for sem, val in need.items():
                        eng.wait_ge(sems[sem], val)
                        watermark[sem] = val
                    inst = fn(eng)
                    if tok is not None:
                        inst.then_inc(sems[tok[0]], 16 if dma else 1)
            return body

        block.sync(mk("sync"))
        block.tensor(mk("tensor"))
        block.vector(mk("vector"))
        block.scalar(mk("scalar"))
        block.gpsimd(mk("gpsimd"))
        blk.__exit__(None, None, None)



class Bld:

    def __init__(self, arena_cols=51500):
        self.nc = bass.Bass("TRN2", target_bir_lowering=False)
        self.S = Sched()
        self.AR = self.nc.alloc_sbuf_tensor("AR", [128, arena_cols], F32).ap()
        self.PS = [self.nc.alloc_psum_tensor(f"PB{i}", [128, 512], F32).ap() for i in range(8)]
        self.psb = [Buf(f"ps{i}") for i in range(8)]
        self._cur = 0
        self._arena_cols = arena_cols
        self.bufs = {}
        self.marks = []

    def alloc(self, n, name=""):
        o = self._cur
        self._cur += n
        assert self._cur <= self._arena_cols, f"arena overflow at {name}: {self._cur}"
        return o

    def push(self):
        self.marks.append(self._cur)

    def pop(self):
        self._cur = self.marks.pop()

    def V(self, off, parts, cols):
        return self.AR[0:parts, off:off + cols]

    def VR(self, off, p0, p1, cols):
        return self.AR[p0:p1, off:off + cols]

    def B(self, key):
        if key not in self.bufs:
            self.bufs[key] = Buf(str(key))
        return self.bufs[key]

    def dma(self, dst, src, reads=(), writes=(), eng="sync"):
        return self.S.op(eng, lambda e: e.dma_start(out=dst, in_=src),
                         reads=reads, writes=writes, dma=True)

    def mm(self, bank, n, lhsT, rhs, start, stop, reads=(), c0=0, parts=128):
        ps = self.PS[bank][0:parts, c0:c0 + n]
        extra = ()
        if start:
            extra = tuple(self.psb[bank].readers)
            if self.psb[bank].writer is not None:
                extra += (self.psb[bank].writer,)
        return self.S.op("tensor",
                         lambda e: e.matmul(ps, lhsT, rhs, start=start, stop=stop),
                         reads=reads,
                         writes=(self.psb[bank],) if stop else (), extra=extra,
                         quiet=False)

    def trans(self, bank, out_parts, out_cols, in_, ident, reads=(), c0=0):
        ps = self.PS[bank][0:out_parts, c0:c0 + out_cols]
        extra = tuple(self.psb[bank].readers)
        if self.psb[bank].writer is not None:
            extra += (self.psb[bank].writer,)
        return self.S.op("tensor", lambda e: e.transpose(ps, in_, ident),
                         reads=reads, writes=(self.psb[bank],), extra=extra)

    def vo(self, fn, reads=(), writes=(), eng="vector"):
        return self.S.op(eng, fn, reads=reads, writes=writes)

    def cp(self, dst, src, reads=(), writes=()):
        eng = self.S.ev()
        if eng == "scalar":
            return self.S.op(eng, lambda e: e.copy(dst, src), reads=reads, writes=writes)
        return self.S.op(eng, lambda e: e.tensor_copy(dst, src), reads=reads, writes=writes)

    def ev2(self):
        return "vector"

    def alloc_state(self):
        self.O_S = self.alloc(HV * DK, "S")
        self.b_S = [self.B(("S", h)) for h in range(HV)]
        for h in range(HV):
            self.vo(lambda e, h=h: e.memset(self.V(self.O_S + h * DK, 128, DK), 0.0),
                    writes=(self.b_S[h],))

    def alloc_scan_ws(self):
        a = self.alloc
        self.O_GBTM = a(32, "gbtm")
        self.O_CGTM = a(16, "cgtm")
        self.O_LAM0 = a(16, "lam0")
        self.O_CCOL = a(16, "ccol")
        self.O_CGF = [a(128, "cgf0"), a(128, "cgf1")]
        self.O_BFM = [a(128, "bfm0"), a(128, "bfm1")]
        self.O_BD = a(1024, "bd")
        self.O_E = a(1024, "etile")
        self.O_EP = a(1024, "eptile")
        self.O_BB = a(1024, "bbeta")
        self.O_KK = a(512, "kk4")
        self.O_QK = a(512, "qk4")
        self.O_G = a(1024, "gtile")
        self.O_E127 = a(8, "e127")
        self.O_LAMC = a(8, "lamc")
        self.O_KTM = a(1024, "ktm")
        self.O_VTM = a(512, "vtm")
        self.O_PA = a(512, "pa")
        self.O_PB = a(512, "pb")
        self.O_X = a(1024, "xsolve")
        self.O_UB = a(512, "ubar")
        self.O_WFM = a(512, "wfm")
        self.O_CORE = a(128, "core")
        self.O_ZS = a(128, "zsil")

    def alloc_scan_consts(self):
        a = self.alloc
        self.O_ONESR = a(128, "onesr")
        self.O_ONESC = a(1, "onesc")
        self.O_ONES8 = a(128, "ones8")
        self.O_ID8 = a(8, "id8")
        self.O_ID16 = a(16, "id16")
        self.O_ID32 = a(32, "id32")
        self.O_ID128 = a(128, "id128")
        self.O_MSU = a(128, "msu")
        self.O_MSL = a(128, "msl")
        self.O_MIU = a(128, "miu")
        self.O_BLK8 = a(1024, "blk8")
        self.O_NWBC = a(128, "nwbc")

    def load_scan_consts(self, din):
        V, b = self.V, self.B("const")
        for dst, name, shape in [
            (V(self.O_ONESR, 1, 128), "ones_row", [1, 128]),
            (V(self.O_ONESC, 128, 1), "ones_col", [128, 1]),
            (V(self.O_ONES8, 8, 128), "ones8", [8, 128]),
            (V(self.O_ID8, 8, 8), "id8", [8, 8]),
            (V(self.O_ID16, 16, 16), "id16", [16, 16]),
            (V(self.O_ID32, 32, 32), "id32", [32, 32]),
            (V(self.O_ID128, 128, 128), "id128", [128, 128]),
            (V(self.O_MSU, 128, 128), "mask_su", [128, 128]),
            (V(self.O_MSL, 128, 128), "mask_sl", [128, 128]),
            (V(self.O_MIU, 128, 128), "mask_iu", [128, 128]),
            (V(self.O_BLK8, 8, 1024), "blk8", [8, 1024]),
        ]:
            self.dma(dst, din(name, shape), writes=(b,))

    def emit_chunk(self, kq_slices, v_slices, o_g, o_bt, gcol, b_p, b_gb, with_out,
                   core_sink=None, probe=None):
        S, V, VR, B, PS = self.S, self.V, self.VR, self.B, self.PS
        psb = self.psb
        bc = B("const")

        bgbt = B("gbtm")
        self.trans(4, 128, 16, VR(o_g + gcol, 0, 16, CH), V(self.O_ID16, 16, 16),
                   reads=(b_gb,))
        self.vo(lambda e: e.tensor_copy(V(self.O_GBTM, 128, 16), PS[4][:, 0:16]),
                reads=(psb[4],), writes=(bgbt,))
        self.trans(5, 128, 16, VR(o_bt + gcol, 0, 16, CH), V(self.O_ID16, 16, 16),
                   reads=(b_gb,))
        self.vo(lambda e: e.tensor_copy(VR(self.O_GBTM + 16, 0, 128, 16), PS[5][:, 0:16]),
                reads=(psb[5],), writes=(bgbt,))
        mm = self.mm
        mm(5, 16, V(self.O_MIU, 128, 128), V(self.O_GBTM, 128, 16), True, True,
           reads=(bc, bgbt))
        bcg = B("cgtm")
        self.vo(lambda e: e.tensor_copy(V(self.O_CGTM, 128, 16), PS[5][:, 0:16]),
                reads=(psb[5],), writes=(bcg,))
        if probe: probe("cgtm", V(self.O_CGTM, 128, 16), (bcg,))
        blam = B("lam0")
        S.op("scalar", lambda e: e.activation(V(self.O_LAM0, 128, 16), V(self.O_CGTM, 128, 16), AF.Exp),
             reads=(bcg,), writes=(blam,))
        bccol = B("ccol")
        self.vo(lambda e: e.tensor_tensor(V(self.O_CCOL, 128, 16), V(self.O_LAM0, 128, 16),
                                          VR(self.O_GBTM + 16, 0, 128, 16), ALU.mult),
                reads=(blam, bgbt), writes=(bccol,))
        bcgf = [B("cgf0"), B("cgf1")]
        bbfm = [B("bfm0"), B("bfm1")]
        for half in range(2):
            self.trans(4, 8, 128, V(self.O_CGTM + 8 * half, 128, 8),
                       V(self.O_ID128, 128, 128), reads=(bcg,), c0=128 * half)
            self.vo(lambda e, half=half: e.tensor_copy(
                V(self.O_CGF[half], 8, 128), PS[4][0:8, 128 * half:128 * half + 128]),
                reads=(psb[4],), writes=(bcgf[half],))
        for half in range(2):
            self.trans(5, 8, 128, VR(self.O_GBTM + 16 + 8 * half, 0, 128, 8),
                       V(self.O_ID128, 128, 128), reads=(bgbt,), c0=128 * half)
            self.vo(lambda e, half=half: e.tensor_scalar(
                V(self.O_BFM[half], 8, 128), PS[5][0:8, 128 * half:128 * half + 128],
                -1.0, None, ALU.mult),
                reads=(psb[5],), writes=(bbfm[half],))

        bktm = B("ktm")
        for kh in range(HK):
            k_ap, _ = kq_slices(kh)
            bank = 4 if kh % 2 == 0 else 5
            self.trans(bank, 128, 128, k_ap, V(self.O_ID128, 128, 128), reads=(b_p,))
            self.cp(V(self.O_KTM + kh * 128, 128, 128), PS[bank][:, 0:128],
                    reads=(psb[bank],), writes=(bktm,))

        for bt in range(2):
            bD = B("etile")
            bDP = B("eptile")
            bBB = B("bbeta")
            bKK = B("kk4")
            bQK = B("qk4")
            bG = B("gtile")
            bE127 = B("e127")
            bLC = B("lamc")

            for i in range(4):
                kh = 4 * bt + i
                k_ap, _ = kq_slices(kh)
                mm(0, 128, k_ap, k_ap, True, True, reads=(b_p,), c0=i * 128)
            tkk = self.vo(lambda e: e.tensor_copy(V(self.O_KK, 128, 512), PS[0][:, 0:512]),
                          reads=(psb[0],), writes=(bKK,))
            if with_out:
                for i in range(4):
                    kh = 4 * bt + i
                    k_ap, q_ap = kq_slices(kh)
                    mm(1, 128, k_ap, q_ap, True, True, reads=(b_p,), c0=i * 128)
                self.vo(lambda e: e.tensor_copy(V(self.O_QK, 128, 512), PS[1][:, 0:512]),
                        reads=(psb[1],), writes=(bQK,))

            bBD = B("bd")
            cgf = V(self.O_CGF[bt], 8, 128)
            cgf_b = cgf.rearrange("p (o f) -> p o f", o=1).broadcast_to([8, 8, 128])
            blkv = V(self.O_BLK8, 8, 1024).rearrange("p (o f) -> p o f", o=8)
            bdv = V(self.O_BD, 8, 1024).rearrange("p (o f) -> p o f", o=8)
            self.vo(lambda e, bdv=bdv, cgf_b=cgf_b, blkv=blkv: e.tensor_tensor(bdv, cgf_b, blkv, ALU.mult),
                    reads=(bcgf[bt], bc), writes=(bBD,))
            if probe: probe(f"BD{bt}", self.V(self.O_BD, 8, 1024), (bBD,))
            for half in range(2):
                mm(2, 512, cgf, VR(self.O_BLK8 + half * 512, 0, 8, 512), True, True,
                   reads=(bcgf[bt], bc), c0=0)
                self.vo(lambda e, half=half: e.tensor_copy(
                    V(self.O_EP + half * 512, 128, 512), PS[2][:, 0:512]),
                    reads=(psb[2],), writes=(bDP,))
            for half in range(2):
                mm(3, 512, V(self.O_ONES8, 8, 128), VR(self.O_BD + half * 512, 0, 8, 512),
                   True, True, reads=(bBD, bc), c0=0)
                self.vo(lambda e, half=half: e.tensor_copy(
                    V(self.O_E + half * 512, 128, 512), PS[3][:, 0:512]),
                    reads=(psb[3],), writes=(bD,))
            if probe: probe(f"CGS{bt}", V(self.O_EP, 128, 1024), (bDP,))
            if probe: probe(f"BCG{bt}", V(self.O_E, 128, 1024), (bD,))
            bfm = V(self.O_BFM[bt], 8, 128)
            bfm_b = bfm.rearrange("p (o f) -> p o f", o=1).broadcast_to([8, 8, 128])
            self.vo(lambda e, bdv=bdv, bfm_b=bfm_b, blkv=blkv: e.tensor_tensor(bdv, bfm_b, blkv, ALU.mult),
                    reads=(bbfm[bt], bc), writes=(bBD,))
            for half in range(2):
                mm(2, 512, V(self.O_ONES8, 8, 128), VR(self.O_BD + half * 512, 0, 8, 512),
                   True, True, reads=(bBD, bc), c0=0)
                self.vo(lambda e, half=half: e.tensor_copy(
                    V(self.O_BB + half * 512, 128, 512), PS[2][:, 0:512]),
                    reads=(psb[2],), writes=(bBB,))

            if probe: probe(f"CGF{bt}", V(self.O_CGF[bt], 8, 128), (bcgf[bt],))
            ev_ = V(self.O_E, 128, 1024)
            epv = V(self.O_EP, 128, 1024)
            self.vo(lambda e: e.tensor_tensor(ev_, ev_, epv, ALU.subtract),
                    reads=(bDP,), writes=(bD,))
            self.vo(lambda e: e.tensor_scalar(epv, ev_, -1.0, 0.0, ALU.mult, ALU.min),
                    reads=(bD,), writes=(bDP,))
            S.op("scalar", lambda e: e.activation(epv, epv, AF.Exp), reads=(), writes=(bDP,))
            self.vo(lambda e: e.tensor_scalar(ev_, ev_, 1.0, 0.0, ALU.mult, ALU.min),
                    reads=(), writes=(bD,))
            S.op("scalar", lambda e: e.activation(ev_, ev_, AF.Exp), reads=(), writes=(bD,))
            if probe: probe(f"E{bt}", ev_, (bD,))
            if probe: probe(f"EP{bt}", epv, (bDP,))

            e127v = V(self.O_E127, 128, 8)
            esel = V(self.O_E, 128, 1024).rearrange("p (h t) -> p h t", h=8)[:, :, 127:128]
            self.vo(lambda e, e127v=e127v, esel=esel: e.tensor_copy(e127v.rearrange("p (h o) -> p h o", o=1), esel),
                    reads=(bD,), writes=(bE127,))
            self.vo(lambda e, bt=bt: e.tensor_tensor(
                V(self.O_LAMC, 128, 8), e127v, VR(self.O_LAM0 + 8 * bt, 0, 128, 8), ALU.mult),
                reads=(bE127, blam), writes=(bLC,))

            kkpair = V(self.O_KK, 128, 512).rearrange("p (h o t) -> p h o t", h=4, o=1) \
                .broadcast_to([128, 4, 2, 128])
            msuv = V(self.O_MSU, 128, 128).rearrange("p (o t) -> p o t", o=1).broadcast_to([128, 8, 128])
            mslv = V(self.O_MSL, 128, 128).rearrange("p (o t) -> p o t", o=1).broadcast_to([128, 8, 128])
            miuv = V(self.O_MIU, 128, 128).rearrange("p (o t) -> p o t", o=1).broadcast_to([128, 8, 128])
            ev4 = V(self.O_E, 128, 1024).rearrange("p (h r t) -> p h r t", h=4, r=2)
            epv4 = V(self.O_EP, 128, 1024).rearrange("p (h r t) -> p h r t", h=4, r=2)
            ev8 = V(self.O_E, 128, 1024).rearrange("p (h t) -> p h t", h=8)
            epv8 = V(self.O_EP, 128, 1024).rearrange("p (h t) -> p h t", h=8)
            gv8 = V(self.O_G, 128, 1024).rearrange("p (h t) -> p h t", h=8)
            gv4 = V(self.O_G, 128, 1024).rearrange("p (h r t) -> p h r t", h=4, r=2)

            if with_out:
                qkpair = V(self.O_QK, 128, 512).rearrange("p (h o t) -> p h o t", h=4, o=1) \
                    .broadcast_to([128, 4, 2, 128])
                self.vo(lambda e: e.tensor_tensor(gv8, ev8, miuv, ALU.mult),
                        reads=(bD, bc), writes=(bG,))
                self.vo(lambda e: e.tensor_tensor(gv4, gv4, qkpair, ALU.mult),
                        reads=(bQK,), writes=(bG,))
            self.vo(lambda e: e.tensor_tensor(ev8, ev8, msuv, ALU.mult),
                    reads=(bc,), writes=(bD,))
            self.vo(lambda e: e.tensor_tensor(ev4, ev4, kkpair, ALU.mult),
                    reads=(bKK,), writes=(bD,))
            self.vo(lambda e: e.tensor_tensor(V(self.O_E, 128, 1024), V(self.O_E, 128, 1024),
                                              V(self.O_BB, 128, 1024), ALU.mult),
                    reads=(bBB,), writes=(bD,))
            if probe: probe(f"M{bt}", V(self.O_E, 128, 1024), (bD,))
            self.vo(lambda e: e.tensor_tensor(epv8, epv8, mslv, ALU.mult),
                    reads=(bc,), writes=(bDP,))
            self.vo(lambda e: e.tensor_tensor(epv4, epv4, kkpair, ALU.mult),
                    reads=(bKK,), writes=(bDP,))
            for i in range(8):
                h = 8 * bt + i
                asl = V(self.O_EP + i * 128, 128, 128)
                bcol = VR(self.O_GBTM + 16 + h, 0, 128, 1)
                eng = "vector"
                self.vo(lambda e, asl=asl, bcol=bcol:
                    e.tensor_scalar(asl, asl, bcol, -1.0, ALU.mult, ALU.mult),
                    reads=(bgbt,), writes=(bDP,), eng=eng)
            if probe: probe(f"A{bt}", V(self.O_EP, 128, 1024), (bDP,))

            for g in range(2):
                heads = [8 * bt + 4 * g + i for i in range(4)]
                bVT = B("vtm")
                for i, h in enumerate(heads):
                    bank = 4 if i % 2 == 0 else 5
                    self.trans(bank, 128, 128, v_slices(h), V(self.O_ID128, 128, 128),
                               reads=(b_p,))
                    self.cp(V(self.O_VTM + i * 128, 128, 128), PS[bank][:, 0:128],
                            reads=(psb[bank],), writes=(bVT,))
                bX = B("xsolve")
                for i, h in enumerate(heads):
                    kh = h // 2
                    ccol = VR(self.O_CCOL + h, 0, 128, 1)
                    btcol = VR(self.O_GBTM + 16 + h, 0, 128, 1)
                    self.vo(lambda e, i=i, kh=kh, ccol=ccol: e.tensor_scalar(
                        V(self.O_X + i * 256, 128, 128), V(self.O_KTM + kh * 128, 128, 128),
                        ccol, None, ALU.mult),
                        reads=(bktm, bccol), writes=(bX,), eng=self.ev2())
                    self.vo(lambda e, i=i, btcol=btcol: e.tensor_scalar(
                        V(self.O_X + i * 256 + 128, 128, 128), V(self.O_VTM + i * 128, 128, 128),
                        btcol, None, ALU.mult),
                        reads=(bVT, bgbt), writes=(bX,), eng=self.ev2())
                if probe: probe(f"R{bt}{g}", V(self.O_X, 128, 1024), (bX,))
                bPA = B("pa")
                bPB = B("pb")
                for j in range(7):
                    if j == 0:
                        paof, pa_b = self.O_E, bD
                        pbof, pb_b = self.O_EP, bDP
                        sl = lambda of, i: V(of + (4 * g + i) * 128, 128, 128)
                    else:
                        paof, pa_b = self.O_PA, bPA
                        pbof, pb_b = self.O_PB, bPB
                        sl = lambda of, i: V(of + i * 128, 128, 128)
                    for i in range(4):
                        bank = 2 if i < 2 else 3
                        mm(bank, 256, sl(paof, i), V(self.O_X + i * 256, 128, 256),
                           True, True, reads=(pa_b, bX), c0=(i % 2) * 256)
                    for half in range(2):
                        bank = 2 + half
                        xv = V(self.O_X + half * 512, 128, 512)
                        self.vo(lambda e, xv=xv, bank=bank:
                                e.tensor_tensor(xv, xv, PS[bank][:, 0:512], ALU.add),
                                reads=(psb[bank],), writes=(bX,))
                    if j < 6:
                        for i in range(4):
                            mm(0, 128, sl(pbof, i), sl(paof, i), True, True,
                               reads=(pa_b, pb_b), c0=i * 128)
                        for i in range(4):
                            mm(1, 128, sl(paof, i), sl(pbof, i), True, True,
                               reads=(pa_b, pb_b), c0=i * 128)
                        self.cp(V(self.O_PA, 128, 512), PS[0][:, 0:512],
                                reads=(psb[0],), writes=(bPA,))
                        self.cp(V(self.O_PB, 128, 512), PS[1][:, 0:512],
                                reads=(psb[1],), writes=(bPB,))
                if probe: probe(f"X{bt}{g}", V(self.O_X, 128, 1024), (bX,))
                bWF = B("wfm")
                bUB = B("ubar")
                for i, h in enumerate(heads):
                    bank = 4 if i % 2 == 0 else 5
                    self.trans(bank, 128, 128, V(self.O_X + i * 256, 128, 128),
                               V(self.O_ID128, 128, 128), reads=(bX,))
                    self.cp(V(self.O_WFM + i * 128, 128, 128), PS[bank][:, 0:128],
                            reads=(psb[bank],), writes=(bWF,))
                for i, h in enumerate(heads):
                    mm(4, 128, V(self.O_WFM + i * 128, 128, 128),
                       V(self.O_S + h * DK, 128, 128), True, True,
                       reads=(bWF, self.b_S[h]), c0=(i % 4) * 128)
                for i, h in enumerate(heads):
                    self.vo(lambda e, i=i: e.tensor_tensor(
                        V(self.O_UB + i * 128, 128, 128), V(self.O_X + i * 256 + 128, 128, 128),
                        PS[4][:, i * 128:(i + 1) * 128], ALU.subtract),
                        reads=(psb[4], bX), writes=(bUB,))
                if probe: probe(f"UB{bt}{g}", V(self.O_UB, 128, 512), (bUB,))
                if with_out:
                    for i, h in enumerate(heads):
                        kh = h // 2
                        _, q_ap = kq_slices(kh)
                        mm(5, 128, q_ap, V(self.O_S + h * DK, 128, 128), True, True,
                           reads=(b_p, self.b_S[h]), c0=0)
                        gsl_h = V(self.O_G + (4 * g + i) * 128, 128, 128)
                        mm(6, 128, gsl_h, V(self.O_UB + i * 128, 128, 128), True, True,
                           reads=(bG, bUB), c0=0)
                        lam0col = VR(self.O_LAM0 + h, 0, 128, 1)
                        bCORE = B("core")
                        cv = V(self.O_CORE, 128, 128)
                        self.vo(lambda e, cv=cv, lam0col=lam0col: e.tensor_scalar(
                            cv, PS[5][:, 0:128], lam0col, None, ALU.mult),
                            reads=(psb[5], blam), writes=(bCORE,))
                        self.vo(lambda e, cv=cv: e.tensor_tensor(cv, cv, PS[6][:, 0:128], ALU.add),
                                reads=(psb[6],), writes=(bCORE,))
                        core_sink(h, cv, (bCORE,))
                for i, h in enumerate(heads):
                    kh = h // 2
                    e127c = VR(self.O_E127 + (h - 8 * bt), 0, 128, 1)
                    self.vo(lambda e, i=i, kh=kh, e127c=e127c: e.tensor_scalar(
                        V(self.O_X + i * 256, 128, 128), V(self.O_KTM + kh * 128, 128, 128),
                        e127c, None, ALU.mult),
                        reads=(bktm, bE127), writes=(bX,), eng=self.ev2())
                for i, h in enumerate(heads):
                    mm(7, 128, V(self.O_X + i * 256, 128, 128), V(self.O_UB + i * 128, 128, 128),
                       True, True, reads=(bX, bUB), c0=(i % 4) * 128)
                for i, h in enumerate(heads):
                    lamcc = VR(self.O_LAMC + (h - 8 * bt), 0, 128, 1)
                    sv = V(self.O_S + h * DK, 128, 128)
                    self.vo(lambda e, sv=sv, lamcc=lamcc, i=i: e.scalar_tensor_tensor(
                        sv, sv, lamcc, PS[7][:, i * 128:(i + 1) * 128], ALU.mult, ALU.add),
                        reads=(psb[7], bLC), writes=(self.b_S[h],))


FULL_BUILD_PLACEHOLDER = True




def build_full(debug=False, phases=("prefix", "span", "ffn")):
    b = Bld(53200)
    nc = b.nc
    S, V, VR, PS, psb = b.S, b.V, b.VR, b.PS, b.psb
    B = b.B
    BF16 = mybir.dt.bfloat16

    def bfv(off, cols_f32):
        return b.AR[0:128, off:off + cols_f32].bitcast(BF16)

    def din(name, shape):
        return nc.dram_tensor(name, shape, F32, kind="ExternalInput").ap()

    xpre = din("xpre", [HID, NPT * PCOLS])
    premask = din("premask", [1, NPT * PCOLS])
    xspan = din("xspan", [HID, NQ * QCOLS])
    tailm = din("tailm", [128, 1])
    wqkv = din("wqkv", [CONV_DIM, HID])
    wz = din("wz", [HID, HID])
    wout_w = din("wout_w", [VAL_DIM, HID])
    wgup = din("wgup", [NEXP * 22 * 128, HID])
    wdwn = din("wdwn", [NEXP * 16 * 128, FF])
    wsgu = din("wsgu", [44 * 128, HID])
    wsd = din("wsd", [16 * 128, FS])
    wa_t = din("wa_t", [128, NKC * 16])
    wb_t = din("wb_t", [128, NKC * 16])
    wgate = din("wgate", [128, NKC * NEXP])
    bqkv_t = din("bqkv_t", [128, 32])
    cb_t = din("cb_t", [128, 32])
    cw_t = din("cw_t", [128, 128])
    bz_row = din("bz_row", [1, VAL_DIM])
    ba_c = din("ba_c", [16, 1])
    bb_c = din("bb_c", [16, 1])
    dtb = din("dtb", [16, 1])
    alog = din("alog", [16, 1])
    nw_row = din("nw_row", [1, DV])
    bout_t = din("bout_t", [128, 16])

    out = nc.dram_tensor("out", [HID, SPAN], F32, kind="ExternalOutput").ap()
    wqkv_s = nc.dram_tensor("wqkv_s", [CONV_DIM, 2 * HID], mybir.dt.bfloat16, kind="Internal").ap()
    wz_s = nc.dram_tensor("wz_s", [HID, 2 * HID], mybir.dt.bfloat16, kind="Internal").ap()
    wout_s = nc.dram_tensor("wout_s", [VAL_DIM, 2 * HID], mybir.dt.bfloat16, kind="Internal").ap()
    wgup_s = nc.dram_tensor("wgup_s", [NEXP * 22 * 128, 2 * HID], mybir.dt.bfloat16, kind="Internal").ap()
    wdwn_s = nc.dram_tensor("wdwn_s", [NEXP * 16 * 128, 2 * FF], mybir.dt.bfloat16, kind="Internal").ap()
    wsgu_s = nc.dram_tensor("wsgu_s", [44 * 128, 2 * HID], mybir.dt.bfloat16, kind="Internal").ap()
    wsd_s = nc.dram_tensor("wsd_s", [16 * 128, 2 * FS], mybir.dt.bfloat16, kind="Internal").ap()
    xnh_scr = nc.dram_tensor("xnh_scr", [HID, SPAN], BF16, kind="Internal").ap()
    xnl_scr = nc.dram_tensor("xnl_scr", [HID, SPAN], BF16, kind="Internal").ap()
    ax_scr = nc.dram_tensor("ax_scr", [HID, SPAN], F32, kind="Internal").ap()
    out_re = out.rearrange("(k p) t -> p k t", p=128)
    xnh_re = xnh_scr.rearrange("(k p) t -> p k t", p=128)
    xnl_re = xnl_scr.rearrange("(k p) t -> p k t", p=128)
    ax_re = ax_scr.rearrange("(k p) t -> p k t", p=128)
    xpre_re = xpre.rearrange("(k p) t -> p k t", p=128)
    xspan_re = xspan.rearrange("(k p) t -> p k t", p=128)
    if debug:
        d_spre = nc.dram_tensor("d_spre", [128, HV * DK], F32, kind="ExternalOutput").ap()
        d_ax = nc.dram_tensor("d_ax", [HID, SPAN], F32, kind="ExternalOutput").ap()

    b.alloc_scan_consts()
    O_WAH = b.alloc(128, "wah"); O_WAL = b.alloc(128, "wal")
    O_WBH = b.alloc(128, "wbh"); O_WBL = b.alloc(128, "wbl")
    O_WGH = b.alloc(64, "wgh"); O_WGL = b.alloc(64, "wgl")
    O_WSTAGE0 = b.alloc(256, "wstage_ab")
    O_BQKV = b.alloc(32, "bqkv")
    O_CBt = b.alloc(32, "cbt")
    O_CW = b.alloc(128, "cw")
    O_BOUT = b.alloc(16, "bout")
    O_BA = b.alloc(1, "ba")
    O_BB2 = b.alloc(1, "bb")
    O_DTB = b.alloc(1, "dtb")
    O_NEXPA = b.alloc(1, "nexpa")
    O_TAILM = b.alloc(1, "tailm")
    O_RC = b.alloc(1, "rcol")
    b.alloc_state()
    O_SHARED = b.alloc(3104, "shared")
    b.alloc_scan_ws()
    BIG0 = b._cur
    BIGSZ = 53200 - BIG0
    assert BIGSZ >= 27000, BIGSZ

    bc = B("const")
    b_shared = B("shared")
    b_xnscr = B("xn_scr")
    b_axscr = B("ax_scr")

    b.load_scan_consts(din)
    for dst, src in [
        (V(O_WSTAGE0, 128, 256), wa_t),
        (V(O_BQKV, 128, 32), bqkv_t),
        (V(O_CBt, 128, 32), cb_t), (V(O_CW, 128, 128), cw_t),
        (V(O_BOUT, 128, 16), bout_t), (V(O_BA, 16, 1), ba_c),
        (V(O_BB2, 16, 1), bb_c), (V(O_DTB, 16, 1), dtb),
        (V(O_NEXPA, 16, 1), alog), (V(O_TAILM, 128, 1), tailm),
    ]:
        b.dma(dst, src, writes=(bc,))

    def split_fl(o_src, o_hi, o_lo, cols, br, bw, src_parts=128):
        sv = V(o_src, src_parts, cols)
        hv_ = VR(o_hi, 0, src_parts, cols // 2).bitcast(BF16)
        lv = VR(o_lo, 0, src_parts, cols // 2).bitcast(BF16)
        S.op("scalar", lambda e: e.copy(hv_, sv), reads=(br,), writes=(bw,))
        b.vo(lambda e: e.tensor_tensor(lv, sv, hv_, ALU.subtract), reads=(br,), writes=(bw,))

    split_fl(O_WSTAGE0, O_WAH, O_WAL, 256, bc, bc)
    b.dma(V(O_WSTAGE0, 128, 256), wb_t, writes=(bc,))
    split_fl(O_WSTAGE0, O_WBH, O_WBL, 256, bc, bc)
    b.dma(VR(O_SHARED, 0, 128, 128), wgate, writes=(b_shared,))
    split_fl(O_SHARED, O_WGH, O_WGL, 128, b_shared, bc)

    S.op("scalar", lambda e: e.activation(V(O_NEXPA, 16, 1), V(O_NEXPA, 16, 1), AF.Exp),
         reads=(bc,), writes=(bc,))
    b.vo(lambda e: e.tensor_scalar(V(O_NEXPA, 16, 1), V(O_NEXPA, 16, 1), -1.0, None, ALU.mult),
         reads=(bc,), writes=(bc,))
    b.dma(V(b.O_NWBC, 1, 128), nw_row, writes=(B("nwtmp"),))
    b.mm(0, 128, V(b.O_ONESR, 1, 128), V(b.O_NWBC, 1, 128), True, True,
         reads=(bc, B("nwtmp")))
    b.vo(lambda e: e.tensor_copy(V(b.O_NWBC, 128, 128), PS[0][:, 0:128]),
         reads=(psb[0],), writes=(bc,))


    def emit_wload(o_slot, scr, r0, K, b_slot):
        b.dma(VR(o_slot, 0, 128, K).bitcast(mybir.dt.bfloat16),
              scr[r0:r0 + 128, 0:2 * K],
              reads=(B(("wscr", scr.tensor.name)),), writes=(b_slot,))

    def wall():
        for e in S.ENGINES:
            if S.pending_ph[e]:
                S.op(e, lambda en: en.nop(), reads=(), writes=(B(("wallnop", e)),))
        snap = tuple((s, v) for s, v in S.counts.items() if v > 0)
        wt = S.op("sync", lambda e: e.nop(), extra=snap)
        for bf in b.bufs.values():
            bf.writer = wt
            bf.readers = []
        for pb in psb:
            pb.writer = wt
            pb.readers = []

    def mm3(bank, n, whi, wlo, xhi, xlo, start, stop, reads=(), c0=0, parts=128):
        b.mm(bank, n, whi, xhi, start, False, reads=reads, c0=c0, parts=parts)
        b.mm(bank, n, whi, xlo, False, False, reads=reads, c0=c0, parts=parts)
        b.mm(bank, n, wlo, xhi, False, stop, reads=reads, c0=c0, parts=parts)

    def xsl(o_x, ncols, kc, part, t0, tn):
        base = o_x + kc * ncols + part * (ncols // 2)
        return b.AR[0:128, base:base + ncols // 2].bitcast(BF16)[:, t0:t0 + tn]

    def emit_ln(o_x, ncols, c0, n, o_sq, o_row, o_bc, b_x):
        assert n <= 512 and ncols % 2 == 0
        for kc in range(NKC):
            xa = V(o_x + kc * ncols + c0, 128, n)
            b.mm(6, n, V(b.O_ONESC, 128, 1), xa, kc == 0, kc == NKC - 1,
                 reads=(bc, b_x), parts=1)
        for kc in range(NKC):
            xa = V(o_x + kc * ncols + c0, 128, n)
            sq = V(o_sq, 128, n)
            bsq = B("lnsq")
            S.op("scalar", lambda e, sq=sq, xa=xa: e.activation(sq, xa, AF.Square),
                 reads=(b_x,), writes=(bsq,))
            b.mm(7, n, V(b.O_ONESC, 128, 1), sq, kc == 0, kc == NKC - 1,
                 reads=(bc, bsq), parts=1)
        r_m = V(o_row, 1, n)
        r_v = V(o_row + 512, 1, n)
        r_t = V(o_row + 1024, 1, n)
        bm, bv, bt_ = B("lnm"), B("lnv"), B("lnt")
        b.vo(lambda e: e.tensor_scalar(r_m, PS[6][0:1, 0:n], 1.0 / HID, None, ALU.mult),
             reads=(psb[6],), writes=(bm,))
        b.vo(lambda e: e.tensor_scalar(r_v, PS[7][0:1, 0:n], 1.0 / HID, LN_EPS, ALU.mult, ALU.add),
             reads=(psb[7],), writes=(bv,))
        b.vo(lambda e: e.tensor_tensor(r_t, r_m, r_m, ALU.mult), reads=(bm,), writes=(bt_,))
        b.vo(lambda e: e.tensor_tensor(r_v, r_v, r_t, ALU.subtract), reads=(bt_,), writes=(bv,))
        b.vo(lambda e: e.reciprocal(r_v, r_v), reads=(), writes=(bv,))
        S.op("scalar", lambda e: e.activation(r_v, r_v, AF.Sqrt), reads=(), writes=(bv,))
        b.mm(6, n, V(b.O_ONESR, 1, 128), r_v, True, True, reads=(bc, bv))
        bc1 = B("lnc1")
        b.vo(lambda e: e.tensor_copy(V(o_bc, 128, n), PS[6][:, 0:n]),
             reads=(psb[6],), writes=(bc1,))
        b.mm(7, n, V(b.O_ONESR, 1, 128), r_m, True, True, reads=(bc, bm))
        bc0 = B("lnc0")
        b.vo(lambda e: e.tensor_copy(V(o_bc + 512, 128, n), PS[7][:, 0:n]),
             reads=(psb[7],), writes=(bc0,))
        for kc in range(NKC):
            xa = V(o_x + kc * ncols + c0, 128, n)
            t2 = V(o_sq, 128, n)
            bsq = B("lnsq")
            b.vo(lambda e, xa=xa: e.tensor_tensor(xa, xa, V(o_bc + 512, 128, n), ALU.subtract),
                 reads=(bc0,), writes=(b_x,))
            b.vo(lambda e, xa=xa, t2=t2: e.tensor_tensor(t2, xa, V(o_bc, 128, n), ALU.mult),
                 reads=(bc1, b_x), writes=(bsq,))
            hi = bfv(o_x + kc * ncols, ncols // 2)[:, c0:c0 + n]
            lo = bfv(o_x + kc * ncols + ncols // 2, ncols // 2)[:, c0:c0 + n]
            S.op("scalar", lambda e, hi=hi, t2=t2: e.copy(hi, t2), reads=(bsq,), writes=(b_x,))
            b.vo(lambda e, lo=lo, t2=t2, hi=hi: e.tensor_tensor(lo, t2, hi, ALU.subtract),
                 reads=(bsq, b_x), writes=(b_x,))

    def emit_wsplit(o_stage, o_slot, cols, wdram, r0, b_st, b_sl):
        b.dma(V(o_stage, 128, cols), wdram[r0:r0 + 128, 0:cols], writes=(b_st,))
        split_fl(o_stage, o_slot, o_slot + cols // 2, cols, b_st, b_sl)

    def emit_proj(o_dst, ncols_dst, bfts, gfts, o_xn, ncols_xn, c0, n, wsrc_scr,
                  o_wsl, b_wsl, b_x, b_p, o_pm=None, b_pm=None, pm_c0=0):
        assert n <= 512
        for i, (bft, gft) in enumerate(zip(bfts, gfts)):
            sl = i % 2
            emit_wload(o_wsl + sl * 2048, wsrc_scr, gft * 128, 2048, b_wsl[sl])
            bank = 6 if i % 2 == 0 else 7
            for kc in range(NKC):
                whi = bfv(o_wsl + sl * 2048 + kc * 64, 64)
                wlo = bfv(o_wsl + sl * 2048 + 1024 + kc * 64, 64)
                xhi = xsl(o_xn, ncols_xn, kc, 0, c0, n)
                xlo = xsl(o_xn, ncols_xn, kc, 1, c0, n)
                mm3(bank, n, whi, wlo, xhi, xlo, kc == 0, kc == NKC - 1,
                    reads=(b_wsl[sl], b_x))
            da = V(o_dst + bft * ncols_dst + c0, 128, n)
            bcol = V(O_BQKV + gft, 128, 1)
            if o_pm is not None:
                pmv = V(o_pm + pm_c0, 128, n)
                b.vo(lambda e, da=da, bank=bank, bcol=bcol, pmv=pmv:
                     e.scalar_tensor_tensor(da, PS[bank][:, 0:n], bcol, pmv, ALU.add, ALU.mult),
                     reads=(psb[bank], bc, b_pm), writes=(b_p,))
            else:
                eng = b.S.ev()
                if eng == "scalar":
                    b.vo(lambda e, da=da, bank=bank, bcol=bcol:
                         e.activation(da, PS[bank][:, 0:n], AF.Identity, bias=bcol),
                         reads=(psb[bank], bc), writes=(b_p,), eng="scalar")
                else:
                    b.vo(lambda e, da=da, bank=bank, bcol=bcol:
                         e.tensor_scalar(da, PS[bank][:, 0:n], bcol, None, ALU.add),
                         reads=(psb[bank], bc), writes=(b_p,))

    def emit_conv_silu(o_p, ncols, n, nft, gft0, o_cv, b_p):
        for ft in range(nft):
            gft = gft0 + ft
            acc = V(o_cv, 128, n)
            bacc = B("cvacc")
            for j in range(KCONV):
                src = V(o_p + ft * ncols + 1 + j, 128, n)
                cwc = V(O_CW + gft * 4 + j, 128, 1)
                if j == 0:
                    b.vo(lambda e, acc=acc, src=src, cwc=cwc: e.tensor_scalar(acc, src, cwc, None, ALU.mult),
                         reads=(b_p, bc), writes=(bacc,))
                else:
                    b.vo(lambda e, acc=acc, src=src, cwc=cwc:
                         e.scalar_tensor_tensor(acc, src, cwc, acc, ALU.mult, ALU.add),
                         reads=(b_p, bc), writes=(bacc,))
            dstv = V(o_p + ft * ncols + 4, 128, n)
            cbcol = V(O_CBt + gft, 128, 1)
            S.op("scalar", lambda e, dstv=dstv, acc=acc, cbcol=cbcol:
                 e.activation(dstv, acc, AF.Silu, bias=cbcol),
                 reads=(bacc, bc), writes=(b_p,))

    def emit_l2(o_p, ncols, n, bfts, is_q, o_sq, o_row, b_p):
        assert n <= 512
        for ft in bfts:
            base = V(o_p + ft * ncols + 4, 128, n)
            sq = V(o_sq, 128, n)
            bsq = B("l2sq")
            S.op("scalar", lambda e, sq=sq, base=base: e.activation(sq, base, AF.Square),
                 reads=(b_p,), writes=(bsq,))
            b.mm(6, n, V(b.O_ONESC, 128, 1), sq, True, True, reads=(bc, bsq), parts=1)
            rrow = V(o_row, 1, n)
            brow = B("l2row")
            b.vo(lambda e, rrow=rrow: e.tensor_scalar(rrow, PS[6][0:1, 0:n], L2_EPS, None, ALU.add),
                 reads=(psb[6],), writes=(brow,))
            b.vo(lambda e, rrow=rrow: e.reciprocal(rrow, rrow), reads=(), writes=(brow,))
            sc = (1.0 / DK) if is_q else 1.0
            S.op("scalar", lambda e, rrow=rrow, sc=sc: e.activation(rrow, rrow, AF.Sqrt, scale=sc),
                 reads=(), writes=(brow,))
            b.mm(7, n, V(b.O_ONESR, 1, 128), rrow, True, True, reads=(bc, brow))
            b.vo(lambda e, base=base: e.tensor_tensor(base, base, PS[7][:, 0:n], ALU.mult),
                 reads=(psb[7],), writes=(b_p,))

    def emit_gb(o_g, o_bt, o_gt, o_xn, ncols_xn, c0, n, b_x, b_gb, o_pm=None,
                b_pm=None, pm_c0=0):
        assert n <= 512
        for z_i, (o_dst, o_wh, o_wl, o_bias) in enumerate(
                [(o_g, O_WAH, O_WAL, O_BA), (o_bt, O_WBH, O_WBL, O_BB2)]):
            bank = 6 + z_i
            for kc in range(NKC):
                whi = bfv(o_wh + kc * 8, 8)
                wlo = bfv(o_wl + kc * 8, 8)
                xhi = xsl(o_xn, ncols_xn, kc, 0, c0, n)
                xlo = xsl(o_xn, ncols_xn, kc, 1, c0, n)
                mm3(bank, n, whi, wlo, xhi, xlo, kc == 0, kc == NKC - 1,
                    reads=(bc, b_x), parts=16)
            da = VR(o_dst, 0, 16, n)
            b.vo(lambda e, da=da, bank=bank, o_bias=o_bias:
                 e.tensor_scalar(da, PS[bank][0:16, 0:n], V(o_bias, 16, 1), None, ALU.add),
                 reads=(psb[bank], bc), writes=(b_gb,))
        ga = VR(o_g, 0, 16, n)
        bet = VR(o_bt, 0, 16, n)
        t1 = VR(o_gt, 0, 16, n)
        b1 = B("gbt1")
        S.op("scalar", lambda e: e.activation(bet, bet, AF.Sigmoid), reads=(), writes=(b_gb,))
        S.op("scalar", lambda e: e.activation(t1, ga, AF.Abs, bias=V(O_DTB, 16, 1)),
             reads=(b_gb, bc), writes=(b1,))
        S.op("scalar", lambda e: e.activation(t1, t1, AF.Exp, scale=-1.0), reads=(), writes=(b1,))
        S.op("scalar", lambda e: e.activation(t1, t1, AF.Ln, bias=1.0), reads=(), writes=(b1,))
        S.op("scalar", lambda e: e.activation(ga, ga, AF.Relu, bias=V(O_DTB, 16, 1)),
             reads=(bc,), writes=(b_gb,))
        b.vo(lambda e: e.tensor_tensor(ga, ga, t1, ALU.add), reads=(b1,), writes=(b_gb,))
        b.vo(lambda e: e.tensor_scalar(ga, ga, V(O_NEXPA, 16, 1), None, ALU.mult),
             reads=(bc,), writes=(b_gb,))
        if o_pm is not None:
            pmg = VR(o_pm + pm_c0, 0, 16, n)
            b.vo(lambda e: e.tensor_tensor(ga, ga, pmg, ALU.mult), reads=(b_pm,), writes=(b_gb,))
            b.vo(lambda e: e.tensor_tensor(bet, bet, pmg, ALU.mult), reads=(b_pm,), writes=(b_gb,))

    b._cur = BIG0
    o_xq = b.alloc(16 * PCOLS, "p_xq")
    o_sq = b.alloc(512, "p_sq")
    o_row = b.alloc(1536, "p_row")
    o_bcb = b.alloc(1024, "p_bc")
    o_pkv = b.alloc(24 * PCOLS, "p_pkv")
    o_g = b.alloc(PTILE, "p_g")
    o_bt = b.alloc(PTILE, "p_bt")
    o_gt = b.alloc(PTILE, "p_gt")
    o_cv = b.alloc(584, "p_cv")
    o_pm = b.alloc(512, "p_pm")
    o_wsl = b.alloc(4096, "p_wsl")
    b_wsl = [B("wsl0"), B("wsl1")]
    b.dma(VR(O_SHARED, 0, 1, NPT * PCOLS), premask, writes=(b_shared,))

    b_x0 = B("px")
    b.dma(V(o_xq, 128, 16 * PCOLS).rearrange("p (k t) -> p k t", k=16),
          xpre_re[:, :, 0:PCOLS], writes=(b_x0,))
    emit_ln(o_xq, PCOLS, 4, 512, o_sq, o_row, o_bcb, b_x0)

    o_pstage = o_pkv
    o_pslot = o_pkv + 8192
    b_pst = [B(f"pst{i}") for i in range(4)]
    b_psl = [B(f"psl{i}") for i in range(4)]
    psl_i = 0
    for scr, wdram, nft, K in [
        (wqkv_s, wqkv, CONV_DIM // 128, HID),
        (wz_s, wz, NKC, HID),
        (wout_s, wout_w, VAL_DIM // 128, HID),
        (wgup_s, wgup, NEXP * 22, HID),
        (wdwn_s, wdwn, NEXP * 16, FF),
        (wsgu_s, wsgu, 44, HID),
        (wsd_s, wsd, 16, FS),
    ]:
        npc = (K + 2047) // 2048
        w = K // npc
        scr_re = scr.rearrange("r (two k) -> r two k", two=2)
        for ft in range(nft):
            for pc in range(npc):
                c0 = pc * w
                sl = psl_i % 4
                psl_i += 1
                b.dma(V(o_pstage + sl * 2048, 128, w),
                      wdram[ft * 128:(ft + 1) * 128, c0:c0 + w], writes=(b_pst[sl],))
                split_fl(o_pstage + sl * 2048, o_pslot + sl * 2048,
                         o_pslot + sl * 2048 + w // 2, w, b_pst[sl], b_psl[sl])
                b.dma(scr_re[ft * 128:(ft + 1) * 128, :, c0:c0 + w],
                      VR(o_pslot + sl * 2048, 0, 128, w).bitcast(mybir.dt.bfloat16)
                      .rearrange("p (two k) -> p two k", two=2),
                      reads=(b_psl[sl],), writes=(B(("wscr", scr.tensor.name)),),
                      eng="gpsimd")
    wall()

    for pt in range(NPT):
        b_x = B("px")
        b_p = B("pp")
        b_gb = B("pgb")
        b_pm = B("ppm")
        if pt > 0:
            b.dma(V(o_xq, 128, 16 * PCOLS).rearrange("p (k t) -> p k t", k=16),
                  xpre_re[:, :, pt * PCOLS:(pt + 1) * PCOLS], writes=(b_x,))
        b.mm(6, 512, V(b.O_ONESR, 1, 128), VR(O_SHARED + pt * PCOLS + 4, 0, 1, 512),
             True, True, reads=(bc, b_shared))
        b.vo(lambda e: e.tensor_copy(V(o_pm, 128, 512), PS[6][:, 0:512]),
             reads=(psb[6],), writes=(b_pm,))
        if pt > 0:
            emit_ln(o_xq, PCOLS, 4, 512, o_sq, o_row, o_bcb, b_x)
        emit_proj(o_pkv, PCOLS, list(range(24)), list(range(8, 32)),
                  o_xq, PCOLS, 4, 512, wqkv_s, o_wsl, b_wsl, b_x, b_p,
                  o_pm=o_pm, b_pm=b_pm, pm_c0=0)
        hsl = V(o_pkv, 128, 24 * PCOLS).rearrange("p (f t) -> p f t", f=24)
        stash = V(o_cv + 512, 128, 24 * 3).rearrange("p (f t) -> p f t", f=24)
        if pt == 0:
            b.vo(lambda e: e.memset(hsl[:, :, 1:4], 0.0), reads=(), writes=(b_p,))
        else:
            b.vo(lambda e: e.tensor_copy(hsl[:, :, 1:4], stash),
                 reads=(B("cvhalo"),), writes=(b_p,))
        b.vo(lambda e: e.tensor_copy(stash, hsl[:, :, PCOLS - 3:PCOLS]),
             reads=(b_p,), writes=(B("cvhalo"),))
        emit_conv_silu(o_pkv, PCOLS, PTILE, 24, 8, o_cv, b_p)
        emit_l2(o_pkv, PCOLS, PTILE, list(range(8)), False, o_sq, o_row, b_p)
        emit_gb(o_g, o_bt, o_gt, o_xq, PCOLS, 4, PTILE, b_x, b_gb, o_pm=o_pm,
                b_pm=b_pm, pm_c0=0)

        def kqp(kh):
            return (V(o_pkv + kh * PCOLS + 4 + kqp.c * CH, 128, 128), None)

        def vsp(h):
            return V(o_pkv + (8 + h) * PCOLS + 4 + vsp.c * CH, 128, 128)

        for ci in range(PTILE // CH):
            kqp.c = ci
            vsp.c = ci
            b.emit_chunk(kqp, vsp, o_g, o_bt, ci * CH, b_p, b_gb, False)

    if debug:
        for h in range(HV):
            b.dma(d_spre[:, h * DK:(h + 1) * DK], V(b.O_S + h * DK, 128, DK),
                  reads=(b.b_S[h],))

    if "span" not in phases:
        fence = [(s, v) for s, v in S.counts.items() if "_d" in s and v > 0]
        S.op("sync", lambda e: e.nop(), extra=tuple(fence))
        S.emit(nc)
        return nc
    wall()
    b._cur = BIG0
    o_xq2 = b.alloc(16 * QCOLS, "s_xq")
    o_p2 = b.alloc(32 * QCOLS, "s_p")
    o_z = b.alloc(2 * VAL_DIM, "s_z")
    o_g2 = b.alloc(QTOK, "s_g")
    o_bt2 = b.alloc(QTOK, "s_bt")
    o_gt2 = b.alloc(QTOK, "s_gt")
    o_sq2 = b.alloc(512, "s_sq")
    o_row2 = b.alloc(1536, "s_row")
    o_bc2 = b.alloc(1024, "s_bc")
    o_cv2 = b.alloc(584, "s_cv")
    o_gh = b.alloc(16 * 128, "s_gatedh")
    o_gl = b.alloc(16 * 128, "s_gatedl")
    o_wsl2 = b.alloc(4096, "s_wsl")
    b_wsl = [B("wsl0"), B("wsl1")]
    o_ax = o_xq2
    o_xres = o_z

    b.dma(VR(O_SHARED, 0, 1, VAL_DIM), bz_row, writes=(b_shared,))
    for fb in range(4):
        b.mm(6, 512, V(b.O_ONESR, 1, 128), VR(O_SHARED + fb * 512, 0, 1, 512),
             True, True, reads=(bc, b_shared))
        b.vo(lambda e, fb=fb: e.tensor_copy(V(O_SHARED + 1024 + fb * 512, 128, 512), PS[6][:, 0:512]),
             reads=(psb[6],), writes=(b_shared,))
    O_BZBC = O_SHARED + 1024

    for q in range(NQ):
        b_x = B("sx")
        b_p = B("sp")
        b_gb = B("sgb")
        b_z = B("sz")
        b_gt = B("sgt")
        b_ax = B("sx")
        b_xr = B("sz")
        b.dma(V(o_xq2, 128, 16 * QCOLS).rearrange("p (k t) -> p k t", k=16),
              xspan_re[:, :, q * QCOLS:(q + 1) * QCOLS], writes=(b_x,))
        emit_ln(o_xq2, QCOLS, 0, QCOLS, o_sq2, o_row2, o_bc2, b_x)
        b.dma(xnh_re[:, :, q * QTOK:(q + 1) * QTOK],
              V(o_xq2, 128, 16 * QCOLS).rearrange("p (k t) -> p k t", k=16)
              [:, :, 0:QCOLS // 2].bitcast(BF16)[:, :, 4:QCOLS],
              reads=(b_x,), writes=(b_xnscr,))
        b.dma(xnl_re[:, :, q * QTOK:(q + 1) * QTOK],
              V(o_xq2, 128, 16 * QCOLS).rearrange("p (k t) -> p k t", k=16)
              [:, :, QCOLS // 2:QCOLS].bitcast(BF16)[:, :, 4:QCOLS],
              reads=(b_x,), writes=(b_xnscr,))
        emit_proj(o_p2, QCOLS, list(range(32)), list(range(32)),
                  o_xq2, QCOLS, 1, 259, wqkv_s, o_wsl2, b_wsl, b_x, b_p)
        if q == 0:
            b.vo(lambda e: e.tensor_scalar(
                V(o_p2, 128, 32 * QCOLS).rearrange("p (f t) -> p f t", f=32)[:, :, 1:4],
                V(o_p2, 128, 32 * QCOLS).rearrange("p (f t) -> p f t", f=32)[:, :, 1:4],
                V(O_TAILM, 128, 1), None, ALU.mult),
                reads=(bc,), writes=(b_p,))
        emit_conv_silu(o_p2, QCOLS, QTOK, 32, 0, o_cv2, b_p)
        emit_l2(o_p2, QCOLS, QTOK, list(range(8)), True, o_sq2, o_row2, b_p)
        emit_l2(o_p2, QCOLS, QTOK, list(range(8, 16)), False, o_sq2, o_row2, b_p)
        emit_gb(o_g2, o_bt2, o_gt2, o_xq2, QCOLS, 4, QTOK, b_x, b_gb)
        for kc in range(NKC):
            sl = kc % 2
            emit_wload(o_wsl2 + sl * 2048, wz_s, kc * 128, 2048, b_wsl[sl])
            for tc in range(2):
                lh = xsl(o_xq2, QCOLS, kc, 0, 4 + tc * 128, 128)
                ll = xsl(o_xq2, QCOLS, kc, 1, 4 + tc * 128, 128)
                for fb in range(4):
                    whi = bfv(o_wsl2 + sl * 2048 + fb * 256, 256)
                    wlo = bfv(o_wsl2 + sl * 2048 + 1024 + fb * 256, 256)
                    mm3(tc * 4 + fb, 512, lh, ll, whi, wlo, kc == 0, kc == NKC - 1,
                        reads=(b_wsl[sl], b_x))
        for tc in range(2):
            for fb in range(4):
                bank = tc * 4 + fb
                da = V(o_z + tc * VAL_DIM + fb * 512, 128, 512)
                b.vo(lambda e, da=da, bank=bank, fb=fb: e.tensor_tensor(
                    da, PS[bank][:, 0:512], V(O_BZBC + fb * 512, 128, 512), ALU.add),
                    reads=(psb[bank], b_shared), writes=(b_z,))

        def kqs2(kh):
            return (V(o_p2 + (8 + kh) * QCOLS + 4 + kqs2.c * CH, 128, 128),
                    V(o_p2 + kh * QCOLS + 4 + kqs2.c * CH, 128, 128))

        def vss(h):
            return V(o_p2 + (16 + h) * QCOLS + 4 + vss.c * CH, 128, 128)

        for ci in range(QTOK // CH):
            kqs2.c = ci
            vss.c = ci

            def sink(h, cv, reads, ci=ci, b_z=b_z, b_gt=b_gt):
                rc = V(O_RC, 128, 1)
                brc = B("rcol")
                csq = V(b.O_ZS, 128, 128)
                bzs = B("zsil")
                S.op("scalar", lambda e, cv=cv: e.activation(csq, cv, AF.Square),
                     reads=tuple(reads), writes=(bzs,))
                b.vo(lambda e: e.tensor_reduce(rc, csq, AX.X, ALU.add),
                     reads=(bzs,), writes=(brc,))
                b.vo(lambda e: e.tensor_scalar(rc, rc, 1.0 / DV, RMS_EPS, ALU.mult, ALU.add),
                     reads=(), writes=(brc,))
                b.vo(lambda e: e.reciprocal(rc, rc), reads=(), writes=(brc,))
                S.op("scalar", lambda e: e.activation(rc, rc, AF.Sqrt), reads=(), writes=(brc,))
                b.vo(lambda e, cv=cv: e.tensor_scalar(cv, cv, rc, None, ALU.mult),
                     reads=(brc,), writes=tuple(reads))
                b.vo(lambda e, cv=cv: e.tensor_tensor(cv, cv, V(b.O_NWBC, 128, 128), ALU.mult),
                     reads=(bc,), writes=tuple(reads))
                zs = V(b.O_ZS, 128, 128)
                zsl = V(o_z + ci * VAL_DIM + h * 128, 128, 128)
                S.op("scalar", lambda e, zsl=zsl: e.activation(zs, zsl, AF.Silu),
                     reads=(b_z,), writes=(bzs,))
                b.vo(lambda e, cv=cv: e.tensor_tensor(cv, cv, zs, ALU.mult),
                     reads=(bzs,), writes=tuple(reads))
                b.trans(5, 128, 128, cv, V(b.O_ID128, 128, 128), reads=tuple(reads))
                hi = bfv(o_gh + h * 128, 128)[:, ci * 128:ci * 128 + 128]
                lo = bfv(o_gl + h * 128, 128)[:, ci * 128:ci * 128 + 128]
                S.op("scalar", lambda e, hi=hi: e.copy(hi, PS[5][:, 0:128]),
                     reads=(psb[5],), writes=(b_gt,))
                b.vo(lambda e, lo=lo, hi=hi: e.tensor_tensor(lo, PS[5][:, 0:128], hi, ALU.subtract),
                     reads=(psb[5],), writes=(b_gt,))

            b.emit_chunk(kqs2, vss, o_g2, o_bt2, ci * CH, b_p, b_gb, True, sink)

        b.dma(V(o_xres, 128, 16 * QTOK).rearrange("p (k t) -> p k t", k=16),
              xspan_re[:, :, q * QCOLS + 4:(q + 1) * QCOLS], writes=(b_xr,))
        for fo in range(16):
            sl = fo % 2
            emit_wload(o_wsl2 + sl * 2048, wout_s, fo * 128, 2048, b_wsl[sl])
            bank = 6 if fo % 2 == 0 else 7
            for kc in range(NKC):
                whi = bfv(o_wsl2 + sl * 2048 + kc * 64, 64)
                wlo = bfv(o_wsl2 + sl * 2048 + 1024 + kc * 64, 64)
                ghi = bfv(o_gh + kc * 128, 128)
                glo = bfv(o_gl + kc * 128, 128)
                mm3(bank, QTOK, whi, wlo, ghi, glo, kc == 0, kc == NKC - 1,
                    reads=(b_wsl[sl], b_gt))
            da = V(o_ax + fo * QCOLS, 128, QTOK)
            b.vo(lambda e, da=da, bank=bank, fo=fo: e.scalar_tensor_tensor(
                da, PS[bank][:, 0:QTOK], V(O_BOUT + fo, 128, 1),
                V(o_xres + fo * QTOK, 128, QTOK), ALU.add, ALU.add),
                reads=(psb[bank], bc, b_xr), writes=(b_ax,))
        b.dma(ax_re[:, :, q * QTOK:(q + 1) * QTOK],
              V(o_ax, 128, 16 * QCOLS).rearrange("p (k t) -> p k t", k=16)[:, :, 0:QTOK],
              reads=(b_ax,), writes=(b_axscr,))

    if debug:
        b.dma(d_ax, ax_scr, reads=(b_axscr,))

    if "ffn" not in phases:
        fence = [(s, v) for s, v in S.counts.items() if "_d" in s and v > 0]
        S.op("sync", lambda e: e.nop(), extra=tuple(fence))
        S.emit(nc)
        return nc
    wall()
    b._cur = BIG0
    o_fxn = b.alloc(16 * 512, "f_xn")
    o_acc = b.alloc(16 * 512, "f_acc")
    o_act = b.alloc(11 * 512, "f_act")
    o_fsc = b.alloc(512, "f_scr")
    o_cmb = b.alloc(512, "f_cmb")
    o_cb = b.alloc(512, "f_cb")
    o_gs = b.alloc(8, "f_gs")
    o_gs2 = b.alloc(8, "f_gs2")
    o_gcl = b.alloc(4, "f_gcol")
    o_bd8 = b.alloc(512, "f_bd8")
    o_wsl3 = b.alloc(4096, "f_wsl")
    b_wsl = [B("wsl0"), B("wsl1")]

    def fxn_sl(kc, part, t0, tn):
        base = o_fxn + kc * 512 + part * 256
        return b.AR[0:128, base:base + 256].bitcast(BF16)[:, t0:t0 + tn]

    for half in range(2):
        b_x = B("fx")
        b_a = B("fa")
        b_cmb = B("fcmb")
        b_st = B("wstage")
        b.dma(V(o_fxn, 128, 16 * 512).rearrange("p (k t) -> p k t", k=16)
              [:, :, 0:256].bitcast(BF16),
              xnh_re[:, :, half * 512:(half + 1) * 512], reads=(b_xnscr,), writes=(b_x,))
        b.dma(V(o_fxn, 128, 16 * 512).rearrange("p (k t) -> p k t", k=16)
              [:, :, 256:512].bitcast(BF16),
              xnl_re[:, :, half * 512:(half + 1) * 512], reads=(b_xnscr,), writes=(b_x,))
        b.dma(V(o_acc, 128, 16 * 512).rearrange("p (k t) -> p k t", k=16),
              ax_re[:, :, half * 512:(half + 1) * 512], reads=(b_axscr,), writes=(b_a,))
        for tc in range(4):
            for kc in range(NKC):
                lh = fxn_sl(kc, 0, tc * 128, 128)
                ll = fxn_sl(kc, 1, tc * 128, 128)
                wgh = bfv(O_WGH + kc * 4, 4)
                wgl = bfv(O_WGL + kc * 4, 4)
                mm3(6, 8, lh, ll, wgh, wgl, kc == 0, kc == NKC - 1, reads=(bc, b_x))
            sc_ = V(o_gs, 128, 8)
            mc = V(o_gcl, 128, 1)
            m2c = V(o_gcl + 1, 128, 1)
            rs_ = V(o_gcl + 2, 128, 1)
            bsc, bmc = B("fsc"), B("fmc")
            b.vo(lambda e: e.tensor_reduce(mc, PS[6][:, 0:8], AX.X, ALU.max),
                 reads=(psb[6],), writes=(bmc,))
            b.vo(lambda e: e.tensor_scalar(mc, mc, -1.0, None, ALU.mult), reads=(), writes=(bmc,))
            S.op("scalar", lambda e: e.activation(sc_, PS[6][:, 0:8], AF.Exp, bias=mc),
                 reads=(psb[6], bmc), writes=(bsc,))
            b.vo(lambda e: e.tensor_reduce(rs_, sc_, AX.X, ALU.add), reads=(bsc,), writes=(bmc,))
            b.vo(lambda e: e.reciprocal(rs_, rs_), reads=(), writes=(bmc,))
            b.vo(lambda e: e.tensor_scalar(sc_, sc_, rs_, None, ALU.mult),
                 reads=(bmc,), writes=(bsc,))
            b.vo(lambda e: e.tensor_reduce(mc, sc_, AX.X, ALU.max), reads=(bsc,), writes=(bmc,))
            ge1 = V(o_gs2, 128, 8)
            bge = B("fge")
            b.vo(lambda e: e.tensor_scalar(ge1, sc_, mc, None, ALU.is_ge),
                 reads=(bsc, bmc), writes=(bge,))
            b.vo(lambda e: e.scalar_tensor_tensor(ge1, ge1, -1e9, sc_, ALU.mult, ALU.add),
                 reads=(bsc,), writes=(bge,))
            b.vo(lambda e: e.tensor_reduce(m2c, ge1, AX.X, ALU.max), reads=(bge,), writes=(bmc,))
            b.vo(lambda e: e.tensor_scalar(ge1, sc_, m2c, None, ALU.is_ge),
                 reads=(bsc, bmc), writes=(bge,))
            b.vo(lambda e: e.tensor_tensor(ge1, ge1, sc_, ALU.mult), reads=(bsc,), writes=(bge,))
            b.trans(5, 8, 128, ge1, V(b.O_ID128, 128, 128), reads=(bge,))
            b.vo(lambda e, tc=tc: e.tensor_copy(VR(o_cmb + tc * 128, 0, 8, 128), PS[5][0:8, 0:128]),
                 reads=(psb[5],), writes=(b_cmb,))
        for ei in range(NEXP):
            bcb = B("fcb")
            bd8 = VR(o_bd8, 0, 8, 512)
            b.vo(lambda e, ei=ei: e.tensor_scalar(bd8, VR(o_cmb, 0, 8, 512),
                                                  V(b.O_ID8 + ei, 8, 1), None, ALU.mult),
                 reads=(b_cmb, bc), writes=(B("fbd8"),))
            b.mm(6, 512, V(b.O_ONES8, 8, 128), bd8, True, True, reads=(B("fbd8"), bc))
            b.vo(lambda e: e.tensor_copy(V(o_cb, 128, 512), PS[6][:, 0:512]),
                 reads=(psb[6],), writes=(bcb,))
            bact = B("fact")
            for fp in range(11):
                r0 = (ei * 22 + fp) * 128
                r1 = (ei * 22 + fp + 11) * 128
                emit_wload(o_wsl3, wgup_s, r0, 2048, b_wsl[0])
                emit_wload(o_wsl3 + 2048, wgup_s, r1, 2048, b_wsl[1])
                for kc in range(NKC):
                    xh = fxn_sl(kc, 0, 0, 512)
                    xlo = fxn_sl(kc, 1, 0, 512)
                    mm3(0, 512, bfv(o_wsl3 + kc * 64, 64), bfv(o_wsl3 + 1024 + kc * 64, 64),
                        xh, xlo, kc == 0, kc == NKC - 1, reads=(b_wsl[0], b_x))
                for kc in range(NKC):
                    xh = fxn_sl(kc, 0, 0, 512)
                    xlo = fxn_sl(kc, 1, 0, 512)
                    mm3(1, 512, bfv(o_wsl3 + 2048 + kc * 64, 64), bfv(o_wsl3 + 3072 + kc * 64, 64),
                        xh, xlo, kc == 0, kc == NKC - 1, reads=(b_wsl[1], b_x))
                fs_ = V(o_fsc, 128, 512)
                bfs = B("ffs")
                S.op("scalar", lambda e, fs_=fs_: e.activation(fs_, PS[0][:, 0:512], AF.Silu),
                     reads=(psb[0],), writes=(bfs,))
                b.vo(lambda e, fs_=fs_: e.tensor_tensor(fs_, fs_, PS[1][:, 0:512], ALU.mult),
                     reads=(psb[1],), writes=(bfs,))
                b.vo(lambda e, fs_=fs_: e.tensor_tensor(fs_, fs_, V(o_cb, 128, 512), ALU.mult),
                     reads=(bcb,), writes=(bfs,))
                ahi = bfv(o_act + fp * 512, 256)
                alo_ = bfv(o_act + fp * 512 + 256, 256)
                S.op("scalar", lambda e, ahi=ahi, fs_=fs_: e.copy(ahi, fs_),
                     reads=(bfs,), writes=(bact,))
                b.vo(lambda e, alo_=alo_, fs_=fs_, ahi=ahi: e.tensor_tensor(alo_, fs_, ahi, ALU.subtract),
                     reads=(bfs,), writes=(bact,))
            for fo in range(16):
                sl = fo % 2
                r0 = (ei * 16 + fo) * 128
                emit_wload(o_wsl3 + sl * 2048, wdwn_s, r0, FF, b_wsl[sl])
                bank = 2 + (fo % 2)
                for kc in range(11):
                    whi = bfv(o_wsl3 + sl * 2048 + kc * 64, 64)
                    wlo = bfv(o_wsl3 + sl * 2048 + 704 + kc * 64, 64)
                    ahi = bfv(o_act + kc * 512, 256)
                    alo_ = bfv(o_act + kc * 512 + 256, 256)
                    mm3(bank, 512, whi, wlo, ahi, alo_, kc == 0, kc == 10,
                        reads=(b_wsl[sl], bact))
                av = V(o_acc + fo * 512, 128, 512)
                b.vo(lambda e, av=av, bank=bank: e.tensor_tensor(av, av, PS[bank][:, 0:512], ALU.add),
                     reads=(psb[bank],), writes=(b_a,))
        for st in range(2):
            bact = B("fact")
            for fp in range(11):
                ft = st * 11 + fp
                emit_wload(o_wsl3, wsgu_s, ft * 128, 2048, b_wsl[0])
                emit_wload(o_wsl3 + 2048, wsgu_s, (ft + 22) * 128, 2048, b_wsl[1])
                for kc in range(NKC):
                    xh = fxn_sl(kc, 0, 0, 512)
                    xlo = fxn_sl(kc, 1, 0, 512)
                    mm3(0, 512, bfv(o_wsl3 + kc * 64, 64), bfv(o_wsl3 + 1024 + kc * 64, 64),
                        xh, xlo, kc == 0, kc == NKC - 1, reads=(b_wsl[0], b_x))
                for kc in range(NKC):
                    xh = fxn_sl(kc, 0, 0, 512)
                    xlo = fxn_sl(kc, 1, 0, 512)
                    mm3(1, 512, bfv(o_wsl3 + 2048 + kc * 64, 64), bfv(o_wsl3 + 3072 + kc * 64, 64),
                        xh, xlo, kc == 0, kc == NKC - 1, reads=(b_wsl[1], b_x))
                fs_ = V(o_fsc, 128, 512)
                bfs = B("ffs")
                S.op("scalar", lambda e, fs_=fs_: e.activation(fs_, PS[0][:, 0:512], AF.Silu),
                     reads=(psb[0],), writes=(bfs,))
                b.vo(lambda e, fs_=fs_: e.tensor_tensor(fs_, fs_, PS[1][:, 0:512], ALU.mult),
                     reads=(psb[1],), writes=(bfs,))
                ahi = bfv(o_act + fp * 512, 256)
                alo_ = bfv(o_act + fp * 512 + 256, 256)
                S.op("scalar", lambda e, ahi=ahi, fs_=fs_: e.copy(ahi, fs_),
                     reads=(bfs,), writes=(bact,))
                b.vo(lambda e, alo_=alo_, fs_=fs_, ahi=ahi: e.tensor_tensor(alo_, fs_, ahi, ALU.subtract),
                     reads=(bfs,), writes=(bact,))
            for fo in range(16):
                sl = fo % 2
                b.dma(VR(o_wsl3 + sl * 2048, 0, 128, FF).bitcast(mybir.dt.bfloat16)
                      .rearrange("p (h t) -> p h t", h=2),
                      wsd_s[fo * 128:(fo + 1) * 128, :].rearrange("p (h s) -> p h s", h=2)
                      [:, :, st * FF:(st + 1) * FF],
                      reads=(B(("wscr", "wsd_s")),), writes=(b_wsl[sl],))
                bank = 2 + (fo % 2)
                for kc in range(11):
                    whi = bfv(o_wsl3 + sl * 2048 + kc * 64, 64)
                    wlo = bfv(o_wsl3 + sl * 2048 + 704 + kc * 64, 64)
                    ahi = bfv(o_act + kc * 512, 256)
                    alo_ = bfv(o_act + kc * 512 + 256, 256)
                    mm3(bank, 512, whi, wlo, ahi, alo_, kc == 0, kc == 10,
                        reads=(b_wsl[sl], bact))
                av = V(o_acc + fo * 512, 128, 512)
                b.vo(lambda e, av=av, bank=bank: e.tensor_tensor(av, av, PS[bank][:, 0:512], ALU.add),
                     reads=(psb[bank],), writes=(b_a,))
        b.dma(out_re[:, :, half * 512:(half + 1) * 512],
              V(o_acc, 128, 16 * 512).rearrange("p (k t) -> p k t", k=16),
              reads=(b_a,), writes=(B("fout"),))

    fence = []
    for s, v in S.counts.items():
        if "_d" in s and v > 0:
            fence.append((s, v))
    S.op("sync", lambda e: e.nop(), extra=tuple(fence))
    S.emit(nc)
    return nc



def _tile_lhsT(W, K=HID):
    F = W.shape[0]
    nft = F // 128
    nkc = K // 128
    H = W.reshape(nft, 128, nkc, 128).transpose(0, 3, 2, 1).reshape(nft * 128, K)
    return np.ascontiguousarray(H)


def _prep_weights(inp):
    f32 = np.float32
    g = lambda k: np.asarray(inp[k], dtype=f32)
    W = {}
    W["wqkv"] = _tile_lhsT(g("L0_attn_Wqkv"))
    W["wz"] = np.ascontiguousarray(g("L0_attn_Wz").T.reshape(NKC * 128, HID))
    W["wout_w"] = _tile_lhsT(g("L0_attn_Wout"), K=VAL_DIM)
    gup = g("L0_ffn_gate_up_weights")
    W["wgup"] = np.concatenate([_tile_lhsT(gup[e]) for e in range(NEXP)], 0)
    dwn = g("L0_ffn_down_weights")
    W["wdwn"] = np.concatenate([_tile_lhsT(dwn[e], K=FF) for e in range(NEXP)], 0)
    W["wsgu"] = _tile_lhsT(g("L0_ffn_shared_gate_up"))
    W["wsd"] = _tile_lhsT(g("L0_ffn_shared_down"), K=FS)
    W["wa_t"] = np.ascontiguousarray(
        g("L0_attn_Wa").T.reshape(NKC, 128, 16).transpose(1, 0, 2).reshape(128, NKC * 16))
    W["wb_t"] = np.ascontiguousarray(
        g("L0_attn_Wb").T.reshape(NKC, 128, 16).transpose(1, 0, 2).reshape(128, NKC * 16))
    W["wgate"] = np.ascontiguousarray(
        g("L0_ffn_gate_weight").T.reshape(NKC, 128, 8).transpose(1, 0, 2).reshape(128, NKC * 8))
    W["bqkv_t"] = np.ascontiguousarray(g("L0_attn_bqkv").reshape(32, 128).T)
    W["cb_t"] = np.ascontiguousarray(g("L0_attn_conv_b").reshape(32, 128).T)
    W["cw_t"] = np.ascontiguousarray(
        g("L0_attn_conv_w").reshape(32, 128, 4).transpose(1, 0, 2).reshape(128, 128))
    W["bz_row"] = g("L0_attn_bz")[None, :]
    W["ba_c"] = g("L0_attn_ba")[:, None]
    W["bb_c"] = g("L0_attn_bb")[:, None]
    W["dtb"] = g("L0_attn_dt_bias")[:, None]
    W["alog"] = g("L0_attn_A_log")[:, None]
    W["nw_row"] = g("L0_attn_norm_w")[None, :]
    W["bout_t"] = np.ascontiguousarray(g("L0_attn_bout").reshape(16, 128).T)
    W["ones_row"] = np.ones((1, 128), f32)
    W["ones_col"] = np.ones((128, 1), f32)
    W["ones8"] = np.ones((8, 128), f32)
    W["id8"] = np.eye(8, dtype=f32)
    W["id16"] = np.eye(16, dtype=f32)
    W["id32"] = np.eye(32, dtype=f32)
    W["id128"] = np.eye(128, dtype=f32)
    W["blk8"] = np.repeat(np.eye(8, dtype=f32), 128, axis=1)
    W["mask_su"] = np.triu(np.ones((128, 128), f32), 1)
    W["mask_sl"] = np.tril(np.ones((128, 128), f32), -1)
    W["mask_iu"] = np.triu(np.ones((128, 128), f32), 0)
    return W


def _prep_core(x, bidx, t0):
    f32 = np.float32
    xb = np.asarray(x[bidx], dtype=f32)
    xpre = np.zeros((NPT * PCOLS, HID), f32)
    premask = np.zeros((1, NPT * PCOLS), f32)
    start0 = t0 - PREF
    for pt in range(NPT):
        ts = start0 + pt * PTILE
        for j in range(PCOLS):
            tok = ts - 3 + (j - 1) + 1 if False else ts + (j - 4)
            if j == 0:
                continue
            tok = ts + (j - 4)
            if 0 <= tok < 4096:
                xpre[pt * PCOLS + j] = xb[tok]
                premask[0, pt * PCOLS + j] = 1.0
    xspan = np.zeros((NQ * QCOLS, HID), f32)
    for qq in range(NQ):
        ts = t0 + qq * QTOK
        for j in range(1, QCOLS):
            tok = ts + (j - 4)
            if 0 <= tok < 4096:
                xspan[qq * QCOLS + j] = xb[tok]
    tailm = np.full((128, 1), 0.0 if t0 == 0 else 1.0, f32)
    return {
        "xpre": np.ascontiguousarray(xpre.T),
        "premask": premask,
        "xspan": np.ascontiguousarray(xspan.T),
        "tailm": tailm,
    }


_NC_CACHE = {}


def get_nc(debug=False):
    if debug not in _NC_CACHE:
        _NC_CACHE[debug] = build_full(debug)
    return _NC_CACHE[debug]


def kernel(**inputs):
    x = np.asarray(inputs["L0_attn_hidden_states"], dtype=np.float32)
    W = _prep_weights(inputs)
    nc = get_nc(False)
    in_maps = []
    for c in range(NCORES):
        bidx, t0 = c // 4, (c % 4) * SPAN
        m = dict(W)
        m.update(_prep_core(x, bidx, t0))
        in_maps.append(m)
    res = run_bass_kernel_spmd(nc, in_maps, core_ids=list(range(NCORES)))
    outf = np.empty((2, 4096, HID), np.float32)
    for c in range(NCORES):
        bidx, t0 = c // 4, (c % 4) * SPAN
        outf[bidx, t0:t0 + SPAN] = res.results[c]["out"].T
    return outf


# version 08
# speedup vs baseline: 1.0712x; 1.0712x over previous
import os
from collections import defaultdict

import numpy as np
import concourse.bass as bass
import concourse.mybir as mybir
from concourse.bass_utils import run_bass_kernel_spmd

F32 = mybir.dt.float32
AF = mybir.ActivationFunctionType
ALU = mybir.AluOpType
AX = mybir.AxisListType

HID = 2048
HK = 8
HV = 16
DK = 128
DV = 128
KCONV = 4
KEY_DIM = HK * DK
VAL_DIM = HV * DV
CONV_DIM = 2 * KEY_DIM + VAL_DIM
RMS_EPS = 1e-6
L2_EPS = 1e-6
LN_EPS = 1e-5
NEXP = 8
FF = 1408
FS = 2816
NKC = HID // 128

SPAN = 1024
PREF = 3072
PTILE = 512
NPT = PREF // PTILE
PCOLS = PTILE + 4
QTOK = 256
NQ = SPAN // QTOK
QCOLS = QTOK + 4
CH = 128

NCORES = 8



class Buf:
    __slots__ = ("writer", "readers", "name")
    def __init__(self, name=""):
        self.writer = None
        self.readers = []
        self.name = name


class PH:
    __slots__ = ("eng", "tok")
    def __init__(self, eng):
        self.eng = eng
        self.tok = None


class Sched:
    ENGINES = ("tensor", "vector", "scalar", "gpsimd", "sync")
    KSEM = 4

    def __init__(self):
        self.ops = {e: [] for e in self.ENGINES}
        self.counts = defaultdict(int)
        self.opidx = defaultdict(int)
        self.waiters = defaultdict(list)
        self.pending_ph = {e: [] for e in self.ENGINES}
        self._rr = 0

    def _force_token(self, eng):
        return self.op(eng, lambda e: e.nop(), reads=(), writes=(), dma=False,
                       extra=(), quiet=False, _force=True)

    def _resolve(self, d, eng):
        if not isinstance(d, PH):
            return d
        if d.tok is not None:
            return d.tok
        if d.eng == eng:
            return None
        return self._force_token(d.eng)

    def op(self, eng, fn, reads=(), writes=(), dma=False, extra=(), quiet=False,
           _force=False):
        deps = []
        for d in extra:
            if d is None:
                continue
            d = self._resolve(d, eng)
            if d is not None:
                deps.append(d)
        for b in reads:
            w = self._resolve(b.writer, eng) if b.writer is not None else None
            if w is not None:
                deps.append(w)
        for b in writes:
            w = self._resolve(b.writer, eng) if b.writer is not None else None
            if w is not None:
                deps.append(w)
            for r in list(b.readers):
                r = self._resolve(r, eng)
                if r is not None:
                    deps.append(r)
        if quiet and not writes and not dma:
            ph = PH(eng)
            self.ops[eng].append((fn, deps, None, dma))
            for (s, v) in deps:
                self.waiters[s].append((v, ph))
            for b in reads:
                b.readers.append(ph)
            self.pending_ph[eng].append(ph)
            return ph
        kind = "d" if dma else "c"
        idx = self.opidx[(eng, kind)]
        self.opidx[(eng, kind)] = idx + 1
        sem = f"{eng}_{kind}{idx % self.KSEM}"
        inc = 16 if dma else 1
        start_val = self.counts[sem]
        wl = self.waiters[sem]
        if wl:
            keep = []
            for (v, wtok) in wl:
                if v == start_val:
                    if isinstance(wtok, PH):
                        if wtok.tok is not None:
                            deps.append(wtok.tok)
                        elif wtok.eng != eng:
                            raise AssertionError("unresolved waiter placeholder")
                    else:
                        deps.append(wtok)
                elif v > start_val:
                    keep.append((v, wtok))
            self.waiters[sem] = keep
        self.counts[sem] = start_val + inc
        tok = (sem, start_val + inc)
        self.ops[eng].append((fn, deps, tok, dma))
        for (s, v) in deps:
            self.waiters[s].append((v, tok))
        for ph in self.pending_ph[eng]:
            ph.tok = tok
        self.pending_ph[eng].clear()
        for b in reads:
            b.readers.append(tok)
        for b in writes:
            b.writer = tok
            b.readers = []
        return tok

    def ev(self):
        self._rr += 1
        return "vector" if self._rr % 2 else "scalar"

    def emit(self, nc):
        sems = {}
        for e in self.ENGINES:
            for kind in ("c", "d"):
                for i in range(self.KSEM):
                    name = f"{e}_{kind}{i}"
                    sems[name] = nc.semaphore(f"S_{name}").__enter__()
        blk = nc.Block()
        block = blk.__enter__()

        def mk(eng_name):
            own = eng_name + "_c" if eng_name == "tensor" else "\0"
            def body(eng):
                watermark = defaultdict(int)
                for fn, deps, tok, dma in self.ops[eng_name]:
                    need = {}
                    for (sem, val) in deps:
                        if sem.startswith(own):
                            continue
                        if val > watermark[sem] and val > need.get(sem, 0):
                            need[sem] = val
                    for sem, val in need.items():
                        eng.wait_ge(sems[sem], val)
                        watermark[sem] = val
                    inst = fn(eng)
                    if tok is not None:
                        inst.then_inc(sems[tok[0]], 16 if dma else 1)
            return body

        block.sync(mk("sync"))
        block.tensor(mk("tensor"))
        block.vector(mk("vector"))
        block.scalar(mk("scalar"))
        block.gpsimd(mk("gpsimd"))
        blk.__exit__(None, None, None)



class Bld:

    def __init__(self, arena_cols=51500):
        self.nc = bass.Bass("TRN2", target_bir_lowering=False)
        self.S = Sched()
        self.AR = self.nc.alloc_sbuf_tensor("AR", [128, arena_cols], F32).ap()
        self.PS = [self.nc.alloc_psum_tensor(f"PB{i}", [128, 512], F32).ap() for i in range(8)]
        self.psb = [Buf(f"ps{i}") for i in range(8)]
        self._cur = 0
        self._arena_cols = arena_cols
        self.bufs = {}
        self.marks = []

    def alloc(self, n, name=""):
        o = self._cur
        self._cur += n
        assert self._cur <= self._arena_cols, f"arena overflow at {name}: {self._cur}"
        return o

    def push(self):
        self.marks.append(self._cur)

    def pop(self):
        self._cur = self.marks.pop()

    def V(self, off, parts, cols):
        return self.AR[0:parts, off:off + cols]

    def VR(self, off, p0, p1, cols):
        return self.AR[p0:p1, off:off + cols]

    def B(self, key):
        if key not in self.bufs:
            self.bufs[key] = Buf(str(key))
        return self.bufs[key]

    def dma(self, dst, src, reads=(), writes=(), eng="sync"):
        return self.S.op(eng, lambda e: e.dma_start(out=dst, in_=src),
                         reads=reads, writes=writes, dma=True)

    def mm(self, bank, n, lhsT, rhs, start, stop, reads=(), c0=0, parts=128):
        ps = self.PS[bank][0:parts, c0:c0 + n]
        extra = ()
        if start:
            extra = tuple(self.psb[bank].readers)
            if self.psb[bank].writer is not None:
                extra += (self.psb[bank].writer,)
        return self.S.op("tensor",
                         lambda e: e.matmul(ps, lhsT, rhs, start=start, stop=stop),
                         reads=reads,
                         writes=(self.psb[bank],) if stop else (), extra=extra,
                         quiet=False)

    def trans(self, bank, out_parts, out_cols, in_, ident, reads=(), c0=0):
        ps = self.PS[bank][0:out_parts, c0:c0 + out_cols]
        extra = tuple(self.psb[bank].readers)
        if self.psb[bank].writer is not None:
            extra += (self.psb[bank].writer,)
        return self.S.op("tensor", lambda e: e.transpose(ps, in_, ident),
                         reads=reads, writes=(self.psb[bank],), extra=extra)

    def vo(self, fn, reads=(), writes=(), eng="vector"):
        return self.S.op(eng, fn, reads=reads, writes=writes)

    def cp(self, dst, src, reads=(), writes=()):
        eng = self.S.ev()
        if eng == "scalar":
            return self.S.op(eng, lambda e: e.copy(dst, src), reads=reads, writes=writes)
        return self.S.op(eng, lambda e: e.tensor_copy(dst, src), reads=reads, writes=writes)

    def ev2(self):
        return "vector"

    def alloc_state(self):
        self.O_S = self.alloc(HV * DK, "S")
        self.b_S = [self.B(("S", h)) for h in range(HV)]
        for h in range(HV):
            self.vo(lambda e, h=h: e.memset(self.V(self.O_S + h * DK, 128, DK), 0.0),
                    writes=(self.b_S[h],))

    def alloc_scan_ws(self):
        a = self.alloc
        self.O_GBTM = a(32, "gbtm")
        self.O_CGTM = a(16, "cgtm")
        self.O_LAM0 = a(16, "lam0")
        self.O_CCOL = a(16, "ccol")
        self.O_CGF = [a(128, "cgf0"), a(128, "cgf1")]
        self.O_BFM = [a(128, "bfm0"), a(128, "bfm1")]
        self.O_BD = a(1024, "bd")
        self.O_E = a(1024, "etile")
        self.O_EP = a(1024, "eptile")
        self.O_BB = a(1024, "bbeta")
        self.O_KK = a(512, "kk4")
        self.O_QK = a(512, "qk4")
        self.O_G = a(1024, "gtile")
        self.O_E127 = a(8, "e127")
        self.O_LAMC = a(8, "lamc")
        self.O_KTM = a(1024, "ktm")
        self.O_VTM = a(512, "vtm")
        self.O_PA = a(512, "pa")
        self.O_PB = a(512, "pb")
        self.O_X = a(1024, "xsolve")
        self.O_UB = a(512, "ubar")
        self.O_WFM = a(512, "wfm")
        self.O_CORE = a(128, "core")
        self.O_ZS = a(128, "zsil")

    def alloc_scan_consts(self):
        a = self.alloc
        self.O_ONESR = a(128, "onesr")
        self.O_ONESC = a(1, "onesc")
        self.O_ONES8 = a(128, "ones8")
        self.O_ID8 = a(8, "id8")
        self.O_ID16 = a(16, "id16")
        self.O_ID32 = a(32, "id32")
        self.O_ID128 = a(128, "id128")
        self.O_MSU = a(128, "msu")
        self.O_MSL = a(128, "msl")
        self.O_MIU = a(128, "miu")
        self.O_BLK8 = a(1024, "blk8")
        self.O_NWBC = a(128, "nwbc")

    def load_scan_consts(self, din):
        V, b = self.V, self.B("const")
        for dst, name, shape in [
            (V(self.O_ONESR, 1, 128), "ones_row", [1, 128]),
            (V(self.O_ONESC, 128, 1), "ones_col", [128, 1]),
            (V(self.O_ONES8, 8, 128), "ones8", [8, 128]),
            (V(self.O_ID8, 8, 8), "id8", [8, 8]),
            (V(self.O_ID16, 16, 16), "id16", [16, 16]),
            (V(self.O_ID32, 32, 32), "id32", [32, 32]),
            (V(self.O_ID128, 128, 128), "id128", [128, 128]),
            (V(self.O_MSU, 128, 128), "mask_su", [128, 128]),
            (V(self.O_MSL, 128, 128), "mask_sl", [128, 128]),
            (V(self.O_MIU, 128, 128), "mask_iu", [128, 128]),
            (V(self.O_BLK8, 8, 1024), "blk8", [8, 1024]),
        ]:
            self.dma(dst, din(name, shape), writes=(b,))

    def emit_chunk(self, kq_slices, v_slices, o_g, o_bt, gcol, b_p, b_gb, with_out,
                   core_sink=None, probe=None):
        S, V, VR, B, PS = self.S, self.V, self.VR, self.B, self.PS
        psb = self.psb
        bc = B("const")

        bgbt = B("gbtm")
        self.trans(4, 128, 16, VR(o_g + gcol, 0, 16, CH), V(self.O_ID16, 16, 16),
                   reads=(b_gb,))
        self.vo(lambda e: e.tensor_copy(V(self.O_GBTM, 128, 16), PS[4][:, 0:16]),
                reads=(psb[4],), writes=(bgbt,))
        self.trans(5, 128, 16, VR(o_bt + gcol, 0, 16, CH), V(self.O_ID16, 16, 16),
                   reads=(b_gb,))
        self.vo(lambda e: e.tensor_copy(VR(self.O_GBTM + 16, 0, 128, 16), PS[5][:, 0:16]),
                reads=(psb[5],), writes=(bgbt,))
        mm = self.mm
        mm(5, 16, V(self.O_MIU, 128, 128), V(self.O_GBTM, 128, 16), True, True,
           reads=(bc, bgbt))
        bcg = B("cgtm")
        self.vo(lambda e: e.tensor_copy(V(self.O_CGTM, 128, 16), PS[5][:, 0:16]),
                reads=(psb[5],), writes=(bcg,))
        if probe: probe("cgtm", V(self.O_CGTM, 128, 16), (bcg,))
        blam = B("lam0")
        S.op("scalar", lambda e: e.activation(V(self.O_LAM0, 128, 16), V(self.O_CGTM, 128, 16), AF.Exp),
             reads=(bcg,), writes=(blam,))
        bccol = B("ccol")
        self.vo(lambda e: e.tensor_tensor(V(self.O_CCOL, 128, 16), V(self.O_LAM0, 128, 16),
                                          VR(self.O_GBTM + 16, 0, 128, 16), ALU.mult),
                reads=(blam, bgbt), writes=(bccol,))
        bcgf = [B("cgf0"), B("cgf1")]
        bbfm = [B("bfm0"), B("bfm1")]
        for half in range(2):
            self.trans(4, 8, 128, V(self.O_CGTM + 8 * half, 128, 8),
                       V(self.O_ID128, 128, 128), reads=(bcg,), c0=128 * half)
            self.vo(lambda e, half=half: e.tensor_copy(
                V(self.O_CGF[half], 8, 128), PS[4][0:8, 128 * half:128 * half + 128]),
                reads=(psb[4],), writes=(bcgf[half],))
        for half in range(2):
            self.trans(5, 8, 128, VR(self.O_GBTM + 16 + 8 * half, 0, 128, 8),
                       V(self.O_ID128, 128, 128), reads=(bgbt,), c0=128 * half)
            self.vo(lambda e, half=half: e.tensor_scalar(
                V(self.O_BFM[half], 8, 128), PS[5][0:8, 128 * half:128 * half + 128],
                -1.0, None, ALU.mult),
                reads=(psb[5],), writes=(bbfm[half],))

        bktm = B("ktm")
        for kh in range(HK):
            k_ap, _ = kq_slices(kh)
            bank = 4 if kh % 2 == 0 else 5
            self.trans(bank, 128, 128, k_ap, V(self.O_ID128, 128, 128), reads=(b_p,))
            self.cp(V(self.O_KTM + kh * 128, 128, 128), PS[bank][:, 0:128],
                    reads=(psb[bank],), writes=(bktm,))

        for bt in range(2):
            bD = B("etile")
            bDP = B("eptile")
            bBB = B("bbeta")
            bKK = B("kk4")
            bQK = B("qk4")
            bG = B("gtile")
            bE127 = B("e127")
            bLC = B("lamc")

            for i in range(4):
                kh = 4 * bt + i
                k_ap, _ = kq_slices(kh)
                mm(0, 128, k_ap, k_ap, True, True, reads=(b_p,), c0=i * 128)
            tkk = self.vo(lambda e: e.tensor_copy(V(self.O_KK, 128, 512), PS[0][:, 0:512]),
                          reads=(psb[0],), writes=(bKK,))
            if with_out:
                for i in range(4):
                    kh = 4 * bt + i
                    k_ap, q_ap = kq_slices(kh)
                    mm(1, 128, k_ap, q_ap, True, True, reads=(b_p,), c0=i * 128)
                self.vo(lambda e: e.tensor_copy(V(self.O_QK, 128, 512), PS[1][:, 0:512]),
                        reads=(psb[1],), writes=(bQK,))

            bBD = B("bd")
            cgf = V(self.O_CGF[bt], 8, 128)
            cgf_b = cgf.rearrange("p (o f) -> p o f", o=1).broadcast_to([8, 8, 128])
            blkv = V(self.O_BLK8, 8, 1024).rearrange("p (o f) -> p o f", o=8)
            bdv = V(self.O_BD, 8, 1024).rearrange("p (o f) -> p o f", o=8)
            self.vo(lambda e, bdv=bdv, cgf_b=cgf_b, blkv=blkv: e.tensor_tensor(bdv, cgf_b, blkv, ALU.mult),
                    reads=(bcgf[bt], bc), writes=(bBD,))
            if probe: probe(f"BD{bt}", self.V(self.O_BD, 8, 1024), (bBD,))
            for half in range(2):
                mm(2, 512, cgf, VR(self.O_BLK8 + half * 512, 0, 8, 512), True, True,
                   reads=(bcgf[bt], bc), c0=0)
                self.vo(lambda e, half=half: e.tensor_copy(
                    V(self.O_EP + half * 512, 128, 512), PS[2][:, 0:512]),
                    reads=(psb[2],), writes=(bDP,))
            for half in range(2):
                mm(3, 512, V(self.O_ONES8, 8, 128), VR(self.O_BD + half * 512, 0, 8, 512),
                   True, True, reads=(bBD, bc), c0=0)
                self.vo(lambda e, half=half: e.tensor_copy(
                    V(self.O_E + half * 512, 128, 512), PS[3][:, 0:512]),
                    reads=(psb[3],), writes=(bD,))
            if probe: probe(f"CGS{bt}", V(self.O_EP, 128, 1024), (bDP,))
            if probe: probe(f"BCG{bt}", V(self.O_E, 128, 1024), (bD,))
            bfm = V(self.O_BFM[bt], 8, 128)
            bfm_b = bfm.rearrange("p (o f) -> p o f", o=1).broadcast_to([8, 8, 128])
            self.vo(lambda e, bdv=bdv, bfm_b=bfm_b, blkv=blkv: e.tensor_tensor(bdv, bfm_b, blkv, ALU.mult),
                    reads=(bbfm[bt], bc), writes=(bBD,))
            for half in range(2):
                mm(2, 512, V(self.O_ONES8, 8, 128), VR(self.O_BD + half * 512, 0, 8, 512),
                   True, True, reads=(bBD, bc), c0=0)
                self.vo(lambda e, half=half: e.tensor_copy(
                    V(self.O_BB + half * 512, 128, 512), PS[2][:, 0:512]),
                    reads=(psb[2],), writes=(bBB,))

            if probe: probe(f"CGF{bt}", V(self.O_CGF[bt], 8, 128), (bcgf[bt],))
            ev_ = V(self.O_E, 128, 1024)
            epv = V(self.O_EP, 128, 1024)
            self.vo(lambda e: e.tensor_tensor(ev_, ev_, epv, ALU.subtract),
                    reads=(bDP,), writes=(bD,))
            self.vo(lambda e: e.tensor_scalar(epv, ev_, -1.0, 0.0, ALU.mult, ALU.min),
                    reads=(bD,), writes=(bDP,))
            S.op("scalar", lambda e: e.activation(epv, epv, AF.Exp), reads=(), writes=(bDP,))
            self.vo(lambda e: e.tensor_scalar(ev_, ev_, 1.0, 0.0, ALU.mult, ALU.min),
                    reads=(), writes=(bD,))
            S.op("scalar", lambda e: e.activation(ev_, ev_, AF.Exp), reads=(), writes=(bD,))
            if probe: probe(f"E{bt}", ev_, (bD,))
            if probe: probe(f"EP{bt}", epv, (bDP,))

            e127v = V(self.O_E127, 128, 8)
            esel = V(self.O_E, 128, 1024).rearrange("p (h t) -> p h t", h=8)[:, :, 127:128]
            self.vo(lambda e, e127v=e127v, esel=esel: e.tensor_copy(e127v.rearrange("p (h o) -> p h o", o=1), esel),
                    reads=(bD,), writes=(bE127,))
            self.vo(lambda e, bt=bt: e.tensor_tensor(
                V(self.O_LAMC, 128, 8), e127v, VR(self.O_LAM0 + 8 * bt, 0, 128, 8), ALU.mult),
                reads=(bE127, blam), writes=(bLC,))

            kkpair = V(self.O_KK, 128, 512).rearrange("p (h o t) -> p h o t", h=4, o=1) \
                .broadcast_to([128, 4, 2, 128])
            msuv = V(self.O_MSU, 128, 128).rearrange("p (o t) -> p o t", o=1).broadcast_to([128, 8, 128])
            mslv = V(self.O_MSL, 128, 128).rearrange("p (o t) -> p o t", o=1).broadcast_to([128, 8, 128])
            miuv = V(self.O_MIU, 128, 128).rearrange("p (o t) -> p o t", o=1).broadcast_to([128, 8, 128])
            ev4 = V(self.O_E, 128, 1024).rearrange("p (h r t) -> p h r t", h=4, r=2)
            epv4 = V(self.O_EP, 128, 1024).rearrange("p (h r t) -> p h r t", h=4, r=2)
            ev8 = V(self.O_E, 128, 1024).rearrange("p (h t) -> p h t", h=8)
            epv8 = V(self.O_EP, 128, 1024).rearrange("p (h t) -> p h t", h=8)
            gv8 = V(self.O_G, 128, 1024).rearrange("p (h t) -> p h t", h=8)
            gv4 = V(self.O_G, 128, 1024).rearrange("p (h r t) -> p h r t", h=4, r=2)

            if with_out:
                qkpair = V(self.O_QK, 128, 512).rearrange("p (h o t) -> p h o t", h=4, o=1) \
                    .broadcast_to([128, 4, 2, 128])
                self.vo(lambda e: e.tensor_tensor(gv8, ev8, miuv, ALU.mult),
                        reads=(bD, bc), writes=(bG,))
                self.vo(lambda e: e.tensor_tensor(gv4, gv4, qkpair, ALU.mult),
                        reads=(bQK,), writes=(bG,))
            self.vo(lambda e: e.tensor_tensor(ev8, ev8, msuv, ALU.mult),
                    reads=(bc,), writes=(bD,))
            self.vo(lambda e: e.tensor_tensor(ev4, ev4, kkpair, ALU.mult),
                    reads=(bKK,), writes=(bD,))
            self.vo(lambda e: e.tensor_tensor(V(self.O_E, 128, 1024), V(self.O_E, 128, 1024),
                                              V(self.O_BB, 128, 1024), ALU.mult),
                    reads=(bBB,), writes=(bD,))
            if probe: probe(f"M{bt}", V(self.O_E, 128, 1024), (bD,))
            self.vo(lambda e: e.tensor_tensor(epv8, epv8, mslv, ALU.mult),
                    reads=(bc,), writes=(bDP,))
            self.vo(lambda e: e.tensor_tensor(epv4, epv4, kkpair, ALU.mult),
                    reads=(bKK,), writes=(bDP,))
            for i in range(8):
                h = 8 * bt + i
                asl = V(self.O_EP + i * 128, 128, 128)
                bcol = VR(self.O_GBTM + 16 + h, 0, 128, 1)
                eng = "vector"
                self.vo(lambda e, asl=asl, bcol=bcol:
                    e.tensor_scalar(asl, asl, bcol, -1.0, ALU.mult, ALU.mult),
                    reads=(bgbt,), writes=(bDP,), eng=eng)
            if probe: probe(f"A{bt}", V(self.O_EP, 128, 1024), (bDP,))

            for g in range(2):
                heads = [8 * bt + 4 * g + i for i in range(4)]
                bVT = B("vtm")
                for i, h in enumerate(heads):
                    bank = 4 if i % 2 == 0 else 5
                    self.trans(bank, 128, 128, v_slices(h), V(self.O_ID128, 128, 128),
                               reads=(b_p,))
                    self.cp(V(self.O_VTM + i * 128, 128, 128), PS[bank][:, 0:128],
                            reads=(psb[bank],), writes=(bVT,))
                bX = B("xsolve")
                for i, h in enumerate(heads):
                    kh = h // 2
                    ccol = VR(self.O_CCOL + h, 0, 128, 1)
                    btcol = VR(self.O_GBTM + 16 + h, 0, 128, 1)
                    self.vo(lambda e, i=i, kh=kh, ccol=ccol: e.tensor_scalar(
                        V(self.O_X + i * 256, 128, 128), V(self.O_KTM + kh * 128, 128, 128),
                        ccol, None, ALU.mult),
                        reads=(bktm, bccol), writes=(bX,), eng=self.ev2())
                    self.vo(lambda e, i=i, btcol=btcol: e.tensor_scalar(
                        V(self.O_X + i * 256 + 128, 128, 128), V(self.O_VTM + i * 128, 128, 128),
                        btcol, None, ALU.mult),
                        reads=(bVT, bgbt), writes=(bX,), eng=self.ev2())
                if probe: probe(f"R{bt}{g}", V(self.O_X, 128, 1024), (bX,))
                bPA = B("pa")
                bPB = B("pb")
                for j in range(7):
                    if j == 0:
                        paof, pa_b = self.O_E, bD
                        pbof, pb_b = self.O_EP, bDP
                        sl = lambda of, i: V(of + (4 * g + i) * 128, 128, 128)
                    else:
                        paof, pa_b = self.O_PA, bPA
                        pbof, pb_b = self.O_PB, bPB
                        sl = lambda of, i: V(of + i * 128, 128, 128)
                    for i in range(4):
                        bank = 2 if i < 2 else 3
                        mm(bank, 256, sl(paof, i), V(self.O_X + i * 256, 128, 256),
                           True, True, reads=(pa_b, bX), c0=(i % 2) * 256)
                    for half in range(2):
                        bank = 2 + half
                        xv = V(self.O_X + half * 512, 128, 512)
                        self.vo(lambda e, xv=xv, bank=bank:
                                e.tensor_tensor(xv, xv, PS[bank][:, 0:512], ALU.add),
                                reads=(psb[bank],), writes=(bX,))
                    if j < 6:
                        for i in range(4):
                            mm(0, 128, sl(pbof, i), sl(paof, i), True, True,
                               reads=(pa_b, pb_b), c0=i * 128)
                        for i in range(4):
                            mm(1, 128, sl(paof, i), sl(pbof, i), True, True,
                               reads=(pa_b, pb_b), c0=i * 128)
                        self.cp(V(self.O_PA, 128, 512), PS[0][:, 0:512],
                                reads=(psb[0],), writes=(bPA,))
                        self.cp(V(self.O_PB, 128, 512), PS[1][:, 0:512],
                                reads=(psb[1],), writes=(bPB,))
                if probe: probe(f"X{bt}{g}", V(self.O_X, 128, 1024), (bX,))
                bWF = B("wfm")
                bUB = B("ubar")
                for i, h in enumerate(heads):
                    bank = 4 if i % 2 == 0 else 5
                    self.trans(bank, 128, 128, V(self.O_X + i * 256, 128, 128),
                               V(self.O_ID128, 128, 128), reads=(bX,))
                    self.cp(V(self.O_WFM + i * 128, 128, 128), PS[bank][:, 0:128],
                            reads=(psb[bank],), writes=(bWF,))
                for i, h in enumerate(heads):
                    mm(4, 128, V(self.O_WFM + i * 128, 128, 128),
                       V(self.O_S + h * DK, 128, 128), True, True,
                       reads=(bWF, self.b_S[h]), c0=(i % 4) * 128)
                for i, h in enumerate(heads):
                    self.vo(lambda e, i=i: e.tensor_tensor(
                        V(self.O_UB + i * 128, 128, 128), V(self.O_X + i * 256 + 128, 128, 128),
                        PS[4][:, i * 128:(i + 1) * 128], ALU.subtract),
                        reads=(psb[4], bX), writes=(bUB,))
                if probe: probe(f"UB{bt}{g}", V(self.O_UB, 128, 512), (bUB,))
                if with_out:
                    for i, h in enumerate(heads):
                        kh = h // 2
                        _, q_ap = kq_slices(kh)
                        mm(5, 128, q_ap, V(self.O_S + h * DK, 128, 128), True, True,
                           reads=(b_p, self.b_S[h]), c0=0)
                        gsl_h = V(self.O_G + (4 * g + i) * 128, 128, 128)
                        mm(6, 128, gsl_h, V(self.O_UB + i * 128, 128, 128), True, True,
                           reads=(bG, bUB), c0=0)
                        lam0col = VR(self.O_LAM0 + h, 0, 128, 1)
                        bCORE = B("core")
                        cv = V(self.O_CORE, 128, 128)
                        self.vo(lambda e, cv=cv, lam0col=lam0col: e.tensor_scalar(
                            cv, PS[5][:, 0:128], lam0col, None, ALU.mult),
                            reads=(psb[5], blam), writes=(bCORE,))
                        self.vo(lambda e, cv=cv: e.tensor_tensor(cv, cv, PS[6][:, 0:128], ALU.add),
                                reads=(psb[6],), writes=(bCORE,))
                        core_sink(h, cv, (bCORE,))
                for i, h in enumerate(heads):
                    kh = h // 2
                    e127c = VR(self.O_E127 + (h - 8 * bt), 0, 128, 1)
                    self.vo(lambda e, i=i, kh=kh, e127c=e127c: e.tensor_scalar(
                        V(self.O_X + i * 256, 128, 128), V(self.O_KTM + kh * 128, 128, 128),
                        e127c, None, ALU.mult),
                        reads=(bktm, bE127), writes=(bX,), eng=self.ev2())
                for i, h in enumerate(heads):
                    mm(7, 128, V(self.O_X + i * 256, 128, 128), V(self.O_UB + i * 128, 128, 128),
                       True, True, reads=(bX, bUB), c0=(i % 4) * 128)
                for i, h in enumerate(heads):
                    lamcc = VR(self.O_LAMC + (h - 8 * bt), 0, 128, 1)
                    sv = V(self.O_S + h * DK, 128, 128)
                    self.vo(lambda e, sv=sv, lamcc=lamcc, i=i: e.scalar_tensor_tensor(
                        sv, sv, lamcc, PS[7][:, i * 128:(i + 1) * 128], ALU.mult, ALU.add),
                        reads=(psb[7], bLC), writes=(self.b_S[h],))


FULL_BUILD_PLACEHOLDER = True




def build_full(debug=False, phases=("prefix", "span", "ffn")):
    b = Bld(53200)
    nc = b.nc
    S, V, VR, PS, psb = b.S, b.V, b.VR, b.PS, b.psb
    B = b.B
    BF16 = mybir.dt.bfloat16

    def bfv(off, cols_f32):
        return b.AR[0:128, off:off + cols_f32].bitcast(BF16)

    def din(name, shape):
        return nc.dram_tensor(name, shape, F32, kind="ExternalInput").ap()

    xpre = din("xpre", [HID, NPT * PCOLS])
    premask = din("premask", [1, NPT * PCOLS])
    xspan = din("xspan", [HID, NQ * QCOLS])
    tailm = din("tailm", [128, 1])
    wqkv = din("wqkv", [CONV_DIM, HID])
    wz = din("wz", [HID, HID])
    wout_w = din("wout_w", [VAL_DIM, HID])
    wgup = din("wgup", [NEXP * 22 * 128, HID])
    wdwn = din("wdwn", [NEXP * 16 * 128, FF])
    wsgu = din("wsgu", [44 * 128, HID])
    wsd = din("wsd", [16 * 128, FS])
    wa_t = din("wa_t", [128, NKC * 16])
    wb_t = din("wb_t", [128, NKC * 16])
    wgate = din("wgate", [128, NKC * NEXP])
    bqkv_t = din("bqkv_t", [128, 32])
    cb_t = din("cb_t", [128, 32])
    cw_t = din("cw_t", [128, 128])
    bz_row = din("bz_row", [1, VAL_DIM])
    ba_c = din("ba_c", [16, 1])
    bb_c = din("bb_c", [16, 1])
    dtb = din("dtb", [16, 1])
    alog = din("alog", [16, 1])
    nw_row = din("nw_row", [1, DV])
    bout_t = din("bout_t", [128, 16])

    out = nc.dram_tensor("out", [HID, SPAN], F32, kind="ExternalOutput").ap()
    wqkv_s = nc.dram_tensor("wqkv_s", [CONV_DIM, 2 * HID], mybir.dt.bfloat16, kind="Internal").ap()
    wz_s = nc.dram_tensor("wz_s", [HID, 2 * HID], mybir.dt.bfloat16, kind="Internal").ap()
    wout_s = nc.dram_tensor("wout_s", [VAL_DIM, 2 * HID], mybir.dt.bfloat16, kind="Internal").ap()
    wgup_s = nc.dram_tensor("wgup_s", [NEXP * 22 * 128, 2 * HID], mybir.dt.bfloat16, kind="Internal").ap()
    wdwn_s = nc.dram_tensor("wdwn_s", [NEXP * 16 * 128, 2 * FF], mybir.dt.bfloat16, kind="Internal").ap()
    wsgu_s = nc.dram_tensor("wsgu_s", [44 * 128, 2 * HID], mybir.dt.bfloat16, kind="Internal").ap()
    wsd_s = nc.dram_tensor("wsd_s", [16 * 128, 2 * FS], mybir.dt.bfloat16, kind="Internal").ap()
    xnh_scr = nc.dram_tensor("xnh_scr", [HID, SPAN], BF16, kind="Internal").ap()
    xnl_scr = nc.dram_tensor("xnl_scr", [HID, SPAN], BF16, kind="Internal").ap()
    ax_scr = nc.dram_tensor("ax_scr", [HID, SPAN], F32, kind="Internal").ap()
    out_re = out.rearrange("(k p) t -> p k t", p=128)
    xnh_re = xnh_scr.rearrange("(k p) t -> p k t", p=128)
    xnl_re = xnl_scr.rearrange("(k p) t -> p k t", p=128)
    ax_re = ax_scr.rearrange("(k p) t -> p k t", p=128)
    xpre_re = xpre.rearrange("(k p) t -> p k t", p=128)
    xspan_re = xspan.rearrange("(k p) t -> p k t", p=128)
    if debug:
        d_spre = nc.dram_tensor("d_spre", [128, HV * DK], F32, kind="ExternalOutput").ap()
        d_ax = nc.dram_tensor("d_ax", [HID, SPAN], F32, kind="ExternalOutput").ap()

    b.alloc_scan_consts()
    O_WAH = b.alloc(128, "wah"); O_WAL = b.alloc(128, "wal")
    O_WBH = b.alloc(128, "wbh"); O_WBL = b.alloc(128, "wbl")
    O_WGH = b.alloc(64, "wgh"); O_WGL = b.alloc(64, "wgl")
    O_WSTAGE0 = b.alloc(256, "wstage_ab")
    O_BQKV = b.alloc(32, "bqkv")
    O_CBt = b.alloc(32, "cbt")
    O_CW = b.alloc(128, "cw")
    O_BOUT = b.alloc(16, "bout")
    O_BA = b.alloc(1, "ba")
    O_BB2 = b.alloc(1, "bb")
    O_DTB = b.alloc(1, "dtb")
    O_NEXPA = b.alloc(1, "nexpa")
    O_TAILM = b.alloc(1, "tailm")
    O_RC = b.alloc(1, "rcol")
    b.alloc_state()
    O_SHARED = b.alloc(3104, "shared")
    b.alloc_scan_ws()
    BIG0 = b._cur
    BIGSZ = 53200 - BIG0
    assert BIGSZ >= 27000, BIGSZ

    bc = B("const")
    b_shared = B("shared")
    b_xnscr = B("xn_scr")
    b_axscr = B("ax_scr")

    b.load_scan_consts(din)
    for dst, src in [
        (V(O_WSTAGE0, 128, 256), wa_t),
        (V(O_BQKV, 128, 32), bqkv_t),
        (V(O_CBt, 128, 32), cb_t), (V(O_CW, 128, 128), cw_t),
        (V(O_BOUT, 128, 16), bout_t), (V(O_BA, 16, 1), ba_c),
        (V(O_BB2, 16, 1), bb_c), (V(O_DTB, 16, 1), dtb),
        (V(O_NEXPA, 16, 1), alog), (V(O_TAILM, 128, 1), tailm),
    ]:
        b.dma(dst, src, writes=(bc,))

    def split_fl(o_src, o_hi, o_lo, cols, br, bw, src_parts=128):
        sv = V(o_src, src_parts, cols)
        hv_ = VR(o_hi, 0, src_parts, cols // 2).bitcast(BF16)
        lv = VR(o_lo, 0, src_parts, cols // 2).bitcast(BF16)
        S.op("scalar", lambda e: e.copy(hv_, sv), reads=(br,), writes=(bw,))
        b.vo(lambda e: e.tensor_tensor(lv, sv, hv_, ALU.subtract), reads=(br,), writes=(bw,))

    split_fl(O_WSTAGE0, O_WAH, O_WAL, 256, bc, bc)
    b.dma(V(O_WSTAGE0, 128, 256), wb_t, writes=(bc,))
    split_fl(O_WSTAGE0, O_WBH, O_WBL, 256, bc, bc)
    b.dma(VR(O_SHARED, 0, 128, 128), wgate, writes=(b_shared,))
    split_fl(O_SHARED, O_WGH, O_WGL, 128, b_shared, bc)

    S.op("scalar", lambda e: e.activation(V(O_NEXPA, 16, 1), V(O_NEXPA, 16, 1), AF.Exp),
         reads=(bc,), writes=(bc,))
    b.vo(lambda e: e.tensor_scalar(V(O_NEXPA, 16, 1), V(O_NEXPA, 16, 1), -1.0, None, ALU.mult),
         reads=(bc,), writes=(bc,))
    b.dma(V(b.O_NWBC, 1, 128), nw_row, writes=(B("nwtmp"),))
    b.mm(0, 128, V(b.O_ONESR, 1, 128), V(b.O_NWBC, 1, 128), True, True,
         reads=(bc, B("nwtmp")))
    b.vo(lambda e: e.tensor_copy(V(b.O_NWBC, 128, 128), PS[0][:, 0:128]),
         reads=(psb[0],), writes=(bc,))


    def emit_wload(o_slot, scr, r0, K, b_slot):
        b.dma(VR(o_slot, 0, 128, K).bitcast(mybir.dt.bfloat16),
              scr[r0:r0 + 128, 0:2 * K],
              reads=(B(("wscr", scr.tensor.name)),), writes=(b_slot,))

    def wall():
        for e in S.ENGINES:
            if S.pending_ph[e]:
                S.op(e, lambda en: en.nop(), reads=(), writes=(B(("wallnop", e)),))
        snap = tuple((s, v) for s, v in S.counts.items() if v > 0)
        wt = S.op("sync", lambda e: e.nop(), extra=snap)
        for bf in b.bufs.values():
            bf.writer = wt
            bf.readers = []
        for pb in psb:
            pb.writer = wt
            pb.readers = []

    def mm3(bank, n, whi, wlo, xhi, xlo, start, stop, reads=(), c0=0, parts=128):
        b.mm(bank, n, whi, xhi, start, False, reads=reads, c0=c0, parts=parts)
        b.mm(bank, n, whi, xlo, False, False, reads=reads, c0=c0, parts=parts)
        b.mm(bank, n, wlo, xhi, False, stop, reads=reads, c0=c0, parts=parts)

    def xsl(o_x, ncols, kc, part, t0, tn):
        base = o_x + kc * ncols + part * (ncols // 2)
        return b.AR[0:128, base:base + ncols // 2].bitcast(BF16)[:, t0:t0 + tn]

    def emit_ln(o_x, ncols, c0, n, o_sq, o_row, o_bc, bxk):
        assert n <= 512 and ncols % 2 == 0
        for kc in range(NKC):
            xa = V(o_x + kc * ncols + c0, 128, n)
            b.mm(6, n, V(b.O_ONESC, 128, 1), xa, kc == 0, kc == NKC - 1,
                 reads=(bc, bxk[kc]), parts=1)
        for kc in range(NKC):
            xa = V(o_x + kc * ncols + c0, 128, n)
            sq = V(o_sq + (kc % 2) * 512, 128, n)
            bsq = B(("lnsq", kc % 2))
            S.op("scalar", lambda e, sq=sq, xa=xa: e.activation(sq, xa, AF.Square),
                 reads=(bxk[kc],), writes=(bsq,))
            b.mm(7, n, V(b.O_ONESC, 128, 1), sq, kc == 0, kc == NKC - 1,
                 reads=(bc, bsq), parts=1)
        r_m = V(o_row, 1, n)
        r_v = V(o_row + 512, 1, n)
        r_t = V(o_row + 1024, 1, n)
        bm, bv, bt_ = B("lnm"), B("lnv"), B("lnt")
        b.vo(lambda e: e.tensor_scalar(r_m, PS[6][0:1, 0:n], 1.0 / HID, None, ALU.mult),
             reads=(psb[6],), writes=(bm,))
        b.vo(lambda e: e.tensor_scalar(r_v, PS[7][0:1, 0:n], 1.0 / HID, LN_EPS, ALU.mult, ALU.add),
             reads=(psb[7],), writes=(bv,))
        b.vo(lambda e: e.tensor_tensor(r_t, r_m, r_m, ALU.mult), reads=(bm,), writes=(bt_,))
        b.vo(lambda e: e.tensor_tensor(r_v, r_v, r_t, ALU.subtract), reads=(bt_,), writes=(bv,))
        b.vo(lambda e: e.reciprocal(r_v, r_v), reads=(), writes=(bv,))
        S.op("scalar", lambda e: e.activation(r_v, r_v, AF.Sqrt), reads=(), writes=(bv,))
        b.mm(6, n, V(b.O_ONESR, 1, 128), r_v, True, True, reads=(bc, bv))
        bc1 = B("lnc1")
        b.vo(lambda e: e.tensor_copy(V(o_bc, 128, n), PS[6][:, 0:n]),
             reads=(psb[6],), writes=(bc1,))
        b.mm(7, n, V(b.O_ONESR, 1, 128), r_m, True, True, reads=(bc, bm))
        bc0 = B("lnc0")
        b.vo(lambda e: e.tensor_copy(V(o_bc + 512, 128, n), PS[7][:, 0:n]),
             reads=(psb[7],), writes=(bc0,))
        for kc in range(NKC):
            xa = V(o_x + kc * ncols + c0, 128, n)
            t2 = V(o_sq + (kc % 2) * 512, 128, n)
            bsq = B(("lnsq", kc % 2))
            b.vo(lambda e, xa=xa: e.tensor_tensor(xa, xa, V(o_bc + 512, 128, n), ALU.subtract),
                 reads=(bc0,), writes=(bxk[kc],))
            b.vo(lambda e, xa=xa, t2=t2: e.tensor_tensor(t2, xa, V(o_bc, 128, n), ALU.mult),
                 reads=(bc1, bxk[kc]), writes=(bsq,))
            hi = bfv(o_x + kc * ncols, ncols // 2)[:, c0:c0 + n]
            lo = bfv(o_x + kc * ncols + ncols // 2, ncols // 2)[:, c0:c0 + n]
            S.op("scalar", lambda e, hi=hi, t2=t2: e.copy(hi, t2), reads=(bsq,), writes=(bxk[kc],))
            b.vo(lambda e, lo=lo, t2=t2, hi=hi: e.tensor_tensor(lo, t2, hi, ALU.subtract),
                 reads=(bsq,), writes=(bxk[kc],))

    def emit_wsplit(o_stage, o_slot, cols, wdram, r0, b_st, b_sl):
        b.dma(V(o_stage, 128, cols), wdram[r0:r0 + 128, 0:cols], writes=(b_st,))
        split_fl(o_stage, o_slot, o_slot + cols // 2, cols, b_st, b_sl)

    def emit_proj(o_dst, ncols_dst, bfts, gfts, o_xn, ncols_xn, c0, n, wsrc_scr,
                  o_wsl, b_wsl, bxk, b_p, o_pm=None, b_pm=None, pm_c0=0):
        assert n <= 512
        for i, (bft, gft) in enumerate(zip(bfts, gfts)):
            sl = i % 2
            emit_wload(o_wsl + sl * 2048, wsrc_scr, gft * 128, 2048, b_wsl[sl])
            bank = 6 if i % 2 == 0 else 7
            for kc in range(NKC):
                whi = bfv(o_wsl + sl * 2048 + kc * 64, 64)
                wlo = bfv(o_wsl + sl * 2048 + 1024 + kc * 64, 64)
                xhi = xsl(o_xn, ncols_xn, kc, 0, c0, n)
                xlo = xsl(o_xn, ncols_xn, kc, 1, c0, n)
                mm3(bank, n, whi, wlo, xhi, xlo, kc == 0, kc == NKC - 1,
                    reads=(b_wsl[sl], bxk[kc]))
            da = V(o_dst + bft * ncols_dst + c0, 128, n)
            bcol = V(O_BQKV + gft, 128, 1)
            if o_pm is not None:
                pmv = V(o_pm + pm_c0, 128, n)
                b.vo(lambda e, da=da, bank=bank, bcol=bcol, pmv=pmv:
                     e.scalar_tensor_tensor(da, PS[bank][:, 0:n], bcol, pmv, ALU.add, ALU.mult),
                     reads=(psb[bank], bc, b_pm), writes=(b_p,))
            else:
                eng = b.S.ev()
                if eng == "scalar":
                    b.vo(lambda e, da=da, bank=bank, bcol=bcol:
                         e.activation(da, PS[bank][:, 0:n], AF.Identity, bias=bcol),
                         reads=(psb[bank], bc), writes=(b_p,), eng="scalar")
                else:
                    b.vo(lambda e, da=da, bank=bank, bcol=bcol:
                         e.tensor_scalar(da, PS[bank][:, 0:n], bcol, None, ALU.add),
                         reads=(psb[bank], bc), writes=(b_p,))

    def emit_conv_silu(o_p, ncols, n, nft, gft0, o_cv, b_p):
        for ft in range(nft):
            gft = gft0 + ft
            acc = V(o_cv, 128, n)
            bacc = B("cvacc")
            for j in range(KCONV):
                src = V(o_p + ft * ncols + 1 + j, 128, n)
                cwc = V(O_CW + gft * 4 + j, 128, 1)
                if j == 0:
                    b.vo(lambda e, acc=acc, src=src, cwc=cwc: e.tensor_scalar(acc, src, cwc, None, ALU.mult),
                         reads=(b_p, bc), writes=(bacc,))
                else:
                    b.vo(lambda e, acc=acc, src=src, cwc=cwc:
                         e.scalar_tensor_tensor(acc, src, cwc, acc, ALU.mult, ALU.add),
                         reads=(b_p, bc), writes=(bacc,))
            dstv = V(o_p + ft * ncols + 4, 128, n)
            cbcol = V(O_CBt + gft, 128, 1)
            S.op("scalar", lambda e, dstv=dstv, acc=acc, cbcol=cbcol:
                 e.activation(dstv, acc, AF.Silu, bias=cbcol),
                 reads=(bacc, bc), writes=(b_p,))

    def emit_l2(o_p, ncols, n, bfts, is_q, o_sq, o_row, b_p):
        assert n <= 512
        for ft in bfts:
            base = V(o_p + ft * ncols + 4, 128, n)
            sq = V(o_sq, 128, n)
            bsq = B("l2sq")
            S.op("scalar", lambda e, sq=sq, base=base: e.activation(sq, base, AF.Square),
                 reads=(b_p,), writes=(bsq,))
            b.mm(6, n, V(b.O_ONESC, 128, 1), sq, True, True, reads=(bc, bsq), parts=1)
            rrow = V(o_row, 1, n)
            brow = B("l2row")
            b.vo(lambda e, rrow=rrow: e.tensor_scalar(rrow, PS[6][0:1, 0:n], L2_EPS, None, ALU.add),
                 reads=(psb[6],), writes=(brow,))
            b.vo(lambda e, rrow=rrow: e.reciprocal(rrow, rrow), reads=(), writes=(brow,))
            sc = (1.0 / DK) if is_q else 1.0
            S.op("scalar", lambda e, rrow=rrow, sc=sc: e.activation(rrow, rrow, AF.Sqrt, scale=sc),
                 reads=(), writes=(brow,))
            b.mm(7, n, V(b.O_ONESR, 1, 128), rrow, True, True, reads=(bc, brow))
            b.vo(lambda e, base=base: e.tensor_tensor(base, base, PS[7][:, 0:n], ALU.mult),
                 reads=(psb[7],), writes=(b_p,))

    def emit_gb(o_g, o_bt, o_gt, o_xn, ncols_xn, c0, n, bxk, b_gb, o_pm=None,
                b_pm=None, pm_c0=0):
        assert n <= 512
        for z_i, (o_dst, o_wh, o_wl, o_bias) in enumerate(
                [(o_g, O_WAH, O_WAL, O_BA), (o_bt, O_WBH, O_WBL, O_BB2)]):
            bank = 6 + z_i
            for kc in range(NKC):
                whi = bfv(o_wh + kc * 8, 8)
                wlo = bfv(o_wl + kc * 8, 8)
                xhi = xsl(o_xn, ncols_xn, kc, 0, c0, n)
                xlo = xsl(o_xn, ncols_xn, kc, 1, c0, n)
                mm3(bank, n, whi, wlo, xhi, xlo, kc == 0, kc == NKC - 1,
                    reads=(bc, bxk[kc]), parts=16)
            da = VR(o_dst, 0, 16, n)
            b.vo(lambda e, da=da, bank=bank, o_bias=o_bias:
                 e.tensor_scalar(da, PS[bank][0:16, 0:n], V(o_bias, 16, 1), None, ALU.add),
                 reads=(psb[bank], bc), writes=(b_gb,))
        ga = VR(o_g, 0, 16, n)
        bet = VR(o_bt, 0, 16, n)
        t1 = VR(o_gt, 0, 16, n)
        b1 = B("gbt1")
        S.op("scalar", lambda e: e.activation(bet, bet, AF.Sigmoid), reads=(), writes=(b_gb,))
        S.op("scalar", lambda e: e.activation(t1, ga, AF.Abs, bias=V(O_DTB, 16, 1)),
             reads=(b_gb, bc), writes=(b1,))
        S.op("scalar", lambda e: e.activation(t1, t1, AF.Exp, scale=-1.0), reads=(), writes=(b1,))
        S.op("scalar", lambda e: e.activation(t1, t1, AF.Ln, bias=1.0), reads=(), writes=(b1,))
        S.op("scalar", lambda e: e.activation(ga, ga, AF.Relu, bias=V(O_DTB, 16, 1)),
             reads=(bc,), writes=(b_gb,))
        b.vo(lambda e: e.tensor_tensor(ga, ga, t1, ALU.add), reads=(b1,), writes=(b_gb,))
        b.vo(lambda e: e.tensor_scalar(ga, ga, V(O_NEXPA, 16, 1), None, ALU.mult),
             reads=(bc,), writes=(b_gb,))
        if o_pm is not None:
            pmg = VR(o_pm + pm_c0, 0, 16, n)
            b.vo(lambda e: e.tensor_tensor(ga, ga, pmg, ALU.mult), reads=(b_pm,), writes=(b_gb,))
            b.vo(lambda e: e.tensor_tensor(bet, bet, pmg, ALU.mult), reads=(b_pm,), writes=(b_gb,))

    b._cur = BIG0
    o_xq = b.alloc(16 * PCOLS, "p_xq")
    o_sq = b.alloc(1024, "p_sq")
    o_row = b.alloc(1536, "p_row")
    o_bcb = b.alloc(1024, "p_bc")
    o_pkv = b.alloc(24 * PCOLS, "p_pkv")
    o_g = b.alloc(PTILE, "p_g")
    o_bt = b.alloc(PTILE, "p_bt")
    o_gt = b.alloc(PTILE, "p_gt")
    o_cv = b.alloc(584, "p_cv")
    o_pm = b.alloc(512, "p_pm")
    o_wsl = b.alloc(4096, "p_wsl")
    b_wsl = [B("wsl0"), B("wsl1")]
    b.dma(VR(O_SHARED, 0, 1, NPT * PCOLS), premask, writes=(b_shared,))

    bxk_p = [B(("px", kc)) for kc in range(NKC)]
    b.dma(V(o_xq, 128, 16 * PCOLS).rearrange("p (k t) -> p k t", k=16),
          xpre_re[:, :, 0:PCOLS], writes=tuple(bxk_p))
    emit_ln(o_xq, PCOLS, 4, 512, o_sq, o_row, o_bcb, bxk_p)

    o_pstage = o_pkv
    o_pslot = o_pkv + 8192
    b_pst = [B(f"pst{i}") for i in range(4)]
    b_psl = [B(f"psl{i}") for i in range(4)]
    psl_i = 0
    for scr, wdram, nft, K in [
        (wqkv_s, wqkv, CONV_DIM // 128, HID),
        (wz_s, wz, NKC, HID),
        (wout_s, wout_w, VAL_DIM // 128, HID),
        (wgup_s, wgup, NEXP * 22, HID),
        (wdwn_s, wdwn, NEXP * 16, FF),
        (wsgu_s, wsgu, 44, HID),
        (wsd_s, wsd, 16, FS),
    ]:
        npc = (K + 2047) // 2048
        w = K // npc
        scr_re = scr.rearrange("r (two k) -> r two k", two=2)
        for ft in range(nft):
            for pc in range(npc):
                c0 = pc * w
                sl = psl_i % 4
                psl_i += 1
                b.dma(V(o_pstage + sl * 2048, 128, w),
                      wdram[ft * 128:(ft + 1) * 128, c0:c0 + w], writes=(b_pst[sl],))
                split_fl(o_pstage + sl * 2048, o_pslot + sl * 2048,
                         o_pslot + sl * 2048 + w // 2, w, b_pst[sl], b_psl[sl])
                b.dma(scr_re[ft * 128:(ft + 1) * 128, :, c0:c0 + w],
                      VR(o_pslot + sl * 2048, 0, 128, w).bitcast(mybir.dt.bfloat16)
                      .rearrange("p (two k) -> p two k", two=2),
                      reads=(b_psl[sl],), writes=(B(("wscr", scr.tensor.name)),),
                      eng="gpsimd")
    wall()

    for pt in range(NPT):
        b_p = B("pp")
        b_gb = B("pgb")
        b_pm = B("ppm")
        if pt > 0:
            b.dma(V(o_xq, 128, 16 * PCOLS).rearrange("p (k t) -> p k t", k=16),
                  xpre_re[:, :, pt * PCOLS:(pt + 1) * PCOLS], writes=tuple(bxk_p))
        b.mm(6, 512, V(b.O_ONESR, 1, 128), VR(O_SHARED + pt * PCOLS + 4, 0, 1, 512),
             True, True, reads=(bc, b_shared))
        b.vo(lambda e: e.tensor_copy(V(o_pm, 128, 512), PS[6][:, 0:512]),
             reads=(psb[6],), writes=(b_pm,))
        if pt > 0:
            emit_ln(o_xq, PCOLS, 4, 512, o_sq, o_row, o_bcb, bxk_p)
        emit_proj(o_pkv, PCOLS, list(range(24)), list(range(8, 32)),
                  o_xq, PCOLS, 4, 512, wqkv_s, o_wsl, b_wsl, bxk_p, b_p,
                  o_pm=o_pm, b_pm=b_pm, pm_c0=0)
        hsl = V(o_pkv, 128, 24 * PCOLS).rearrange("p (f t) -> p f t", f=24)
        stash = V(o_cv + 512, 128, 24 * 3).rearrange("p (f t) -> p f t", f=24)
        if pt == 0:
            b.vo(lambda e: e.memset(hsl[:, :, 1:4], 0.0), reads=(), writes=(b_p,))
        else:
            b.vo(lambda e: e.tensor_copy(hsl[:, :, 1:4], stash),
                 reads=(B("cvhalo"),), writes=(b_p,))
        b.vo(lambda e: e.tensor_copy(stash, hsl[:, :, PCOLS - 3:PCOLS]),
             reads=(b_p,), writes=(B("cvhalo"),))
        emit_conv_silu(o_pkv, PCOLS, PTILE, 24, 8, o_cv, b_p)
        emit_l2(o_pkv, PCOLS, PTILE, list(range(8)), False, o_sq, o_row, b_p)
        emit_gb(o_g, o_bt, o_gt, o_xq, PCOLS, 4, PTILE, bxk_p, b_gb, o_pm=o_pm,
                b_pm=b_pm, pm_c0=0)

        def kqp(kh):
            return (V(o_pkv + kh * PCOLS + 4 + kqp.c * CH, 128, 128), None)

        def vsp(h):
            return V(o_pkv + (8 + h) * PCOLS + 4 + vsp.c * CH, 128, 128)

        for ci in range(PTILE // CH):
            kqp.c = ci
            vsp.c = ci
            b.emit_chunk(kqp, vsp, o_g, o_bt, ci * CH, b_p, b_gb, False)

    if debug:
        for h in range(HV):
            b.dma(d_spre[:, h * DK:(h + 1) * DK], V(b.O_S + h * DK, 128, DK),
                  reads=(b.b_S[h],))

    if "span" not in phases:
        fence = [(s, v) for s, v in S.counts.items() if "_d" in s and v > 0]
        S.op("sync", lambda e: e.nop(), extra=tuple(fence))
        S.emit(nc)
        return nc
    wall()
    b._cur = BIG0
    o_xq2 = b.alloc(16 * QCOLS, "s_xq")
    o_p2 = b.alloc(32 * QCOLS, "s_p")
    o_z = b.alloc(2 * VAL_DIM, "s_z")
    o_g2 = b.alloc(QTOK, "s_g")
    o_bt2 = b.alloc(QTOK, "s_bt")
    o_gt2 = b.alloc(QTOK, "s_gt")
    o_sq2 = b.alloc(1024, "s_sq")
    o_row2 = b.alloc(1536, "s_row")
    o_bc2 = b.alloc(1024, "s_bc")
    o_cv2 = b.alloc(584, "s_cv")
    o_gh = b.alloc(16 * 128, "s_gatedh")
    o_gl = b.alloc(16 * 128, "s_gatedl")
    o_wsl2 = b.alloc(4096, "s_wsl")
    b_wsl = [B("wsl0"), B("wsl1")]
    o_ax = o_xq2
    o_xres = o_z

    b.dma(VR(O_SHARED, 0, 1, VAL_DIM), bz_row, writes=(b_shared,))
    for fb in range(4):
        b.mm(6, 512, V(b.O_ONESR, 1, 128), VR(O_SHARED + fb * 512, 0, 1, 512),
             True, True, reads=(bc, b_shared))
        b.vo(lambda e, fb=fb: e.tensor_copy(V(O_SHARED + 1024 + fb * 512, 128, 512), PS[6][:, 0:512]),
             reads=(psb[6],), writes=(b_shared,))
    O_BZBC = O_SHARED + 1024

    bxk_s = [B(("sx", kc)) for kc in range(NKC)]
    for q in range(NQ):
        b_p = B("sp")
        b_gb = B("sgb")
        b_z = B("sz")
        b_gt = B("sgt")
        b_xr = B("sz")
        b.dma(V(o_xq2, 128, 16 * QCOLS).rearrange("p (k t) -> p k t", k=16),
              xspan_re[:, :, q * QCOLS:(q + 1) * QCOLS], writes=tuple(bxk_s))
        emit_ln(o_xq2, QCOLS, 0, QCOLS, o_sq2, o_row2, o_bc2, bxk_s)
        b.dma(xnh_re[:, :, q * QTOK:(q + 1) * QTOK],
              V(o_xq2, 128, 16 * QCOLS).rearrange("p (k t) -> p k t", k=16)
              [:, :, 0:QCOLS // 2].bitcast(BF16)[:, :, 4:QCOLS],
              reads=tuple(bxk_s), writes=(b_xnscr,))
        b.dma(xnl_re[:, :, q * QTOK:(q + 1) * QTOK],
              V(o_xq2, 128, 16 * QCOLS).rearrange("p (k t) -> p k t", k=16)
              [:, :, QCOLS // 2:QCOLS].bitcast(BF16)[:, :, 4:QCOLS],
              reads=tuple(bxk_s), writes=(b_xnscr,))
        emit_proj(o_p2, QCOLS, list(range(32)), list(range(32)),
                  o_xq2, QCOLS, 1, 259, wqkv_s, o_wsl2, b_wsl, bxk_s, b_p)
        if q == 0:
            b.vo(lambda e: e.tensor_scalar(
                V(o_p2, 128, 32 * QCOLS).rearrange("p (f t) -> p f t", f=32)[:, :, 1:4],
                V(o_p2, 128, 32 * QCOLS).rearrange("p (f t) -> p f t", f=32)[:, :, 1:4],
                V(O_TAILM, 128, 1), None, ALU.mult),
                reads=(bc,), writes=(b_p,))
        emit_conv_silu(o_p2, QCOLS, QTOK, 32, 0, o_cv2, b_p)
        emit_l2(o_p2, QCOLS, QTOK, list(range(8)), True, o_sq2, o_row2, b_p)
        emit_l2(o_p2, QCOLS, QTOK, list(range(8, 16)), False, o_sq2, o_row2, b_p)
        emit_gb(o_g2, o_bt2, o_gt2, o_xq2, QCOLS, 4, QTOK, bxk_s, b_gb)
        for kc in range(NKC):
            sl = kc % 2
            emit_wload(o_wsl2 + sl * 2048, wz_s, kc * 128, 2048, b_wsl[sl])
            for tc in range(2):
                lh = xsl(o_xq2, QCOLS, kc, 0, 4 + tc * 128, 128)
                ll = xsl(o_xq2, QCOLS, kc, 1, 4 + tc * 128, 128)
                for fb in range(4):
                    whi = bfv(o_wsl2 + sl * 2048 + fb * 256, 256)
                    wlo = bfv(o_wsl2 + sl * 2048 + 1024 + fb * 256, 256)
                    mm3(tc * 4 + fb, 512, lh, ll, whi, wlo, kc == 0, kc == NKC - 1,
                        reads=(b_wsl[sl], bxk_s[kc]))
        for tc in range(2):
            for fb in range(4):
                bank = tc * 4 + fb
                da = V(o_z + tc * VAL_DIM + fb * 512, 128, 512)
                b.vo(lambda e, da=da, bank=bank, fb=fb: e.tensor_tensor(
                    da, PS[bank][:, 0:512], V(O_BZBC + fb * 512, 128, 512), ALU.add),
                    reads=(psb[bank], b_shared), writes=(b_z,))

        def kqs2(kh):
            return (V(o_p2 + (8 + kh) * QCOLS + 4 + kqs2.c * CH, 128, 128),
                    V(o_p2 + kh * QCOLS + 4 + kqs2.c * CH, 128, 128))

        def vss(h):
            return V(o_p2 + (16 + h) * QCOLS + 4 + vss.c * CH, 128, 128)

        for ci in range(QTOK // CH):
            kqs2.c = ci
            vss.c = ci

            def sink(h, cv, reads, ci=ci, b_z=b_z, b_gt=b_gt):
                rc = V(O_RC, 128, 1)
                brc = B("rcol")
                csq = V(b.O_ZS, 128, 128)
                bzs = B("zsil")
                S.op("scalar", lambda e, cv=cv: e.activation(csq, cv, AF.Square),
                     reads=tuple(reads), writes=(bzs,))
                b.vo(lambda e: e.tensor_reduce(rc, csq, AX.X, ALU.add),
                     reads=(bzs,), writes=(brc,))
                b.vo(lambda e: e.tensor_scalar(rc, rc, 1.0 / DV, RMS_EPS, ALU.mult, ALU.add),
                     reads=(), writes=(brc,))
                b.vo(lambda e: e.reciprocal(rc, rc), reads=(), writes=(brc,))
                S.op("scalar", lambda e: e.activation(rc, rc, AF.Sqrt), reads=(), writes=(brc,))
                b.vo(lambda e, cv=cv: e.tensor_scalar(cv, cv, rc, None, ALU.mult),
                     reads=(brc,), writes=tuple(reads))
                b.vo(lambda e, cv=cv: e.tensor_tensor(cv, cv, V(b.O_NWBC, 128, 128), ALU.mult),
                     reads=(bc,), writes=tuple(reads))
                zs = V(b.O_ZS, 128, 128)
                zsl = V(o_z + ci * VAL_DIM + h * 128, 128, 128)
                S.op("scalar", lambda e, zsl=zsl: e.activation(zs, zsl, AF.Silu),
                     reads=(b_z,), writes=(bzs,))
                b.vo(lambda e, cv=cv: e.tensor_tensor(cv, cv, zs, ALU.mult),
                     reads=(bzs,), writes=tuple(reads))
                b.trans(5, 128, 128, cv, V(b.O_ID128, 128, 128), reads=tuple(reads))
                hi = bfv(o_gh + h * 128, 128)[:, ci * 128:ci * 128 + 128]
                lo = bfv(o_gl + h * 128, 128)[:, ci * 128:ci * 128 + 128]
                S.op("scalar", lambda e, hi=hi: e.copy(hi, PS[5][:, 0:128]),
                     reads=(psb[5],), writes=(b_gt,))
                b.vo(lambda e, lo=lo, hi=hi: e.tensor_tensor(lo, PS[5][:, 0:128], hi, ALU.subtract),
                     reads=(psb[5],), writes=(b_gt,))

            b.emit_chunk(kqs2, vss, o_g2, o_bt2, ci * CH, b_p, b_gb, True, sink)

        b.dma(V(o_xres, 128, 16 * QTOK).rearrange("p (k t) -> p k t", k=16),
              xspan_re[:, :, q * QCOLS + 4:(q + 1) * QCOLS], writes=(b_xr,))
        for fo in range(16):
            sl = fo % 2
            emit_wload(o_wsl2 + sl * 2048, wout_s, fo * 128, 2048, b_wsl[sl])
            bank = 6 if fo % 2 == 0 else 7
            for kc in range(NKC):
                whi = bfv(o_wsl2 + sl * 2048 + kc * 64, 64)
                wlo = bfv(o_wsl2 + sl * 2048 + 1024 + kc * 64, 64)
                ghi = bfv(o_gh + kc * 128, 128)
                glo = bfv(o_gl + kc * 128, 128)
                mm3(bank, QTOK, whi, wlo, ghi, glo, kc == 0, kc == NKC - 1,
                    reads=(b_wsl[sl], b_gt))
            da = V(o_ax + fo * QCOLS, 128, QTOK)
            b.vo(lambda e, da=da, bank=bank, fo=fo: e.scalar_tensor_tensor(
                da, PS[bank][:, 0:QTOK], V(O_BOUT + fo, 128, 1),
                V(o_xres + fo * QTOK, 128, QTOK), ALU.add, ALU.add),
                reads=(psb[bank], bc, b_xr), writes=(bxk_s[fo],))
        b.dma(ax_re[:, :, q * QTOK:(q + 1) * QTOK],
              V(o_ax, 128, 16 * QCOLS).rearrange("p (k t) -> p k t", k=16)[:, :, 0:QTOK],
              reads=tuple(bxk_s), writes=(b_axscr,))

    if debug:
        b.dma(d_ax, ax_scr, reads=(b_axscr,))

    if "ffn" not in phases:
        fence = [(s, v) for s, v in S.counts.items() if "_d" in s and v > 0]
        S.op("sync", lambda e: e.nop(), extra=tuple(fence))
        S.emit(nc)
        return nc
    wall()
    b._cur = BIG0
    o_fxn = b.alloc(16 * 512, "f_xn")
    o_acc = b.alloc(16 * 512, "f_acc")
    o_act = b.alloc(11 * 512, "f_act")
    o_fsc = b.alloc(512, "f_scr")
    o_cmb = b.alloc(512, "f_cmb")
    o_cb = b.alloc(512, "f_cb")
    o_gs = b.alloc(8, "f_gs")
    o_gs2 = b.alloc(8, "f_gs2")
    o_gcl = b.alloc(4, "f_gcol")
    o_bd8 = b.alloc(512, "f_bd8")
    o_wsl3 = b.alloc(8192, "f_wsl")
    b_wsl = [B("wsl0"), B("wsl1"), B("wsl2"), B("wsl3")]

    def fxn_sl(kc, part, t0, tn):
        base = o_fxn + kc * 512 + part * 256
        return b.AR[0:128, base:base + 256].bitcast(BF16)[:, t0:t0 + tn]

    for half in range(2):
        b_x = B("fx")
        b_a = B("fa")
        b_cmb = B("fcmb")
        b_st = B("wstage")
        b.dma(V(o_fxn, 128, 16 * 512).rearrange("p (k t) -> p k t", k=16)
              [:, :, 0:256].bitcast(BF16),
              xnh_re[:, :, half * 512:(half + 1) * 512], reads=(b_xnscr,), writes=(b_x,))
        b.dma(V(o_fxn, 128, 16 * 512).rearrange("p (k t) -> p k t", k=16)
              [:, :, 256:512].bitcast(BF16),
              xnl_re[:, :, half * 512:(half + 1) * 512], reads=(b_xnscr,), writes=(b_x,))
        b.dma(V(o_acc, 128, 16 * 512).rearrange("p (k t) -> p k t", k=16),
              ax_re[:, :, half * 512:(half + 1) * 512], reads=(b_axscr,), writes=(b_a,))
        for tc in range(4):
            for kc in range(NKC):
                lh = fxn_sl(kc, 0, tc * 128, 128)
                ll = fxn_sl(kc, 1, tc * 128, 128)
                wgh = bfv(O_WGH + kc * 4, 4)
                wgl = bfv(O_WGL + kc * 4, 4)
                mm3(6, 8, lh, ll, wgh, wgl, kc == 0, kc == NKC - 1, reads=(bc, b_x))
            sc_ = V(o_gs, 128, 8)
            mc = V(o_gcl, 128, 1)
            m2c = V(o_gcl + 1, 128, 1)
            rs_ = V(o_gcl + 2, 128, 1)
            bsc, bmc = B("fsc"), B("fmc")
            b.vo(lambda e: e.tensor_reduce(mc, PS[6][:, 0:8], AX.X, ALU.max),
                 reads=(psb[6],), writes=(bmc,))
            b.vo(lambda e: e.tensor_scalar(mc, mc, -1.0, None, ALU.mult), reads=(), writes=(bmc,))
            S.op("scalar", lambda e: e.activation(sc_, PS[6][:, 0:8], AF.Exp, bias=mc),
                 reads=(psb[6], bmc), writes=(bsc,))
            b.vo(lambda e: e.tensor_reduce(rs_, sc_, AX.X, ALU.add), reads=(bsc,), writes=(bmc,))
            b.vo(lambda e: e.reciprocal(rs_, rs_), reads=(), writes=(bmc,))
            b.vo(lambda e: e.tensor_scalar(sc_, sc_, rs_, None, ALU.mult),
                 reads=(bmc,), writes=(bsc,))
            b.vo(lambda e: e.tensor_reduce(mc, sc_, AX.X, ALU.max), reads=(bsc,), writes=(bmc,))
            ge1 = V(o_gs2, 128, 8)
            bge = B("fge")
            b.vo(lambda e: e.tensor_scalar(ge1, sc_, mc, None, ALU.is_ge),
                 reads=(bsc, bmc), writes=(bge,))
            b.vo(lambda e: e.scalar_tensor_tensor(ge1, ge1, -1e9, sc_, ALU.mult, ALU.add),
                 reads=(bsc,), writes=(bge,))
            b.vo(lambda e: e.tensor_reduce(m2c, ge1, AX.X, ALU.max), reads=(bge,), writes=(bmc,))
            b.vo(lambda e: e.tensor_scalar(ge1, sc_, m2c, None, ALU.is_ge),
                 reads=(bsc, bmc), writes=(bge,))
            b.vo(lambda e: e.tensor_tensor(ge1, ge1, sc_, ALU.mult), reads=(bsc,), writes=(bge,))
            b.trans(5, 8, 128, ge1, V(b.O_ID128, 128, 128), reads=(bge,))
            b.vo(lambda e, tc=tc: e.tensor_copy(VR(o_cmb + tc * 128, 0, 8, 128), PS[5][0:8, 0:128]),
                 reads=(psb[5],), writes=(b_cmb,))
        for ei in range(NEXP):
            bcb = B("fcb")
            bd8 = VR(o_bd8, 0, 8, 512)
            b.vo(lambda e, ei=ei: e.tensor_scalar(bd8, VR(o_cmb, 0, 8, 512),
                                                  V(b.O_ID8 + ei, 8, 1), None, ALU.mult),
                 reads=(b_cmb, bc), writes=(B("fbd8"),))
            b.mm(6, 512, V(b.O_ONES8, 8, 128), bd8, True, True, reads=(B("fbd8"), bc))
            b.vo(lambda e: e.tensor_copy(V(o_cb, 128, 512), PS[6][:, 0:512]),
                 reads=(psb[6],), writes=(bcb,))
            bact = B("fact")
            for fp in range(11):
                r0 = (ei * 22 + fp) * 128
                r1 = (ei * 22 + fp + 11) * 128
                s0 = o_wsl3 + (fp % 2) * 4096
                w0, w1 = b_wsl[(fp % 2) * 2], b_wsl[(fp % 2) * 2 + 1]
                emit_wload(s0, wgup_s, r0, 2048, w0)
                emit_wload(s0 + 2048, wgup_s, r1, 2048, w1)
                for kc in range(NKC):
                    xh = fxn_sl(kc, 0, 0, 512)
                    xlo = fxn_sl(kc, 1, 0, 512)
                    mm3(0, 512, bfv(s0 + kc * 64, 64), bfv(s0 + 1024 + kc * 64, 64),
                        xh, xlo, kc == 0, kc == NKC - 1, reads=(w0, b_x))
                for kc in range(NKC):
                    xh = fxn_sl(kc, 0, 0, 512)
                    xlo = fxn_sl(kc, 1, 0, 512)
                    mm3(1, 512, bfv(s0 + 2048 + kc * 64, 64), bfv(s0 + 3072 + kc * 64, 64),
                        xh, xlo, kc == 0, kc == NKC - 1, reads=(w1, b_x))
                fs_ = V(o_fsc, 128, 512)
                bfs = B("ffs")
                S.op("scalar", lambda e, fs_=fs_: e.activation(fs_, PS[0][:, 0:512], AF.Silu),
                     reads=(psb[0],), writes=(bfs,))
                b.vo(lambda e, fs_=fs_: e.tensor_tensor(fs_, fs_, PS[1][:, 0:512], ALU.mult),
                     reads=(psb[1],), writes=(bfs,))
                b.vo(lambda e, fs_=fs_: e.tensor_tensor(fs_, fs_, V(o_cb, 128, 512), ALU.mult),
                     reads=(bcb,), writes=(bfs,))
                ahi = bfv(o_act + fp * 512, 256)
                alo_ = bfv(o_act + fp * 512 + 256, 256)
                S.op("scalar", lambda e, ahi=ahi, fs_=fs_: e.copy(ahi, fs_),
                     reads=(bfs,), writes=(bact,))
                b.vo(lambda e, alo_=alo_, fs_=fs_, ahi=ahi: e.tensor_tensor(alo_, fs_, ahi, ALU.subtract),
                     reads=(bfs,), writes=(bact,))
            for fo in range(16):
                sl = fo % 4
                r0 = (ei * 16 + fo) * 128
                emit_wload(o_wsl3 + sl * 2048, wdwn_s, r0, FF, b_wsl[sl])
                bank = 2 + (fo % 2)
                for kc in range(11):
                    whi = bfv(o_wsl3 + sl * 2048 + kc * 64, 64)
                    wlo = bfv(o_wsl3 + sl * 2048 + 704 + kc * 64, 64)
                    ahi = bfv(o_act + kc * 512, 256)
                    alo_ = bfv(o_act + kc * 512 + 256, 256)
                    mm3(bank, 512, whi, wlo, ahi, alo_, kc == 0, kc == 10,
                        reads=(b_wsl[sl], bact))
                av = V(o_acc + fo * 512, 128, 512)
                b.vo(lambda e, av=av, bank=bank: e.tensor_tensor(av, av, PS[bank][:, 0:512], ALU.add),
                     reads=(psb[bank],), writes=(b_a,))
        for st in range(2):
            bact = B("fact")
            for fp in range(11):
                ft = st * 11 + fp
                s0 = o_wsl3 + (ft % 2) * 4096
                w0, w1 = b_wsl[(ft % 2) * 2], b_wsl[(ft % 2) * 2 + 1]
                emit_wload(s0, wsgu_s, ft * 128, 2048, w0)
                emit_wload(s0 + 2048, wsgu_s, (ft + 22) * 128, 2048, w1)
                for kc in range(NKC):
                    xh = fxn_sl(kc, 0, 0, 512)
                    xlo = fxn_sl(kc, 1, 0, 512)
                    mm3(0, 512, bfv(s0 + kc * 64, 64), bfv(s0 + 1024 + kc * 64, 64),
                        xh, xlo, kc == 0, kc == NKC - 1, reads=(w0, b_x))
                for kc in range(NKC):
                    xh = fxn_sl(kc, 0, 0, 512)
                    xlo = fxn_sl(kc, 1, 0, 512)
                    mm3(1, 512, bfv(s0 + 2048 + kc * 64, 64), bfv(s0 + 3072 + kc * 64, 64),
                        xh, xlo, kc == 0, kc == NKC - 1, reads=(w1, b_x))
                fs_ = V(o_fsc, 128, 512)
                bfs = B("ffs")
                S.op("scalar", lambda e, fs_=fs_: e.activation(fs_, PS[0][:, 0:512], AF.Silu),
                     reads=(psb[0],), writes=(bfs,))
                b.vo(lambda e, fs_=fs_: e.tensor_tensor(fs_, fs_, PS[1][:, 0:512], ALU.mult),
                     reads=(psb[1],), writes=(bfs,))
                ahi = bfv(o_act + fp * 512, 256)
                alo_ = bfv(o_act + fp * 512 + 256, 256)
                S.op("scalar", lambda e, ahi=ahi, fs_=fs_: e.copy(ahi, fs_),
                     reads=(bfs,), writes=(bact,))
                b.vo(lambda e, alo_=alo_, fs_=fs_, ahi=ahi: e.tensor_tensor(alo_, fs_, ahi, ALU.subtract),
                     reads=(bfs,), writes=(bact,))
            for fo in range(16):
                sl = fo % 4
                b.dma(VR(o_wsl3 + sl * 2048, 0, 128, FF).bitcast(mybir.dt.bfloat16)
                      .rearrange("p (h t) -> p h t", h=2),
                      wsd_s[fo * 128:(fo + 1) * 128, :].rearrange("p (h s) -> p h s", h=2)
                      [:, :, st * FF:(st + 1) * FF],
                      reads=(B(("wscr", "wsd_s")),), writes=(b_wsl[sl],))
                bank = 2 + (fo % 2)
                for kc in range(11):
                    whi = bfv(o_wsl3 + sl * 2048 + kc * 64, 64)
                    wlo = bfv(o_wsl3 + sl * 2048 + 704 + kc * 64, 64)
                    ahi = bfv(o_act + kc * 512, 256)
                    alo_ = bfv(o_act + kc * 512 + 256, 256)
                    mm3(bank, 512, whi, wlo, ahi, alo_, kc == 0, kc == 10,
                        reads=(b_wsl[sl], bact))
                av = V(o_acc + fo * 512, 128, 512)
                b.vo(lambda e, av=av, bank=bank: e.tensor_tensor(av, av, PS[bank][:, 0:512], ALU.add),
                     reads=(psb[bank],), writes=(b_a,))
        b.dma(out_re[:, :, half * 512:(half + 1) * 512],
              V(o_acc, 128, 16 * 512).rearrange("p (k t) -> p k t", k=16),
              reads=(b_a,), writes=(B("fout"),))

    fence = []
    for s, v in S.counts.items():
        if "_d" in s and v > 0:
            fence.append((s, v))
    S.op("sync", lambda e: e.nop(), extra=tuple(fence))
    S.emit(nc)
    return nc



def _tile_lhsT(W, K=HID):
    F = W.shape[0]
    nft = F // 128
    nkc = K // 128
    H = W.reshape(nft, 128, nkc, 128).transpose(0, 3, 2, 1).reshape(nft * 128, K)
    return np.ascontiguousarray(H)


def _prep_weights(inp):
    f32 = np.float32
    g = lambda k: np.asarray(inp[k], dtype=f32)
    W = {}
    W["wqkv"] = _tile_lhsT(g("L0_attn_Wqkv"))
    W["wz"] = np.ascontiguousarray(g("L0_attn_Wz").T.reshape(NKC * 128, HID))
    W["wout_w"] = _tile_lhsT(g("L0_attn_Wout"), K=VAL_DIM)
    gup = g("L0_ffn_gate_up_weights")
    W["wgup"] = np.concatenate([_tile_lhsT(gup[e]) for e in range(NEXP)], 0)
    dwn = g("L0_ffn_down_weights")
    W["wdwn"] = np.concatenate([_tile_lhsT(dwn[e], K=FF) for e in range(NEXP)], 0)
    W["wsgu"] = _tile_lhsT(g("L0_ffn_shared_gate_up"))
    W["wsd"] = _tile_lhsT(g("L0_ffn_shared_down"), K=FS)
    W["wa_t"] = np.ascontiguousarray(
        g("L0_attn_Wa").T.reshape(NKC, 128, 16).transpose(1, 0, 2).reshape(128, NKC * 16))
    W["wb_t"] = np.ascontiguousarray(
        g("L0_attn_Wb").T.reshape(NKC, 128, 16).transpose(1, 0, 2).reshape(128, NKC * 16))
    W["wgate"] = np.ascontiguousarray(
        g("L0_ffn_gate_weight").T.reshape(NKC, 128, 8).transpose(1, 0, 2).reshape(128, NKC * 8))
    W["bqkv_t"] = np.ascontiguousarray(g("L0_attn_bqkv").reshape(32, 128).T)
    W["cb_t"] = np.ascontiguousarray(g("L0_attn_conv_b").reshape(32, 128).T)
    W["cw_t"] = np.ascontiguousarray(
        g("L0_attn_conv_w").reshape(32, 128, 4).transpose(1, 0, 2).reshape(128, 128))
    W["bz_row"] = g("L0_attn_bz")[None, :]
    W["ba_c"] = g("L0_attn_ba")[:, None]
    W["bb_c"] = g("L0_attn_bb")[:, None]
    W["dtb"] = g("L0_attn_dt_bias")[:, None]
    W["alog"] = g("L0_attn_A_log")[:, None]
    W["nw_row"] = g("L0_attn_norm_w")[None, :]
    W["bout_t"] = np.ascontiguousarray(g("L0_attn_bout").reshape(16, 128).T)
    W["ones_row"] = np.ones((1, 128), f32)
    W["ones_col"] = np.ones((128, 1), f32)
    W["ones8"] = np.ones((8, 128), f32)
    W["id8"] = np.eye(8, dtype=f32)
    W["id16"] = np.eye(16, dtype=f32)
    W["id32"] = np.eye(32, dtype=f32)
    W["id128"] = np.eye(128, dtype=f32)
    W["blk8"] = np.repeat(np.eye(8, dtype=f32), 128, axis=1)
    W["mask_su"] = np.triu(np.ones((128, 128), f32), 1)
    W["mask_sl"] = np.tril(np.ones((128, 128), f32), -1)
    W["mask_iu"] = np.triu(np.ones((128, 128), f32), 0)
    return W


def _prep_core(x, bidx, t0):
    f32 = np.float32
    xb = np.asarray(x[bidx], dtype=f32)
    xpre = np.zeros((NPT * PCOLS, HID), f32)
    premask = np.zeros((1, NPT * PCOLS), f32)
    start0 = t0 - PREF
    for pt in range(NPT):
        ts = start0 + pt * PTILE
        for j in range(PCOLS):
            tok = ts - 3 + (j - 1) + 1 if False else ts + (j - 4)
            if j == 0:
                continue
            tok = ts + (j - 4)
            if 0 <= tok < 4096:
                xpre[pt * PCOLS + j] = xb[tok]
                premask[0, pt * PCOLS + j] = 1.0
    xspan = np.zeros((NQ * QCOLS, HID), f32)
    for qq in range(NQ):
        ts = t0 + qq * QTOK
        for j in range(1, QCOLS):
            tok = ts + (j - 4)
            if 0 <= tok < 4096:
                xspan[qq * QCOLS + j] = xb[tok]
    tailm = np.full((128, 1), 0.0 if t0 == 0 else 1.0, f32)
    return {
        "xpre": np.ascontiguousarray(xpre.T),
        "premask": premask,
        "xspan": np.ascontiguousarray(xspan.T),
        "tailm": tailm,
    }


_NC_CACHE = {}


def get_nc(debug=False):
    if debug not in _NC_CACHE:
        _NC_CACHE[debug] = build_full(debug)
    return _NC_CACHE[debug]


def kernel(**inputs):
    x = np.asarray(inputs["L0_attn_hidden_states"], dtype=np.float32)
    W = _prep_weights(inputs)
    nc = get_nc(False)
    in_maps = []
    for c in range(NCORES):
        bidx, t0 = c // 4, (c % 4) * SPAN
        m = dict(W)
        m.update(_prep_core(x, bidx, t0))
        in_maps.append(m)
    res = run_bass_kernel_spmd(nc, in_maps, core_ids=list(range(NCORES)))
    outf = np.empty((2, 4096, HID), np.float32)
    for c in range(NCORES):
        bidx, t0 = c // 4, (c % 4) * SPAN
        outf[bidx, t0:t0 + SPAN] = res.results[c]["out"].T
    return outf


# version 09
# speedup vs baseline: 1.1068x; 1.0333x over previous
import os
from collections import defaultdict

import numpy as np
import concourse.bass as bass
import concourse.mybir as mybir
from concourse.bass_utils import run_bass_kernel_spmd

F32 = mybir.dt.float32
AF = mybir.ActivationFunctionType
ALU = mybir.AluOpType
AX = mybir.AxisListType

HID = 2048
HK = 8
HV = 16
DK = 128
DV = 128
KCONV = 4
KEY_DIM = HK * DK
VAL_DIM = HV * DV
CONV_DIM = 2 * KEY_DIM + VAL_DIM
RMS_EPS = 1e-6
L2_EPS = 1e-6
LN_EPS = 1e-5
NEXP = 8
FF = 1408
FS = 2816
NKC = HID // 128

SPAN = 1024
PREF = 3072
PTILE = 512
NPT = PREF // PTILE
PCOLS = PTILE + 4
QTOK = 256
NQ = SPAN // QTOK
QCOLS = QTOK + 4
CH = 128

NCORES = 8



class Buf:
    __slots__ = ("writer", "readers", "name")
    def __init__(self, name=""):
        self.writer = None
        self.readers = []
        self.name = name


class PH:
    __slots__ = ("eng", "tok")
    def __init__(self, eng):
        self.eng = eng
        self.tok = None


class Sched:
    ENGINES = ("tensor", "vector", "scalar", "gpsimd", "sync")
    KSEM = 4

    def __init__(self):
        self.ops = {e: [] for e in self.ENGINES}
        self.counts = defaultdict(int)
        self.opidx = defaultdict(int)
        self.waiters = defaultdict(list)
        self.pending_ph = {e: [] for e in self.ENGINES}
        self._rr = 0

    def _force_token(self, eng):
        return self.op(eng, lambda e: e.nop(), reads=(), writes=(), dma=False,
                       extra=(), quiet=False, _force=True)

    def _resolve(self, d, eng):
        if not isinstance(d, PH):
            return d
        if d.tok is not None:
            return d.tok
        if d.eng == eng:
            return None
        return self._force_token(d.eng)

    def op(self, eng, fn, reads=(), writes=(), dma=False, extra=(), quiet=False,
           _force=False):
        deps = []
        for d in extra:
            if d is None:
                continue
            d = self._resolve(d, eng)
            if d is not None:
                deps.append(d)
        for b in reads:
            w = self._resolve(b.writer, eng) if b.writer is not None else None
            if w is not None:
                deps.append(w)
        for b in writes:
            w = self._resolve(b.writer, eng) if b.writer is not None else None
            if w is not None:
                deps.append(w)
            for r in list(b.readers):
                r = self._resolve(r, eng)
                if r is not None:
                    deps.append(r)
        if quiet and not writes and not dma:
            ph = PH(eng)
            self.ops[eng].append((fn, deps, None, dma))
            for (s, v) in deps:
                self.waiters[s].append((v, ph))
            for b in reads:
                b.readers.append(ph)
            self.pending_ph[eng].append(ph)
            return ph
        kind = "d" if dma else "c"
        idx = self.opidx[(eng, kind)]
        self.opidx[(eng, kind)] = idx + 1
        sem = f"{eng}_{kind}{idx % self.KSEM}"
        inc = 16 if dma else 1
        start_val = self.counts[sem]
        wl = self.waiters[sem]
        if wl:
            keep = []
            for (v, wtok) in wl:
                if v == start_val:
                    if isinstance(wtok, PH):
                        if wtok.tok is not None:
                            deps.append(wtok.tok)
                        elif wtok.eng != eng:
                            raise AssertionError("unresolved waiter placeholder")
                    else:
                        deps.append(wtok)
                elif v > start_val:
                    keep.append((v, wtok))
            self.waiters[sem] = keep
        self.counts[sem] = start_val + inc
        tok = (sem, start_val + inc)
        self.ops[eng].append((fn, deps, tok, dma))
        for (s, v) in deps:
            self.waiters[s].append((v, tok))
        for ph in self.pending_ph[eng]:
            ph.tok = tok
        self.pending_ph[eng].clear()
        for b in reads:
            b.readers.append(tok)
        for b in writes:
            b.writer = tok
            b.readers = []
        return tok

    def ev(self):
        self._rr += 1
        return "vector" if self._rr % 2 else "scalar"

    def emit(self, nc):
        sems = {}
        for e in self.ENGINES:
            for kind in ("c", "d"):
                for i in range(self.KSEM):
                    name = f"{e}_{kind}{i}"
                    sems[name] = nc.semaphore(f"S_{name}").__enter__()
        blk = nc.Block()
        block = blk.__enter__()

        def mk(eng_name):
            own = eng_name + "_c" if eng_name == "tensor" else "\0"
            def body(eng):
                watermark = defaultdict(int)
                for fn, deps, tok, dma in self.ops[eng_name]:
                    need = {}
                    for (sem, val) in deps:
                        if sem.startswith(own):
                            continue
                        if val > watermark[sem] and val > need.get(sem, 0):
                            need[sem] = val
                    for sem, val in need.items():
                        eng.wait_ge(sems[sem], val)
                        watermark[sem] = val
                    inst = fn(eng)
                    if tok is not None:
                        inst.then_inc(sems[tok[0]], 16 if dma else 1)
            return body

        block.sync(mk("sync"))
        block.tensor(mk("tensor"))
        block.vector(mk("vector"))
        block.scalar(mk("scalar"))
        block.gpsimd(mk("gpsimd"))
        blk.__exit__(None, None, None)



class Bld:

    def __init__(self, arena_cols=51500):
        self.nc = bass.Bass("TRN2", target_bir_lowering=False)
        self.S = Sched()
        self.AR = self.nc.alloc_sbuf_tensor("AR", [128, arena_cols], F32).ap()
        self.PS = [self.nc.alloc_psum_tensor(f"PB{i}", [128, 512], F32).ap() for i in range(8)]
        self.psb = [Buf(f"ps{i}") for i in range(8)]
        self._cur = 0
        self._arena_cols = arena_cols
        self.bufs = {}
        self.marks = []

    def alloc(self, n, name=""):
        o = self._cur
        self._cur += n
        assert self._cur <= self._arena_cols, f"arena overflow at {name}: {self._cur}"
        return o

    def push(self):
        self.marks.append(self._cur)

    def pop(self):
        self._cur = self.marks.pop()

    def V(self, off, parts, cols):
        return self.AR[0:parts, off:off + cols]

    def VR(self, off, p0, p1, cols):
        return self.AR[p0:p1, off:off + cols]

    def B(self, key):
        if key not in self.bufs:
            self.bufs[key] = Buf(str(key))
        return self.bufs[key]

    def dma(self, dst, src, reads=(), writes=(), eng="sync"):
        return self.S.op(eng, lambda e: e.dma_start(out=dst, in_=src),
                         reads=reads, writes=writes, dma=True)

    def mm(self, bank, n, lhsT, rhs, start, stop, reads=(), c0=0, parts=128):
        ps = self.PS[bank][0:parts, c0:c0 + n]
        extra = ()
        if start:
            extra = tuple(self.psb[bank].readers)
            if self.psb[bank].writer is not None:
                extra += (self.psb[bank].writer,)
        return self.S.op("tensor",
                         lambda e: e.matmul(ps, lhsT, rhs, start=start, stop=stop),
                         reads=reads,
                         writes=(self.psb[bank],) if stop else (), extra=extra,
                         quiet=False)

    def trans(self, bank, out_parts, out_cols, in_, ident, reads=(), c0=0):
        ps = self.PS[bank][0:out_parts, c0:c0 + out_cols]
        extra = tuple(self.psb[bank].readers)
        if self.psb[bank].writer is not None:
            extra += (self.psb[bank].writer,)
        return self.S.op("tensor", lambda e: e.transpose(ps, in_, ident),
                         reads=reads, writes=(self.psb[bank],), extra=extra)

    def vo(self, fn, reads=(), writes=(), eng="vector"):
        return self.S.op(eng, fn, reads=reads, writes=writes)

    def cp(self, dst, src, reads=(), writes=()):
        eng = self.S.ev()
        if eng == "scalar":
            return self.S.op(eng, lambda e: e.copy(dst, src), reads=reads, writes=writes)
        return self.S.op(eng, lambda e: e.tensor_copy(dst, src), reads=reads, writes=writes)

    def ev2(self):
        return "vector"

    def alloc_state(self):
        self.O_S = self.alloc(HV * DK, "S")
        self.b_S = [self.B(("S", h)) for h in range(HV)]
        for h in range(HV):
            self.vo(lambda e, h=h: e.memset(self.V(self.O_S + h * DK, 128, DK), 0.0),
                    writes=(self.b_S[h],))

    def alloc_scan_ws(self):
        a = self.alloc
        self.O_GBTM = a(32, "gbtm")
        self.O_CGTM = a(16, "cgtm")
        self.O_LAM0 = a(16, "lam0")
        self.O_CCOL = a(16, "ccol")
        self.O_CGF = [a(128, "cgf0"), a(128, "cgf1")]
        self.O_BFM = [a(128, "bfm0"), a(128, "bfm1")]
        self.O_BD = a(1024, "bd")
        self.O_E = a(1024, "etile")
        self.O_EP = a(1024, "eptile")
        self.O_BB = a(1024, "bbeta")
        self.O_KK = a(512, "kk4")
        self.O_QK = a(512, "qk4")
        self.O_G = a(1024, "gtile")
        self.O_E127 = a(8, "e127")
        self.O_LAMC = a(8, "lamc")
        self.O_KTM = a(1024, "ktm")
        self.O_VTM = a(512, "vtm")
        self.O_PA = a(512, "pa")
        self.O_PB = a(512, "pb")
        self.O_X = a(1024, "xsolve")
        self.O_UB = a(512, "ubar")
        self.O_WFM = a(512, "wfm")
        self.O_CORE = a(128, "core")
        self.O_ZS = a(128, "zsil")

    def alloc_scan_consts(self):
        a = self.alloc
        self.O_ONESR = a(128, "onesr")
        self.O_ONESC = a(1, "onesc")
        self.O_ONES8 = a(128, "ones8")
        self.O_ID8 = a(8, "id8")
        self.O_ID16 = a(16, "id16")
        self.O_ID32 = a(32, "id32")
        self.O_ID128 = a(128, "id128")
        self.O_MSU = a(128, "msu")
        self.O_MSL = a(128, "msl")
        self.O_MIU = a(128, "miu")
        self.O_BLK8 = a(1024, "blk8")
        self.O_NWBC = a(128, "nwbc")

    def load_scan_consts(self, din):
        V, b = self.V, self.B("const")
        for dst, name, shape in [
            (V(self.O_ONESR, 1, 128), "ones_row", [1, 128]),
            (V(self.O_ONESC, 128, 1), "ones_col", [128, 1]),
            (V(self.O_ONES8, 8, 128), "ones8", [8, 128]),
            (V(self.O_ID8, 8, 8), "id8", [8, 8]),
            (V(self.O_ID16, 16, 16), "id16", [16, 16]),
            (V(self.O_ID32, 32, 32), "id32", [32, 32]),
            (V(self.O_ID128, 128, 128), "id128", [128, 128]),
            (V(self.O_MSU, 128, 128), "mask_su", [128, 128]),
            (V(self.O_MSL, 128, 128), "mask_sl", [128, 128]),
            (V(self.O_MIU, 128, 128), "mask_iu", [128, 128]),
            (V(self.O_BLK8, 8, 1024), "blk8", [8, 1024]),
        ]:
            self.dma(dst, din(name, shape), writes=(b,))

    def emit_chunk(self, kq_slices, v_slices, o_g, o_bt, gcol, b_p, b_gb, with_out,
                   core_sink=None, probe=None, o_alt=None):
        S, V, VR, B, PS = self.S, self.V, self.VR, self.B, self.PS
        psb = self.psb
        bc = B("const")

        bgbt = B("gbtm")
        self.trans(4, 128, 16, VR(o_g + gcol, 0, 16, CH), V(self.O_ID16, 16, 16),
                   reads=(b_gb,))
        self.vo(lambda e: e.tensor_copy(V(self.O_GBTM, 128, 16), PS[4][:, 0:16]),
                reads=(psb[4],), writes=(bgbt,))
        self.trans(5, 128, 16, VR(o_bt + gcol, 0, 16, CH), V(self.O_ID16, 16, 16),
                   reads=(b_gb,))
        self.vo(lambda e: e.tensor_copy(VR(self.O_GBTM + 16, 0, 128, 16), PS[5][:, 0:16]),
                reads=(psb[5],), writes=(bgbt,))
        mm = self.mm
        mm(5, 16, V(self.O_MIU, 128, 128), V(self.O_GBTM, 128, 16), True, True,
           reads=(bc, bgbt))
        bcg = B("cgtm")
        self.vo(lambda e: e.tensor_copy(V(self.O_CGTM, 128, 16), PS[5][:, 0:16]),
                reads=(psb[5],), writes=(bcg,))
        if probe: probe("cgtm", V(self.O_CGTM, 128, 16), (bcg,))
        blam = B("lam0")
        S.op("scalar", lambda e: e.activation(V(self.O_LAM0, 128, 16), V(self.O_CGTM, 128, 16), AF.Exp),
             reads=(bcg,), writes=(blam,))
        bccol = B("ccol")
        self.vo(lambda e: e.tensor_tensor(V(self.O_CCOL, 128, 16), V(self.O_LAM0, 128, 16),
                                          VR(self.O_GBTM + 16, 0, 128, 16), ALU.mult),
                reads=(blam, bgbt), writes=(bccol,))
        bcgf = [B("cgf0"), B("cgf1")]
        bbfm = [B("bfm0"), B("bfm1")]
        for half in range(2):
            self.trans(4, 8, 128, V(self.O_CGTM + 8 * half, 128, 8),
                       V(self.O_ID128, 128, 128), reads=(bcg,), c0=128 * half)
            self.vo(lambda e, half=half: e.tensor_copy(
                V(self.O_CGF[half], 8, 128), PS[4][0:8, 128 * half:128 * half + 128]),
                reads=(psb[4],), writes=(bcgf[half],))
        for half in range(2):
            self.trans(5, 8, 128, VR(self.O_GBTM + 16 + 8 * half, 0, 128, 8),
                       V(self.O_ID128, 128, 128), reads=(bgbt,), c0=128 * half)
            self.vo(lambda e, half=half: e.tensor_scalar(
                V(self.O_BFM[half], 8, 128), PS[5][0:8, 128 * half:128 * half + 128],
                -1.0, None, ALU.mult),
                reads=(psb[5],), writes=(bbfm[half],))

        bktm = B("ktm")
        for kh in range(HK):
            k_ap, _ = kq_slices(kh)
            bank = 4 if kh % 2 == 0 else 5
            self.trans(bank, 128, 128, k_ap, V(self.O_ID128, 128, 128), reads=(b_p,))
            self.cp(V(self.O_KTM + kh * 128, 128, 128), PS[bank][:, 0:128],
                    reads=(psb[bank],), writes=(bktm,))

        for bt in range(2):
            bD = B("etile")
            bDP = B("eptile")
            bBB = B("bbeta")
            bKK = B("kk4")
            bQK = B("qk4")
            bG = B("gtile")
            bE127 = B("e127")
            bLC = B("lamc")

            for i in range(4):
                kh = 4 * bt + i
                k_ap, _ = kq_slices(kh)
                mm(0, 128, k_ap, k_ap, True, True, reads=(b_p,), c0=i * 128)
            tkk = self.vo(lambda e: e.tensor_copy(V(self.O_KK, 128, 512), PS[0][:, 0:512]),
                          reads=(psb[0],), writes=(bKK,))
            if with_out:
                for i in range(4):
                    kh = 4 * bt + i
                    k_ap, q_ap = kq_slices(kh)
                    mm(1, 128, k_ap, q_ap, True, True, reads=(b_p,), c0=i * 128)
                self.vo(lambda e: e.tensor_copy(V(self.O_QK, 128, 512), PS[1][:, 0:512]),
                        reads=(psb[1],), writes=(bQK,))

            bBD = B("bd")
            cgf = V(self.O_CGF[bt], 8, 128)
            cgf_b = cgf.rearrange("p (o f) -> p o f", o=1).broadcast_to([8, 8, 128])
            blkv = V(self.O_BLK8, 8, 1024).rearrange("p (o f) -> p o f", o=8)
            bdv = V(self.O_BD, 8, 1024).rearrange("p (o f) -> p o f", o=8)
            self.vo(lambda e, bdv=bdv, cgf_b=cgf_b, blkv=blkv: e.tensor_tensor(bdv, cgf_b, blkv, ALU.mult),
                    reads=(bcgf[bt], bc), writes=(bBD,))
            if probe: probe(f"BD{bt}", self.V(self.O_BD, 8, 1024), (bBD,))
            for half in range(2):
                mm(2, 512, cgf, VR(self.O_BLK8 + half * 512, 0, 8, 512), True, True,
                   reads=(bcgf[bt], bc), c0=0)
                self.vo(lambda e, half=half: e.tensor_copy(
                    V(self.O_EP + half * 512, 128, 512), PS[2][:, 0:512]),
                    reads=(psb[2],), writes=(bDP,))
            for half in range(2):
                mm(3, 512, V(self.O_ONES8, 8, 128), VR(self.O_BD + half * 512, 0, 8, 512),
                   True, True, reads=(bBD, bc), c0=0)
                self.vo(lambda e, half=half: e.tensor_copy(
                    V(self.O_E + half * 512, 128, 512), PS[3][:, 0:512]),
                    reads=(psb[3],), writes=(bD,))
            if probe: probe(f"CGS{bt}", V(self.O_EP, 128, 1024), (bDP,))
            if probe: probe(f"BCG{bt}", V(self.O_E, 128, 1024), (bD,))
            bfm = V(self.O_BFM[bt], 8, 128)
            bfm_b = bfm.rearrange("p (o f) -> p o f", o=1).broadcast_to([8, 8, 128])
            self.vo(lambda e, bdv=bdv, bfm_b=bfm_b, blkv=blkv: e.tensor_tensor(bdv, bfm_b, blkv, ALU.mult),
                    reads=(bbfm[bt], bc), writes=(bBD,))
            for half in range(2):
                mm(2, 512, V(self.O_ONES8, 8, 128), VR(self.O_BD + half * 512, 0, 8, 512),
                   True, True, reads=(bBD, bc), c0=0)
                self.vo(lambda e, half=half: e.tensor_copy(
                    V(self.O_BB + half * 512, 128, 512), PS[2][:, 0:512]),
                    reads=(psb[2],), writes=(bBB,))

            if probe: probe(f"CGF{bt}", V(self.O_CGF[bt], 8, 128), (bcgf[bt],))
            ev_ = V(self.O_E, 128, 1024)
            epv = V(self.O_EP, 128, 1024)
            self.vo(lambda e: e.tensor_tensor(ev_, ev_, epv, ALU.subtract),
                    reads=(bDP,), writes=(bD,))
            self.vo(lambda e: e.tensor_scalar(epv, ev_, -1.0, 0.0, ALU.mult, ALU.min),
                    reads=(bD,), writes=(bDP,))
            S.op("scalar", lambda e: e.activation(epv, epv, AF.Exp), reads=(), writes=(bDP,))
            self.vo(lambda e: e.tensor_scalar(ev_, ev_, 1.0, 0.0, ALU.mult, ALU.min),
                    reads=(), writes=(bD,))
            S.op("scalar", lambda e: e.activation(ev_, ev_, AF.Exp), reads=(), writes=(bD,))
            if probe: probe(f"E{bt}", ev_, (bD,))
            if probe: probe(f"EP{bt}", epv, (bDP,))

            e127v = V(self.O_E127, 128, 8)
            esel = V(self.O_E, 128, 1024).rearrange("p (h t) -> p h t", h=8)[:, :, 127:128]
            self.vo(lambda e, e127v=e127v, esel=esel: e.tensor_copy(e127v.rearrange("p (h o) -> p h o", o=1), esel),
                    reads=(bD,), writes=(bE127,))
            self.vo(lambda e, bt=bt: e.tensor_tensor(
                V(self.O_LAMC, 128, 8), e127v, VR(self.O_LAM0 + 8 * bt, 0, 128, 8), ALU.mult),
                reads=(bE127, blam), writes=(bLC,))

            kkpair = V(self.O_KK, 128, 512).rearrange("p (h o t) -> p h o t", h=4, o=1) \
                .broadcast_to([128, 4, 2, 128])
            msuv = V(self.O_MSU, 128, 128).rearrange("p (o t) -> p o t", o=1).broadcast_to([128, 8, 128])
            mslv = V(self.O_MSL, 128, 128).rearrange("p (o t) -> p o t", o=1).broadcast_to([128, 8, 128])
            miuv = V(self.O_MIU, 128, 128).rearrange("p (o t) -> p o t", o=1).broadcast_to([128, 8, 128])
            ev4 = V(self.O_E, 128, 1024).rearrange("p (h r t) -> p h r t", h=4, r=2)
            epv4 = V(self.O_EP, 128, 1024).rearrange("p (h r t) -> p h r t", h=4, r=2)
            ev8 = V(self.O_E, 128, 1024).rearrange("p (h t) -> p h t", h=8)
            epv8 = V(self.O_EP, 128, 1024).rearrange("p (h t) -> p h t", h=8)
            gv8 = V(self.O_G, 128, 1024).rearrange("p (h t) -> p h t", h=8)
            gv4 = V(self.O_G, 128, 1024).rearrange("p (h r t) -> p h r t", h=4, r=2)

            if with_out:
                qkpair = V(self.O_QK, 128, 512).rearrange("p (h o t) -> p h o t", h=4, o=1) \
                    .broadcast_to([128, 4, 2, 128])
                self.vo(lambda e: e.tensor_tensor(gv8, ev8, miuv, ALU.mult),
                        reads=(bD, bc), writes=(bG,))
                self.vo(lambda e: e.tensor_tensor(gv4, gv4, qkpair, ALU.mult),
                        reads=(bQK,), writes=(bG,))
            self.vo(lambda e: e.tensor_tensor(ev8, ev8, msuv, ALU.mult),
                    reads=(bc,), writes=(bD,))
            self.vo(lambda e: e.tensor_tensor(ev4, ev4, kkpair, ALU.mult),
                    reads=(bKK,), writes=(bD,))
            self.vo(lambda e: e.tensor_tensor(V(self.O_E, 128, 1024), V(self.O_E, 128, 1024),
                                              V(self.O_BB, 128, 1024), ALU.mult),
                    reads=(bBB,), writes=(bD,))
            if probe: probe(f"M{bt}", V(self.O_E, 128, 1024), (bD,))
            self.vo(lambda e: e.tensor_tensor(epv8, epv8, mslv, ALU.mult),
                    reads=(bc,), writes=(bDP,))
            self.vo(lambda e: e.tensor_tensor(epv4, epv4, kkpair, ALU.mult),
                    reads=(bKK,), writes=(bDP,))
            for i in range(8):
                h = 8 * bt + i
                asl = V(self.O_EP + i * 128, 128, 128)
                bcol = VR(self.O_GBTM + 16 + h, 0, 128, 1)
                eng = "vector"
                self.vo(lambda e, asl=asl, bcol=bcol:
                    e.tensor_scalar(asl, asl, bcol, -1.0, ALU.mult, ALU.mult),
                    reads=(bgbt,), writes=(bDP,), eng=eng)
            if probe: probe(f"A{bt}", V(self.O_EP, 128, 1024), (bDP,))

            wsX = [self.O_X, o_alt]
            wsPA = [self.O_PA, o_alt + 1024]
            wsPB = [self.O_PB, o_alt + 1536]
            wsUB = [self.O_UB, o_alt + 2048]
            wsWF = [self.O_WFM, o_alt + 2560]
            wsVT = [self.O_VTM, o_alt + 3072]
            bX_ = [B("xsolve"), B(("xsolve", 1))]
            bPA_ = [B("pa"), B(("pa", 1))]
            bPB_ = [B("pb"), B(("pb", 1))]
            bUB_ = [B("ubar"), B(("ubar", 1))]
            bWF_ = [B("wfm"), B(("wfm", 1))]
            bVT_ = [B("vtm"), B(("vtm", 1))]
            TRB = [(4, 5), (2, 3)]
            APB = [(2, 3), (6, 7)]
            SQB = [(0, 1), (4, 5)]
            WSB = [4, 6]
            STB = [7, 3]

            def heads_of(g):
                return [8 * bt + 4 * g + i for i in range(4)]

            for g in range(2):
                for i, h in enumerate(heads_of(g)):
                    bank = TRB[g][i % 2]
                    self.trans(bank, 128, 128, v_slices(h), V(self.O_ID128, 128, 128),
                               reads=(b_p,))
                    self.cp(V(wsVT[g] + i * 128, 128, 128), PS[bank][:, 0:128],
                            reads=(psb[bank],), writes=(bVT_[g],))
            for g in range(2):
                for i, h in enumerate(heads_of(g)):
                    kh = h // 2
                    ccol = VR(self.O_CCOL + h, 0, 128, 1)
                    btcol = VR(self.O_GBTM + 16 + h, 0, 128, 1)
                    self.vo(lambda e, i=i, kh=kh, ccol=ccol, g=g: e.tensor_scalar(
                        V(wsX[g] + i * 256, 128, 128), V(self.O_KTM + kh * 128, 128, 128),
                        ccol, None, ALU.mult),
                        reads=(bktm, bccol), writes=(bX_[g],), eng=self.ev2())
                    self.vo(lambda e, i=i, btcol=btcol, g=g: e.tensor_scalar(
                        V(wsX[g] + i * 256 + 128, 128, 128), V(wsVT[g] + i * 128, 128, 128),
                        btcol, None, ALU.mult),
                        reads=(bVT_[g], bgbt), writes=(bX_[g],), eng=self.ev2())
            if probe:
                probe(f"R{bt}0", V(wsX[0], 128, 1024), (bX_[0],))
                probe(f"R{bt}1", V(wsX[1], 128, 1024), (bX_[1],))

            def pslice(g, j, which, i):
                if j == 0:
                    of = self.O_E if which == 0 else self.O_EP
                    return V(of + (4 * g + i) * 128, 128, 128)
                of = wsPA[g] if which == 0 else wsPB[g]
                return V(of + i * 128, 128, 128)

            def pbufs(g, j):
                if j == 0:
                    return (bD, bDP)
                return (bPA_[g], bPB_[g])

            for j in range(7):
                for g in range(2):
                    pa_b, pb_b = pbufs(g, j)
                    for i in range(4):
                        bank = APB[g][0] if i < 2 else APB[g][1]
                        mm(bank, 256, pslice(g, j, 0, i), V(wsX[g] + i * 256, 128, 256),
                           True, True, reads=(pa_b, bX_[g]), c0=(i % 2) * 256)
                for g in range(2):
                    for half in range(2):
                        bank = APB[g][half]
                        xv = V(wsX[g] + half * 512, 128, 512)
                        self.vo(lambda e, xv=xv, bank=bank:
                                e.tensor_tensor(xv, xv, PS[bank][:, 0:512], ALU.add),
                                reads=(psb[bank],), writes=(bX_[g],))
                if j < 6:
                    for g in range(2):
                        pa_b, pb_b = pbufs(g, j)
                        for i in range(4):
                            mm(SQB[g][0], 128, pslice(g, j, 1, i), pslice(g, j, 0, i),
                               True, True, reads=(pa_b, pb_b), c0=i * 128)
                        for i in range(4):
                            mm(SQB[g][1], 128, pslice(g, j, 0, i), pslice(g, j, 1, i),
                               True, True, reads=(pa_b, pb_b), c0=i * 128)
                    for g in range(2):
                        self.cp(V(wsPA[g], 128, 512), PS[SQB[g][0]][:, 0:512],
                                reads=(psb[SQB[g][0]],), writes=(bPA_[g],))
                        self.cp(V(wsPB[g], 128, 512), PS[SQB[g][1]][:, 0:512],
                                reads=(psb[SQB[g][1]],), writes=(bPB_[g],))
            if probe:
                probe(f"X{bt}0", V(wsX[0], 128, 1024), (bX_[0],))
                probe(f"X{bt}1", V(wsX[1], 128, 1024), (bX_[1],))
            for g in range(2):
                for i, h in enumerate(heads_of(g)):
                    bank = TRB[g][i % 2]
                    self.trans(bank, 128, 128, V(wsX[g] + i * 256, 128, 128),
                               V(self.O_ID128, 128, 128), reads=(bX_[g],))
                    self.cp(V(wsWF[g] + i * 128, 128, 128), PS[bank][:, 0:128],
                            reads=(psb[bank],), writes=(bWF_[g],))
            for g in range(2):
                for i, h in enumerate(heads_of(g)):
                    mm(WSB[g], 128, V(wsWF[g] + i * 128, 128, 128),
                       V(self.O_S + h * DK, 128, 128), True, True,
                       reads=(bWF_[g], self.b_S[h]), c0=(i % 4) * 128)
            for g in range(2):
                for i, h in enumerate(heads_of(g)):
                    self.vo(lambda e, i=i, g=g: e.tensor_tensor(
                        V(wsUB[g] + i * 128, 128, 128), V(wsX[g] + i * 256 + 128, 128, 128),
                        PS[WSB[g]][:, i * 128:(i + 1) * 128], ALU.subtract),
                        reads=(psb[WSB[g]], bX_[g]), writes=(bUB_[g],))
            if probe:
                probe(f"UB{bt}0", V(wsUB[0], 128, 512), (bUB_[0],))
                probe(f"UB{bt}1", V(wsUB[1], 128, 512), (bUB_[1],))
            if with_out:
                for g in range(2):
                    for i, h in enumerate(heads_of(g)):
                        kh = h // 2
                        _, q_ap = kq_slices(kh)
                        mm(5, 128, q_ap, V(self.O_S + h * DK, 128, 128), True, True,
                           reads=(b_p, self.b_S[h]), c0=0)
                        gsl_h = V(self.O_G + (4 * g + i) * 128, 128, 128)
                        mm(6, 128, gsl_h, V(wsUB[g] + i * 128, 128, 128), True, True,
                           reads=(bG, bUB_[g]), c0=0)
                        lam0col = VR(self.O_LAM0 + h, 0, 128, 1)
                        bCORE = B("core")
                        cv = V(self.O_CORE, 128, 128)
                        self.vo(lambda e, cv=cv, lam0col=lam0col: e.tensor_scalar(
                            cv, PS[5][:, 0:128], lam0col, None, ALU.mult),
                            reads=(psb[5], blam), writes=(bCORE,))
                        self.vo(lambda e, cv=cv: e.tensor_tensor(cv, cv, PS[6][:, 0:128], ALU.add),
                                reads=(psb[6],), writes=(bCORE,))
                        core_sink(h, cv, (bCORE,))
            for g in range(2):
                for i, h in enumerate(heads_of(g)):
                    kh = h // 2
                    e127c = VR(self.O_E127 + (h - 8 * bt), 0, 128, 1)
                    self.vo(lambda e, i=i, kh=kh, e127c=e127c, g=g: e.tensor_scalar(
                        V(wsX[g] + i * 256, 128, 128), V(self.O_KTM + kh * 128, 128, 128),
                        e127c, None, ALU.mult),
                        reads=(bktm, bE127), writes=(bX_[g],), eng=self.ev2())
            for g in range(2):
                for i, h in enumerate(heads_of(g)):
                    mm(STB[g], 128, V(wsX[g] + i * 256, 128, 128),
                       V(wsUB[g] + i * 128, 128, 128),
                       True, True, reads=(bX_[g], bUB_[g]), c0=(i % 4) * 128)
            for g in range(2):
                for i, h in enumerate(heads_of(g)):
                    lamcc = VR(self.O_LAMC + (h - 8 * bt), 0, 128, 1)
                    sv = V(self.O_S + h * DK, 128, 128)
                    self.vo(lambda e, sv=sv, lamcc=lamcc, i=i, g=g: e.scalar_tensor_tensor(
                        sv, sv, lamcc, PS[STB[g]][:, i * 128:(i + 1) * 128], ALU.mult, ALU.add),
                        reads=(psb[STB[g]], bLC), writes=(self.b_S[h],))


FULL_BUILD_PLACEHOLDER = True




def build_full(debug=False, phases=("prefix", "span", "ffn")):
    b = Bld(53200)
    nc = b.nc
    S, V, VR, PS, psb = b.S, b.V, b.VR, b.PS, b.psb
    B = b.B
    BF16 = mybir.dt.bfloat16

    def bfv(off, cols_f32):
        return b.AR[0:128, off:off + cols_f32].bitcast(BF16)

    def din(name, shape):
        return nc.dram_tensor(name, shape, F32, kind="ExternalInput").ap()

    xpre = din("xpre", [HID, NPT * PCOLS])
    premask = din("premask", [1, NPT * PCOLS])
    xspan = din("xspan", [HID, NQ * QCOLS])
    tailm = din("tailm", [128, 1])
    wqkv = din("wqkv", [CONV_DIM, HID])
    wz = din("wz", [HID, HID])
    wout_w = din("wout_w", [VAL_DIM, HID])
    wgup = din("wgup", [NEXP * 22 * 128, HID])
    wdwn = din("wdwn", [NEXP * 16 * 128, FF])
    wsgu = din("wsgu", [44 * 128, HID])
    wsd = din("wsd", [16 * 128, FS])
    wa_t = din("wa_t", [128, NKC * 16])
    wb_t = din("wb_t", [128, NKC * 16])
    wgate = din("wgate", [128, NKC * NEXP])
    bqkv_t = din("bqkv_t", [128, 32])
    cb_t = din("cb_t", [128, 32])
    cw_t = din("cw_t", [128, 128])
    bz_row = din("bz_row", [1, VAL_DIM])
    ba_c = din("ba_c", [16, 1])
    bb_c = din("bb_c", [16, 1])
    dtb = din("dtb", [16, 1])
    alog = din("alog", [16, 1])
    nw_row = din("nw_row", [1, DV])
    bout_t = din("bout_t", [128, 16])

    out = nc.dram_tensor("out", [HID, SPAN], F32, kind="ExternalOutput").ap()
    wqkv_s = nc.dram_tensor("wqkv_s", [CONV_DIM, 2 * HID], mybir.dt.bfloat16, kind="Internal").ap()
    wz_s = nc.dram_tensor("wz_s", [HID, 2 * HID], mybir.dt.bfloat16, kind="Internal").ap()
    wout_s = nc.dram_tensor("wout_s", [VAL_DIM, 2 * HID], mybir.dt.bfloat16, kind="Internal").ap()
    wgup_s = nc.dram_tensor("wgup_s", [NEXP * 22 * 128, 2 * HID], mybir.dt.bfloat16, kind="Internal").ap()
    wdwn_s = nc.dram_tensor("wdwn_s", [NEXP * 16 * 128, 2 * FF], mybir.dt.bfloat16, kind="Internal").ap()
    wsgu_s = nc.dram_tensor("wsgu_s", [44 * 128, 2 * HID], mybir.dt.bfloat16, kind="Internal").ap()
    wsd_s = nc.dram_tensor("wsd_s", [16 * 128, 2 * FS], mybir.dt.bfloat16, kind="Internal").ap()
    xnh_scr = nc.dram_tensor("xnh_scr", [HID, SPAN], BF16, kind="Internal").ap()
    xnl_scr = nc.dram_tensor("xnl_scr", [HID, SPAN], BF16, kind="Internal").ap()
    ax_scr = nc.dram_tensor("ax_scr", [HID, SPAN], F32, kind="Internal").ap()
    out_re = out.rearrange("(k p) t -> p k t", p=128)
    xnh_re = xnh_scr.rearrange("(k p) t -> p k t", p=128)
    xnl_re = xnl_scr.rearrange("(k p) t -> p k t", p=128)
    ax_re = ax_scr.rearrange("(k p) t -> p k t", p=128)
    xpre_re = xpre.rearrange("(k p) t -> p k t", p=128)
    xspan_re = xspan.rearrange("(k p) t -> p k t", p=128)
    if debug:
        d_spre = nc.dram_tensor("d_spre", [128, HV * DK], F32, kind="ExternalOutput").ap()
        d_ax = nc.dram_tensor("d_ax", [HID, SPAN], F32, kind="ExternalOutput").ap()

    b.alloc_scan_consts()
    O_WAH = b.alloc(128, "wah"); O_WAL = b.alloc(128, "wal")
    O_WBH = b.alloc(128, "wbh"); O_WBL = b.alloc(128, "wbl")
    O_WGH = b.alloc(64, "wgh"); O_WGL = b.alloc(64, "wgl")
    O_WSTAGE0 = b.alloc(256, "wstage_ab")
    O_BQKV = b.alloc(32, "bqkv")
    O_CBt = b.alloc(32, "cbt")
    O_CW = b.alloc(128, "cw")
    O_BOUT = b.alloc(16, "bout")
    O_BA = b.alloc(1, "ba")
    O_BB2 = b.alloc(1, "bb")
    O_DTB = b.alloc(1, "dtb")
    O_NEXPA = b.alloc(1, "nexpa")
    O_TAILM = b.alloc(1, "tailm")
    O_RC = b.alloc(1, "rcol")
    b.alloc_state()
    O_SHARED = b.alloc(3104, "shared")
    b.alloc_scan_ws()
    BIG0 = b._cur
    BIGSZ = 53200 - BIG0
    assert BIGSZ >= 27000, BIGSZ

    bc = B("const")
    b_shared = B("shared")
    b_xnscr = B("xn_scr")
    b_axscr = B("ax_scr")

    b.load_scan_consts(din)
    for dst, src in [
        (V(O_WSTAGE0, 128, 256), wa_t),
        (V(O_BQKV, 128, 32), bqkv_t),
        (V(O_CBt, 128, 32), cb_t), (V(O_CW, 128, 128), cw_t),
        (V(O_BOUT, 128, 16), bout_t), (V(O_BA, 16, 1), ba_c),
        (V(O_BB2, 16, 1), bb_c), (V(O_DTB, 16, 1), dtb),
        (V(O_NEXPA, 16, 1), alog), (V(O_TAILM, 128, 1), tailm),
    ]:
        b.dma(dst, src, writes=(bc,))

    def split_fl(o_src, o_hi, o_lo, cols, br, bw, src_parts=128):
        sv = V(o_src, src_parts, cols)
        hv_ = VR(o_hi, 0, src_parts, cols // 2).bitcast(BF16)
        lv = VR(o_lo, 0, src_parts, cols // 2).bitcast(BF16)
        S.op("scalar", lambda e: e.copy(hv_, sv), reads=(br,), writes=(bw,))
        b.vo(lambda e: e.tensor_tensor(lv, sv, hv_, ALU.subtract), reads=(br,), writes=(bw,))

    split_fl(O_WSTAGE0, O_WAH, O_WAL, 256, bc, bc)
    b.dma(V(O_WSTAGE0, 128, 256), wb_t, writes=(bc,))
    split_fl(O_WSTAGE0, O_WBH, O_WBL, 256, bc, bc)
    b.dma(VR(O_SHARED, 0, 128, 128), wgate, writes=(b_shared,))
    split_fl(O_SHARED, O_WGH, O_WGL, 128, b_shared, bc)

    S.op("scalar", lambda e: e.activation(V(O_NEXPA, 16, 1), V(O_NEXPA, 16, 1), AF.Exp),
         reads=(bc,), writes=(bc,))
    b.vo(lambda e: e.tensor_scalar(V(O_NEXPA, 16, 1), V(O_NEXPA, 16, 1), -1.0, None, ALU.mult),
         reads=(bc,), writes=(bc,))
    b.dma(V(b.O_NWBC, 1, 128), nw_row, writes=(B("nwtmp"),))
    b.mm(0, 128, V(b.O_ONESR, 1, 128), V(b.O_NWBC, 1, 128), True, True,
         reads=(bc, B("nwtmp")))
    b.vo(lambda e: e.tensor_copy(V(b.O_NWBC, 128, 128), PS[0][:, 0:128]),
         reads=(psb[0],), writes=(bc,))


    def emit_wload(o_slot, scr, r0, K, b_slot):
        b.dma(VR(o_slot, 0, 128, K).bitcast(mybir.dt.bfloat16),
              scr[r0:r0 + 128, 0:2 * K],
              reads=(B(("wscr", scr.tensor.name)),), writes=(b_slot,))

    def wall():
        for e in S.ENGINES:
            if S.pending_ph[e]:
                S.op(e, lambda en: en.nop(), reads=(), writes=(B(("wallnop", e)),))
        snap = tuple((s, v) for s, v in S.counts.items() if v > 0)
        wt = S.op("sync", lambda e: e.nop(), extra=snap)
        for bf in b.bufs.values():
            bf.writer = wt
            bf.readers = []
        for pb in psb:
            pb.writer = wt
            pb.readers = []

    def mm3(bank, n, whi, wlo, xhi, xlo, start, stop, reads=(), c0=0, parts=128):
        b.mm(bank, n, whi, xhi, start, False, reads=reads, c0=c0, parts=parts)
        b.mm(bank, n, whi, xlo, False, False, reads=reads, c0=c0, parts=parts)
        b.mm(bank, n, wlo, xhi, False, stop, reads=reads, c0=c0, parts=parts)

    def xsl(o_x, ncols, kc, part, t0, tn):
        base = o_x + kc * ncols + part * (ncols // 2)
        return b.AR[0:128, base:base + ncols // 2].bitcast(BF16)[:, t0:t0 + tn]

    def emit_ln(o_x, ncols, c0, n, o_sq, o_row, o_bc, bxk):
        assert n <= 512 and ncols % 2 == 0
        for kc in range(NKC):
            xa = V(o_x + kc * ncols + c0, 128, n)
            b.mm(6, n, V(b.O_ONESC, 128, 1), xa, kc == 0, kc == NKC - 1,
                 reads=(bc, bxk[kc]), parts=1)
        for kc in range(NKC):
            xa = V(o_x + kc * ncols + c0, 128, n)
            sq = V(o_sq + (kc % 2) * 512, 128, n)
            bsq = B(("lnsq", kc % 2))
            S.op("scalar", lambda e, sq=sq, xa=xa: e.activation(sq, xa, AF.Square),
                 reads=(bxk[kc],), writes=(bsq,))
            b.mm(7, n, V(b.O_ONESC, 128, 1), sq, kc == 0, kc == NKC - 1,
                 reads=(bc, bsq), parts=1)
        r_m = V(o_row, 1, n)
        r_v = V(o_row + 512, 1, n)
        r_t = V(o_row + 1024, 1, n)
        bm, bv, bt_ = B("lnm"), B("lnv"), B("lnt")
        b.vo(lambda e: e.tensor_scalar(r_m, PS[6][0:1, 0:n], 1.0 / HID, None, ALU.mult),
             reads=(psb[6],), writes=(bm,))
        b.vo(lambda e: e.tensor_scalar(r_v, PS[7][0:1, 0:n], 1.0 / HID, LN_EPS, ALU.mult, ALU.add),
             reads=(psb[7],), writes=(bv,))
        b.vo(lambda e: e.tensor_tensor(r_t, r_m, r_m, ALU.mult), reads=(bm,), writes=(bt_,))
        b.vo(lambda e: e.tensor_tensor(r_v, r_v, r_t, ALU.subtract), reads=(bt_,), writes=(bv,))
        b.vo(lambda e: e.reciprocal(r_v, r_v), reads=(), writes=(bv,))
        S.op("scalar", lambda e: e.activation(r_v, r_v, AF.Sqrt), reads=(), writes=(bv,))
        b.mm(6, n, V(b.O_ONESR, 1, 128), r_v, True, True, reads=(bc, bv))
        bc1 = B("lnc1")
        b.vo(lambda e: e.tensor_copy(V(o_bc, 128, n), PS[6][:, 0:n]),
             reads=(psb[6],), writes=(bc1,))
        b.mm(7, n, V(b.O_ONESR, 1, 128), r_m, True, True, reads=(bc, bm))
        bc0 = B("lnc0")
        b.vo(lambda e: e.tensor_copy(V(o_bc + 512, 128, n), PS[7][:, 0:n]),
             reads=(psb[7],), writes=(bc0,))
        for kc in range(NKC):
            xa = V(o_x + kc * ncols + c0, 128, n)
            t2 = V(o_sq + (kc % 2) * 512, 128, n)
            bsq = B(("lnsq", kc % 2))
            b.vo(lambda e, xa=xa: e.tensor_tensor(xa, xa, V(o_bc + 512, 128, n), ALU.subtract),
                 reads=(bc0,), writes=(bxk[kc],))
            b.vo(lambda e, xa=xa, t2=t2: e.tensor_tensor(t2, xa, V(o_bc, 128, n), ALU.mult),
                 reads=(bc1, bxk[kc]), writes=(bsq,))
            hi = bfv(o_x + kc * ncols, ncols // 2)[:, c0:c0 + n]
            lo = bfv(o_x + kc * ncols + ncols // 2, ncols // 2)[:, c0:c0 + n]
            S.op("scalar", lambda e, hi=hi, t2=t2: e.copy(hi, t2), reads=(bsq,), writes=(bxk[kc],))
            b.vo(lambda e, lo=lo, t2=t2, hi=hi: e.tensor_tensor(lo, t2, hi, ALU.subtract),
                 reads=(bsq,), writes=(bxk[kc],))

    def emit_wsplit(o_stage, o_slot, cols, wdram, r0, b_st, b_sl):
        b.dma(V(o_stage, 128, cols), wdram[r0:r0 + 128, 0:cols], writes=(b_st,))
        split_fl(o_stage, o_slot, o_slot + cols // 2, cols, b_st, b_sl)

    def emit_proj(o_dst, ncols_dst, bfts, gfts, o_xn, ncols_xn, c0, n, wsrc_scr,
                  o_wsl, b_wsl, bxk, b_p, o_pm=None, b_pm=None, pm_c0=0):
        assert n <= 512
        for i, (bft, gft) in enumerate(zip(bfts, gfts)):
            sl = i % 2
            emit_wload(o_wsl + sl * 2048, wsrc_scr, gft * 128, 2048, b_wsl[sl])
            bank = 6 if i % 2 == 0 else 7
            for kc in range(NKC):
                whi = bfv(o_wsl + sl * 2048 + kc * 64, 64)
                wlo = bfv(o_wsl + sl * 2048 + 1024 + kc * 64, 64)
                xhi = xsl(o_xn, ncols_xn, kc, 0, c0, n)
                xlo = xsl(o_xn, ncols_xn, kc, 1, c0, n)
                mm3(bank, n, whi, wlo, xhi, xlo, kc == 0, kc == NKC - 1,
                    reads=(b_wsl[sl], bxk[kc]))
            da = V(o_dst + bft * ncols_dst + c0, 128, n)
            bcol = V(O_BQKV + gft, 128, 1)
            if o_pm is not None:
                pmv = V(o_pm + pm_c0, 128, n)
                b.vo(lambda e, da=da, bank=bank, bcol=bcol, pmv=pmv:
                     e.scalar_tensor_tensor(da, PS[bank][:, 0:n], bcol, pmv, ALU.add, ALU.mult),
                     reads=(psb[bank], bc, b_pm), writes=(b_p,))
            else:
                eng = b.S.ev()
                if eng == "scalar":
                    b.vo(lambda e, da=da, bank=bank, bcol=bcol:
                         e.activation(da, PS[bank][:, 0:n], AF.Identity, bias=bcol),
                         reads=(psb[bank], bc), writes=(b_p,), eng="scalar")
                else:
                    b.vo(lambda e, da=da, bank=bank, bcol=bcol:
                         e.tensor_scalar(da, PS[bank][:, 0:n], bcol, None, ALU.add),
                         reads=(psb[bank], bc), writes=(b_p,))

    def emit_conv_silu(o_p, ncols, n, nft, gft0, o_cv, b_p):
        for ft in range(nft):
            gft = gft0 + ft
            acc = V(o_cv, 128, n)
            bacc = B("cvacc")
            for j in range(KCONV):
                src = V(o_p + ft * ncols + 1 + j, 128, n)
                cwc = V(O_CW + gft * 4 + j, 128, 1)
                if j == 0:
                    b.vo(lambda e, acc=acc, src=src, cwc=cwc: e.tensor_scalar(acc, src, cwc, None, ALU.mult),
                         reads=(b_p, bc), writes=(bacc,))
                else:
                    b.vo(lambda e, acc=acc, src=src, cwc=cwc:
                         e.scalar_tensor_tensor(acc, src, cwc, acc, ALU.mult, ALU.add),
                         reads=(b_p, bc), writes=(bacc,))
            dstv = V(o_p + ft * ncols + 4, 128, n)
            cbcol = V(O_CBt + gft, 128, 1)
            S.op("scalar", lambda e, dstv=dstv, acc=acc, cbcol=cbcol:
                 e.activation(dstv, acc, AF.Silu, bias=cbcol),
                 reads=(bacc, bc), writes=(b_p,))

    def emit_l2(o_p, ncols, n, bfts, is_q, o_sq, o_row, b_p):
        assert n <= 512
        for ft in bfts:
            base = V(o_p + ft * ncols + 4, 128, n)
            sq = V(o_sq, 128, n)
            bsq = B("l2sq")
            S.op("scalar", lambda e, sq=sq, base=base: e.activation(sq, base, AF.Square),
                 reads=(b_p,), writes=(bsq,))
            b.mm(6, n, V(b.O_ONESC, 128, 1), sq, True, True, reads=(bc, bsq), parts=1)
            rrow = V(o_row, 1, n)
            brow = B("l2row")
            b.vo(lambda e, rrow=rrow: e.tensor_scalar(rrow, PS[6][0:1, 0:n], L2_EPS, None, ALU.add),
                 reads=(psb[6],), writes=(brow,))
            b.vo(lambda e, rrow=rrow: e.reciprocal(rrow, rrow), reads=(), writes=(brow,))
            sc = (1.0 / DK) if is_q else 1.0
            S.op("scalar", lambda e, rrow=rrow, sc=sc: e.activation(rrow, rrow, AF.Sqrt, scale=sc),
                 reads=(), writes=(brow,))
            b.mm(7, n, V(b.O_ONESR, 1, 128), rrow, True, True, reads=(bc, brow))
            b.vo(lambda e, base=base: e.tensor_tensor(base, base, PS[7][:, 0:n], ALU.mult),
                 reads=(psb[7],), writes=(b_p,))

    def emit_gb(o_g, o_bt, o_gt, o_xn, ncols_xn, c0, n, bxk, b_gb, o_pm=None,
                b_pm=None, pm_c0=0):
        assert n <= 512
        for z_i, (o_dst, o_wh, o_wl, o_bias) in enumerate(
                [(o_g, O_WAH, O_WAL, O_BA), (o_bt, O_WBH, O_WBL, O_BB2)]):
            bank = 6 + z_i
            for kc in range(NKC):
                whi = bfv(o_wh + kc * 8, 8)
                wlo = bfv(o_wl + kc * 8, 8)
                xhi = xsl(o_xn, ncols_xn, kc, 0, c0, n)
                xlo = xsl(o_xn, ncols_xn, kc, 1, c0, n)
                mm3(bank, n, whi, wlo, xhi, xlo, kc == 0, kc == NKC - 1,
                    reads=(bc, bxk[kc]), parts=16)
            da = VR(o_dst, 0, 16, n)
            b.vo(lambda e, da=da, bank=bank, o_bias=o_bias:
                 e.tensor_scalar(da, PS[bank][0:16, 0:n], V(o_bias, 16, 1), None, ALU.add),
                 reads=(psb[bank], bc), writes=(b_gb,))
        ga = VR(o_g, 0, 16, n)
        bet = VR(o_bt, 0, 16, n)
        t1 = VR(o_gt, 0, 16, n)
        b1 = B("gbt1")
        S.op("scalar", lambda e: e.activation(bet, bet, AF.Sigmoid), reads=(), writes=(b_gb,))
        S.op("scalar", lambda e: e.activation(t1, ga, AF.Abs, bias=V(O_DTB, 16, 1)),
             reads=(b_gb, bc), writes=(b1,))
        S.op("scalar", lambda e: e.activation(t1, t1, AF.Exp, scale=-1.0), reads=(), writes=(b1,))
        S.op("scalar", lambda e: e.activation(t1, t1, AF.Ln, bias=1.0), reads=(), writes=(b1,))
        S.op("scalar", lambda e: e.activation(ga, ga, AF.Relu, bias=V(O_DTB, 16, 1)),
             reads=(bc,), writes=(b_gb,))
        b.vo(lambda e: e.tensor_tensor(ga, ga, t1, ALU.add), reads=(b1,), writes=(b_gb,))
        b.vo(lambda e: e.tensor_scalar(ga, ga, V(O_NEXPA, 16, 1), None, ALU.mult),
             reads=(bc,), writes=(b_gb,))
        if o_pm is not None:
            pmg = VR(o_pm + pm_c0, 0, 16, n)
            b.vo(lambda e: e.tensor_tensor(ga, ga, pmg, ALU.mult), reads=(b_pm,), writes=(b_gb,))
            b.vo(lambda e: e.tensor_tensor(bet, bet, pmg, ALU.mult), reads=(b_pm,), writes=(b_gb,))

    b._cur = BIG0
    o_xq = b.alloc(16 * PCOLS, "p_xq")
    o_sq = b.alloc(1024, "p_sq")
    o_row = b.alloc(1536, "p_row")
    o_bcb = b.alloc(1024, "p_bc")
    o_pkv = b.alloc(24 * PCOLS, "p_pkv")
    o_g = b.alloc(PTILE, "p_g")
    o_bt = b.alloc(PTILE, "p_bt")
    o_gt = b.alloc(PTILE, "p_gt")
    o_cv = b.alloc(584, "p_cv")
    o_pm = b.alloc(512, "p_pm")
    o_wsl = b.alloc(4096, "p_wsl")
    b_wsl = [B("wsl0"), B("wsl1")]
    b.dma(VR(O_SHARED, 0, 1, NPT * PCOLS), premask, writes=(b_shared,))

    bxk_p = [B(("px", kc)) for kc in range(NKC)]
    b.dma(V(o_xq, 128, 16 * PCOLS).rearrange("p (k t) -> p k t", k=16),
          xpre_re[:, :, 0:PCOLS], writes=tuple(bxk_p))
    emit_ln(o_xq, PCOLS, 4, 512, o_sq, o_row, o_bcb, bxk_p)

    o_pstage = o_pkv
    o_pslot = o_pkv + 8192
    b_pst = [B(f"pst{i}") for i in range(4)]
    b_psl = [B(f"psl{i}") for i in range(4)]
    psl_i = 0
    for scr, wdram, nft, K in [
        (wqkv_s, wqkv, CONV_DIM // 128, HID),
        (wz_s, wz, NKC, HID),
        (wout_s, wout_w, VAL_DIM // 128, HID),
        (wgup_s, wgup, NEXP * 22, HID),
        (wdwn_s, wdwn, NEXP * 16, FF),
        (wsgu_s, wsgu, 44, HID),
        (wsd_s, wsd, 16, FS),
    ]:
        npc = (K + 2047) // 2048
        w = K // npc
        scr_re = scr.rearrange("r (two k) -> r two k", two=2)
        for ft in range(nft):
            for pc in range(npc):
                c0 = pc * w
                sl = psl_i % 4
                psl_i += 1
                b.dma(V(o_pstage + sl * 2048, 128, w),
                      wdram[ft * 128:(ft + 1) * 128, c0:c0 + w], writes=(b_pst[sl],))
                split_fl(o_pstage + sl * 2048, o_pslot + sl * 2048,
                         o_pslot + sl * 2048 + w // 2, w, b_pst[sl], b_psl[sl])
                b.dma(scr_re[ft * 128:(ft + 1) * 128, :, c0:c0 + w],
                      VR(o_pslot + sl * 2048, 0, 128, w).bitcast(mybir.dt.bfloat16)
                      .rearrange("p (two k) -> p two k", two=2),
                      reads=(b_psl[sl],), writes=(B(("wscr", scr.tensor.name)),),
                      eng="gpsimd")
    wall()

    for pt in range(NPT):
        b_p = B("pp")
        b_gb = B("pgb")
        b_pm = B("ppm")
        if pt > 0:
            b.dma(V(o_xq, 128, 16 * PCOLS).rearrange("p (k t) -> p k t", k=16),
                  xpre_re[:, :, pt * PCOLS:(pt + 1) * PCOLS], writes=tuple(bxk_p))
        b.mm(6, 512, V(b.O_ONESR, 1, 128), VR(O_SHARED + pt * PCOLS + 4, 0, 1, 512),
             True, True, reads=(bc, b_shared))
        b.vo(lambda e: e.tensor_copy(V(o_pm, 128, 512), PS[6][:, 0:512]),
             reads=(psb[6],), writes=(b_pm,))
        if pt > 0:
            emit_ln(o_xq, PCOLS, 4, 512, o_sq, o_row, o_bcb, bxk_p)
        emit_proj(o_pkv, PCOLS, list(range(24)), list(range(8, 32)),
                  o_xq, PCOLS, 4, 512, wqkv_s, o_wsl, b_wsl, bxk_p, b_p,
                  o_pm=o_pm, b_pm=b_pm, pm_c0=0)
        hsl = V(o_pkv, 128, 24 * PCOLS).rearrange("p (f t) -> p f t", f=24)
        stash = V(o_cv + 512, 128, 24 * 3).rearrange("p (f t) -> p f t", f=24)
        if pt == 0:
            b.vo(lambda e: e.memset(hsl[:, :, 1:4], 0.0), reads=(), writes=(b_p,))
        else:
            b.vo(lambda e: e.tensor_copy(hsl[:, :, 1:4], stash),
                 reads=(B("cvhalo"),), writes=(b_p,))
        b.vo(lambda e: e.tensor_copy(stash, hsl[:, :, PCOLS - 3:PCOLS]),
             reads=(b_p,), writes=(B("cvhalo"),))
        emit_conv_silu(o_pkv, PCOLS, PTILE, 24, 8, o_cv, b_p)
        emit_l2(o_pkv, PCOLS, PTILE, list(range(8)), False, o_sq, o_row, b_p)
        emit_gb(o_g, o_bt, o_gt, o_xq, PCOLS, 4, PTILE, bxk_p, b_gb, o_pm=o_pm,
                b_pm=b_pm, pm_c0=0)

        def kqp(kh):
            return (V(o_pkv + kh * PCOLS + 4 + kqp.c * CH, 128, 128), None)

        def vsp(h):
            return V(o_pkv + (8 + h) * PCOLS + 4 + vsp.c * CH, 128, 128)

        ws1b = tuple(B(k) for k in (("xsolve", 1), ("pa", 1), ("pb", 1),
                                    ("ubar", 1), ("wfm", 1), ("vtm", 1)))
        S.op("sync", lambda e: e.nop(), writes=ws1b + (b_wsl[0], b_wsl[1]))
        for ci in range(PTILE // CH):
            kqp.c = ci
            vsp.c = ci
            b.emit_chunk(kqp, vsp, o_g, o_bt, ci * CH, b_p, b_gb, False,
                         o_alt=o_wsl)
        S.op("sync", lambda e: e.nop(), writes=ws1b + (b_wsl[0], b_wsl[1]))

    if debug:
        for h in range(HV):
            b.dma(d_spre[:, h * DK:(h + 1) * DK], V(b.O_S + h * DK, 128, DK),
                  reads=(b.b_S[h],))

    if "span" not in phases:
        fence = [(s, v) for s, v in S.counts.items() if "_d" in s and v > 0]
        S.op("sync", lambda e: e.nop(), extra=tuple(fence))
        S.emit(nc)
        return nc
    wall()
    b._cur = BIG0
    o_xq2 = b.alloc(16 * QCOLS, "s_xq")
    o_p2 = b.alloc(32 * QCOLS, "s_p")
    o_z = b.alloc(2 * VAL_DIM, "s_z")
    o_g2 = b.alloc(QTOK, "s_g")
    o_bt2 = b.alloc(QTOK, "s_bt")
    o_gt2 = b.alloc(QTOK, "s_gt")
    o_sq2 = b.alloc(1024, "s_sq")
    o_row2 = b.alloc(1536, "s_row")
    o_bc2 = b.alloc(1024, "s_bc")
    o_cv2 = b.alloc(584, "s_cv")
    o_gh = b.alloc(16 * 128, "s_gatedh")
    o_gl = b.alloc(16 * 128, "s_gatedl")
    o_wsl2 = b.alloc(4096, "s_wsl")
    b_wsl = [B("wsl0"), B("wsl1")]
    o_ax = o_xq2
    o_xres = o_z

    b.dma(VR(O_SHARED, 0, 1, VAL_DIM), bz_row, writes=(b_shared,))
    for fb in range(4):
        b.mm(6, 512, V(b.O_ONESR, 1, 128), VR(O_SHARED + fb * 512, 0, 1, 512),
             True, True, reads=(bc, b_shared))
        b.vo(lambda e, fb=fb: e.tensor_copy(V(O_SHARED + 1024 + fb * 512, 128, 512), PS[6][:, 0:512]),
             reads=(psb[6],), writes=(b_shared,))
    O_BZBC = O_SHARED + 1024

    bxk_s = [B(("sx", kc)) for kc in range(NKC)]
    for q in range(NQ):
        b_p = B("sp")
        b_gb = B("sgb")
        b_z = B("sz")
        b_gt = B("sgt")
        b_xr = B("sz")
        b.dma(V(o_xq2, 128, 16 * QCOLS).rearrange("p (k t) -> p k t", k=16),
              xspan_re[:, :, q * QCOLS:(q + 1) * QCOLS], writes=tuple(bxk_s))
        emit_ln(o_xq2, QCOLS, 0, QCOLS, o_sq2, o_row2, o_bc2, bxk_s)
        b.dma(xnh_re[:, :, q * QTOK:(q + 1) * QTOK],
              V(o_xq2, 128, 16 * QCOLS).rearrange("p (k t) -> p k t", k=16)
              [:, :, 0:QCOLS // 2].bitcast(BF16)[:, :, 4:QCOLS],
              reads=tuple(bxk_s), writes=(b_xnscr,))
        b.dma(xnl_re[:, :, q * QTOK:(q + 1) * QTOK],
              V(o_xq2, 128, 16 * QCOLS).rearrange("p (k t) -> p k t", k=16)
              [:, :, QCOLS // 2:QCOLS].bitcast(BF16)[:, :, 4:QCOLS],
              reads=tuple(bxk_s), writes=(b_xnscr,))
        emit_proj(o_p2, QCOLS, list(range(32)), list(range(32)),
                  o_xq2, QCOLS, 1, 259, wqkv_s, o_wsl2, b_wsl, bxk_s, b_p)
        if q == 0:
            b.vo(lambda e: e.tensor_scalar(
                V(o_p2, 128, 32 * QCOLS).rearrange("p (f t) -> p f t", f=32)[:, :, 1:4],
                V(o_p2, 128, 32 * QCOLS).rearrange("p (f t) -> p f t", f=32)[:, :, 1:4],
                V(O_TAILM, 128, 1), None, ALU.mult),
                reads=(bc,), writes=(b_p,))
        emit_conv_silu(o_p2, QCOLS, QTOK, 32, 0, o_cv2, b_p)
        emit_l2(o_p2, QCOLS, QTOK, list(range(8)), True, o_sq2, o_row2, b_p)
        emit_l2(o_p2, QCOLS, QTOK, list(range(8, 16)), False, o_sq2, o_row2, b_p)
        emit_gb(o_g2, o_bt2, o_gt2, o_xq2, QCOLS, 4, QTOK, bxk_s, b_gb)
        for kc in range(NKC):
            sl = kc % 2
            emit_wload(o_wsl2 + sl * 2048, wz_s, kc * 128, 2048, b_wsl[sl])
            for tc in range(2):
                lh = xsl(o_xq2, QCOLS, kc, 0, 4 + tc * 128, 128)
                ll = xsl(o_xq2, QCOLS, kc, 1, 4 + tc * 128, 128)
                for fb in range(4):
                    whi = bfv(o_wsl2 + sl * 2048 + fb * 256, 256)
                    wlo = bfv(o_wsl2 + sl * 2048 + 1024 + fb * 256, 256)
                    mm3(tc * 4 + fb, 512, lh, ll, whi, wlo, kc == 0, kc == NKC - 1,
                        reads=(b_wsl[sl], bxk_s[kc]))
        for tc in range(2):
            for fb in range(4):
                bank = tc * 4 + fb
                da = V(o_z + tc * VAL_DIM + fb * 512, 128, 512)
                b.vo(lambda e, da=da, bank=bank, fb=fb: e.tensor_tensor(
                    da, PS[bank][:, 0:512], V(O_BZBC + fb * 512, 128, 512), ALU.add),
                    reads=(psb[bank], b_shared), writes=(b_z,))

        def kqs2(kh):
            return (V(o_p2 + (8 + kh) * QCOLS + 4 + kqs2.c * CH, 128, 128),
                    V(o_p2 + kh * QCOLS + 4 + kqs2.c * CH, 128, 128))

        def vss(h):
            return V(o_p2 + (16 + h) * QCOLS + 4 + vss.c * CH, 128, 128)

        ws1b = tuple(B(k) for k in (("xsolve", 1), ("pa", 1), ("pb", 1),
                                    ("ubar", 1), ("wfm", 1), ("vtm", 1)))
        S.op("sync", lambda e: e.nop(), writes=ws1b + (b_wsl[0], b_wsl[1]))
        for ci in range(QTOK // CH):
            kqs2.c = ci
            vss.c = ci

            def sink(h, cv, reads, ci=ci, b_z=b_z, b_gt=b_gt):
                rc = V(O_RC, 128, 1)
                brc = B("rcol")
                csq = V(b.O_ZS, 128, 128)
                bzs = B("zsil")
                S.op("scalar", lambda e, cv=cv: e.activation(csq, cv, AF.Square),
                     reads=tuple(reads), writes=(bzs,))
                b.vo(lambda e: e.tensor_reduce(rc, csq, AX.X, ALU.add),
                     reads=(bzs,), writes=(brc,))
                b.vo(lambda e: e.tensor_scalar(rc, rc, 1.0 / DV, RMS_EPS, ALU.mult, ALU.add),
                     reads=(), writes=(brc,))
                b.vo(lambda e: e.reciprocal(rc, rc), reads=(), writes=(brc,))
                S.op("scalar", lambda e: e.activation(rc, rc, AF.Sqrt), reads=(), writes=(brc,))
                b.vo(lambda e, cv=cv: e.tensor_scalar(cv, cv, rc, None, ALU.mult),
                     reads=(brc,), writes=tuple(reads))
                b.vo(lambda e, cv=cv: e.tensor_tensor(cv, cv, V(b.O_NWBC, 128, 128), ALU.mult),
                     reads=(bc,), writes=tuple(reads))
                zs = V(b.O_ZS, 128, 128)
                zsl = V(o_z + ci * VAL_DIM + h * 128, 128, 128)
                S.op("scalar", lambda e, zsl=zsl: e.activation(zs, zsl, AF.Silu),
                     reads=(b_z,), writes=(bzs,))
                b.vo(lambda e, cv=cv: e.tensor_tensor(cv, cv, zs, ALU.mult),
                     reads=(bzs,), writes=tuple(reads))
                b.trans(5, 128, 128, cv, V(b.O_ID128, 128, 128), reads=tuple(reads))
                hi = bfv(o_gh + h * 128, 128)[:, ci * 128:ci * 128 + 128]
                lo = bfv(o_gl + h * 128, 128)[:, ci * 128:ci * 128 + 128]
                S.op("scalar", lambda e, hi=hi: e.copy(hi, PS[5][:, 0:128]),
                     reads=(psb[5],), writes=(b_gt,))
                b.vo(lambda e, lo=lo, hi=hi: e.tensor_tensor(lo, PS[5][:, 0:128], hi, ALU.subtract),
                     reads=(psb[5],), writes=(b_gt,))

            b.emit_chunk(kqs2, vss, o_g2, o_bt2, ci * CH, b_p, b_gb, True, sink,
                         o_alt=o_wsl2)

        S.op("sync", lambda e: e.nop(), writes=ws1b + (b_wsl[0], b_wsl[1]))
        b.dma(V(o_xres, 128, 16 * QTOK).rearrange("p (k t) -> p k t", k=16),
              xspan_re[:, :, q * QCOLS + 4:(q + 1) * QCOLS], writes=(b_xr,))
        for fo in range(16):
            sl = fo % 2
            emit_wload(o_wsl2 + sl * 2048, wout_s, fo * 128, 2048, b_wsl[sl])
            bank = 6 if fo % 2 == 0 else 7
            for kc in range(NKC):
                whi = bfv(o_wsl2 + sl * 2048 + kc * 64, 64)
                wlo = bfv(o_wsl2 + sl * 2048 + 1024 + kc * 64, 64)
                ghi = bfv(o_gh + kc * 128, 128)
                glo = bfv(o_gl + kc * 128, 128)
                mm3(bank, QTOK, whi, wlo, ghi, glo, kc == 0, kc == NKC - 1,
                    reads=(b_wsl[sl], b_gt))
            da = V(o_ax + fo * QCOLS, 128, QTOK)
            b.vo(lambda e, da=da, bank=bank, fo=fo: e.scalar_tensor_tensor(
                da, PS[bank][:, 0:QTOK], V(O_BOUT + fo, 128, 1),
                V(o_xres + fo * QTOK, 128, QTOK), ALU.add, ALU.add),
                reads=(psb[bank], bc, b_xr), writes=(bxk_s[fo],))
        b.dma(ax_re[:, :, q * QTOK:(q + 1) * QTOK],
              V(o_ax, 128, 16 * QCOLS).rearrange("p (k t) -> p k t", k=16)[:, :, 0:QTOK],
              reads=tuple(bxk_s), writes=(b_axscr,))

    if debug:
        b.dma(d_ax, ax_scr, reads=(b_axscr,))

    if "ffn" not in phases:
        fence = [(s, v) for s, v in S.counts.items() if "_d" in s and v > 0]
        S.op("sync", lambda e: e.nop(), extra=tuple(fence))
        S.emit(nc)
        return nc
    wall()
    b._cur = BIG0
    o_fxn = b.alloc(16 * 512, "f_xn")
    o_acc = b.alloc(16 * 512, "f_acc")
    o_act = b.alloc(11 * 512, "f_act")
    o_fsc = b.alloc(512, "f_scr")
    o_cmb = b.alloc(512, "f_cmb")
    o_cb = b.alloc(512, "f_cb")
    o_gs = b.alloc(8, "f_gs")
    o_gs2 = b.alloc(8, "f_gs2")
    o_gcl = b.alloc(4, "f_gcol")
    o_bd8 = b.alloc(512, "f_bd8")
    o_wsl3 = b.alloc(8192, "f_wsl")
    b_wsl = [B("wsl0"), B("wsl1"), B("wsl2"), B("wsl3")]

    def fxn_sl(kc, part, t0, tn):
        base = o_fxn + kc * 512 + part * 256
        return b.AR[0:128, base:base + 256].bitcast(BF16)[:, t0:t0 + tn]

    for half in range(2):
        b_x = B("fx")
        b_a = B("fa")
        b_cmb = B("fcmb")
        b_st = B("wstage")
        b.dma(V(o_fxn, 128, 16 * 512).rearrange("p (k t) -> p k t", k=16)
              [:, :, 0:256].bitcast(BF16),
              xnh_re[:, :, half * 512:(half + 1) * 512], reads=(b_xnscr,), writes=(b_x,))
        b.dma(V(o_fxn, 128, 16 * 512).rearrange("p (k t) -> p k t", k=16)
              [:, :, 256:512].bitcast(BF16),
              xnl_re[:, :, half * 512:(half + 1) * 512], reads=(b_xnscr,), writes=(b_x,))
        b.dma(V(o_acc, 128, 16 * 512).rearrange("p (k t) -> p k t", k=16),
              ax_re[:, :, half * 512:(half + 1) * 512], reads=(b_axscr,), writes=(b_a,))
        for tc in range(4):
            for kc in range(NKC):
                lh = fxn_sl(kc, 0, tc * 128, 128)
                ll = fxn_sl(kc, 1, tc * 128, 128)
                wgh = bfv(O_WGH + kc * 4, 4)
                wgl = bfv(O_WGL + kc * 4, 4)
                mm3(6, 8, lh, ll, wgh, wgl, kc == 0, kc == NKC - 1, reads=(bc, b_x))
            sc_ = V(o_gs, 128, 8)
            mc = V(o_gcl, 128, 1)
            m2c = V(o_gcl + 1, 128, 1)
            rs_ = V(o_gcl + 2, 128, 1)
            bsc, bmc = B("fsc"), B("fmc")
            b.vo(lambda e: e.tensor_reduce(mc, PS[6][:, 0:8], AX.X, ALU.max),
                 reads=(psb[6],), writes=(bmc,))
            b.vo(lambda e: e.tensor_scalar(mc, mc, -1.0, None, ALU.mult), reads=(), writes=(bmc,))
            S.op("scalar", lambda e: e.activation(sc_, PS[6][:, 0:8], AF.Exp, bias=mc),
                 reads=(psb[6], bmc), writes=(bsc,))
            b.vo(lambda e: e.tensor_reduce(rs_, sc_, AX.X, ALU.add), reads=(bsc,), writes=(bmc,))
            b.vo(lambda e: e.reciprocal(rs_, rs_), reads=(), writes=(bmc,))
            b.vo(lambda e: e.tensor_scalar(sc_, sc_, rs_, None, ALU.mult),
                 reads=(bmc,), writes=(bsc,))
            b.vo(lambda e: e.tensor_reduce(mc, sc_, AX.X, ALU.max), reads=(bsc,), writes=(bmc,))
            ge1 = V(o_gs2, 128, 8)
            bge = B("fge")
            b.vo(lambda e: e.tensor_scalar(ge1, sc_, mc, None, ALU.is_ge),
                 reads=(bsc, bmc), writes=(bge,))
            b.vo(lambda e: e.scalar_tensor_tensor(ge1, ge1, -1e9, sc_, ALU.mult, ALU.add),
                 reads=(bsc,), writes=(bge,))
            b.vo(lambda e: e.tensor_reduce(m2c, ge1, AX.X, ALU.max), reads=(bge,), writes=(bmc,))
            b.vo(lambda e: e.tensor_scalar(ge1, sc_, m2c, None, ALU.is_ge),
                 reads=(bsc, bmc), writes=(bge,))
            b.vo(lambda e: e.tensor_tensor(ge1, ge1, sc_, ALU.mult), reads=(bsc,), writes=(bge,))
            b.trans(5, 8, 128, ge1, V(b.O_ID128, 128, 128), reads=(bge,))
            b.vo(lambda e, tc=tc: e.tensor_copy(VR(o_cmb + tc * 128, 0, 8, 128), PS[5][0:8, 0:128]),
                 reads=(psb[5],), writes=(b_cmb,))
        for ei in range(NEXP):
            bcb = B("fcb")
            bd8 = VR(o_bd8, 0, 8, 512)
            b.vo(lambda e, ei=ei: e.tensor_scalar(bd8, VR(o_cmb, 0, 8, 512),
                                                  V(b.O_ID8 + ei, 8, 1), None, ALU.mult),
                 reads=(b_cmb, bc), writes=(B("fbd8"),))
            b.mm(6, 512, V(b.O_ONES8, 8, 128), bd8, True, True, reads=(B("fbd8"), bc))
            b.vo(lambda e: e.tensor_copy(V(o_cb, 128, 512), PS[6][:, 0:512]),
                 reads=(psb[6],), writes=(bcb,))
            bact = B("fact")
            for fp in range(11):
                r0 = (ei * 22 + fp) * 128
                r1 = (ei * 22 + fp + 11) * 128
                s0 = o_wsl3 + (fp % 2) * 4096
                w0, w1 = b_wsl[(fp % 2) * 2], b_wsl[(fp % 2) * 2 + 1]
                emit_wload(s0, wgup_s, r0, 2048, w0)
                emit_wload(s0 + 2048, wgup_s, r1, 2048, w1)
                for kc in range(NKC):
                    xh = fxn_sl(kc, 0, 0, 512)
                    xlo = fxn_sl(kc, 1, 0, 512)
                    mm3(0, 512, bfv(s0 + kc * 64, 64), bfv(s0 + 1024 + kc * 64, 64),
                        xh, xlo, kc == 0, kc == NKC - 1, reads=(w0, b_x))
                for kc in range(NKC):
                    xh = fxn_sl(kc, 0, 0, 512)
                    xlo = fxn_sl(kc, 1, 0, 512)
                    mm3(1, 512, bfv(s0 + 2048 + kc * 64, 64), bfv(s0 + 3072 + kc * 64, 64),
                        xh, xlo, kc == 0, kc == NKC - 1, reads=(w1, b_x))
                fs_ = V(o_fsc, 128, 512)
                bfs = B("ffs")
                S.op("scalar", lambda e, fs_=fs_: e.activation(fs_, PS[0][:, 0:512], AF.Silu),
                     reads=(psb[0],), writes=(bfs,))
                b.vo(lambda e, fs_=fs_: e.tensor_tensor(fs_, fs_, PS[1][:, 0:512], ALU.mult),
                     reads=(psb[1],), writes=(bfs,))
                b.vo(lambda e, fs_=fs_: e.tensor_tensor(fs_, fs_, V(o_cb, 128, 512), ALU.mult),
                     reads=(bcb,), writes=(bfs,))
                ahi = bfv(o_act + fp * 512, 256)
                alo_ = bfv(o_act + fp * 512 + 256, 256)
                S.op("scalar", lambda e, ahi=ahi, fs_=fs_: e.copy(ahi, fs_),
                     reads=(bfs,), writes=(bact,))
                b.vo(lambda e, alo_=alo_, fs_=fs_, ahi=ahi: e.tensor_tensor(alo_, fs_, ahi, ALU.subtract),
                     reads=(bfs,), writes=(bact,))
            for fo in range(16):
                sl = fo % 4
                r0 = (ei * 16 + fo) * 128
                emit_wload(o_wsl3 + sl * 2048, wdwn_s, r0, FF, b_wsl[sl])
                bank = 2 + (fo % 2)
                for kc in range(11):
                    whi = bfv(o_wsl3 + sl * 2048 + kc * 64, 64)
                    wlo = bfv(o_wsl3 + sl * 2048 + 704 + kc * 64, 64)
                    ahi = bfv(o_act + kc * 512, 256)
                    alo_ = bfv(o_act + kc * 512 + 256, 256)
                    mm3(bank, 512, whi, wlo, ahi, alo_, kc == 0, kc == 10,
                        reads=(b_wsl[sl], bact))
                av = V(o_acc + fo * 512, 128, 512)
                b.vo(lambda e, av=av, bank=bank: e.tensor_tensor(av, av, PS[bank][:, 0:512], ALU.add),
                     reads=(psb[bank],), writes=(b_a,))
        for st in range(2):
            bact = B("fact")
            for fp in range(11):
                ft = st * 11 + fp
                s0 = o_wsl3 + (ft % 2) * 4096
                w0, w1 = b_wsl[(ft % 2) * 2], b_wsl[(ft % 2) * 2 + 1]
                emit_wload(s0, wsgu_s, ft * 128, 2048, w0)
                emit_wload(s0 + 2048, wsgu_s, (ft + 22) * 128, 2048, w1)
                for kc in range(NKC):
                    xh = fxn_sl(kc, 0, 0, 512)
                    xlo = fxn_sl(kc, 1, 0, 512)
                    mm3(0, 512, bfv(s0 + kc * 64, 64), bfv(s0 + 1024 + kc * 64, 64),
                        xh, xlo, kc == 0, kc == NKC - 1, reads=(w0, b_x))
                for kc in range(NKC):
                    xh = fxn_sl(kc, 0, 0, 512)
                    xlo = fxn_sl(kc, 1, 0, 512)
                    mm3(1, 512, bfv(s0 + 2048 + kc * 64, 64), bfv(s0 + 3072 + kc * 64, 64),
                        xh, xlo, kc == 0, kc == NKC - 1, reads=(w1, b_x))
                fs_ = V(o_fsc, 128, 512)
                bfs = B("ffs")
                S.op("scalar", lambda e, fs_=fs_: e.activation(fs_, PS[0][:, 0:512], AF.Silu),
                     reads=(psb[0],), writes=(bfs,))
                b.vo(lambda e, fs_=fs_: e.tensor_tensor(fs_, fs_, PS[1][:, 0:512], ALU.mult),
                     reads=(psb[1],), writes=(bfs,))
                ahi = bfv(o_act + fp * 512, 256)
                alo_ = bfv(o_act + fp * 512 + 256, 256)
                S.op("scalar", lambda e, ahi=ahi, fs_=fs_: e.copy(ahi, fs_),
                     reads=(bfs,), writes=(bact,))
                b.vo(lambda e, alo_=alo_, fs_=fs_, ahi=ahi: e.tensor_tensor(alo_, fs_, ahi, ALU.subtract),
                     reads=(bfs,), writes=(bact,))
            for fo in range(16):
                sl = fo % 4
                b.dma(VR(o_wsl3 + sl * 2048, 0, 128, FF).bitcast(mybir.dt.bfloat16)
                      .rearrange("p (h t) -> p h t", h=2),
                      wsd_s[fo * 128:(fo + 1) * 128, :].rearrange("p (h s) -> p h s", h=2)
                      [:, :, st * FF:(st + 1) * FF],
                      reads=(B(("wscr", "wsd_s")),), writes=(b_wsl[sl],))
                bank = 2 + (fo % 2)
                for kc in range(11):
                    whi = bfv(o_wsl3 + sl * 2048 + kc * 64, 64)
                    wlo = bfv(o_wsl3 + sl * 2048 + 704 + kc * 64, 64)
                    ahi = bfv(o_act + kc * 512, 256)
                    alo_ = bfv(o_act + kc * 512 + 256, 256)
                    mm3(bank, 512, whi, wlo, ahi, alo_, kc == 0, kc == 10,
                        reads=(b_wsl[sl], bact))
                av = V(o_acc + fo * 512, 128, 512)
                b.vo(lambda e, av=av, bank=bank: e.tensor_tensor(av, av, PS[bank][:, 0:512], ALU.add),
                     reads=(psb[bank],), writes=(b_a,))
        b.dma(out_re[:, :, half * 512:(half + 1) * 512],
              V(o_acc, 128, 16 * 512).rearrange("p (k t) -> p k t", k=16),
              reads=(b_a,), writes=(B("fout"),))

    fence = []
    for s, v in S.counts.items():
        if "_d" in s and v > 0:
            fence.append((s, v))
    S.op("sync", lambda e: e.nop(), extra=tuple(fence))
    S.emit(nc)
    return nc



def _tile_lhsT(W, K=HID):
    F = W.shape[0]
    nft = F // 128
    nkc = K // 128
    H = W.reshape(nft, 128, nkc, 128).transpose(0, 3, 2, 1).reshape(nft * 128, K)
    return np.ascontiguousarray(H)


def _prep_weights(inp):
    f32 = np.float32
    g = lambda k: np.asarray(inp[k], dtype=f32)
    W = {}
    W["wqkv"] = _tile_lhsT(g("L0_attn_Wqkv"))
    W["wz"] = np.ascontiguousarray(g("L0_attn_Wz").T.reshape(NKC * 128, HID))
    W["wout_w"] = _tile_lhsT(g("L0_attn_Wout"), K=VAL_DIM)
    gup = g("L0_ffn_gate_up_weights")
    W["wgup"] = np.concatenate([_tile_lhsT(gup[e]) for e in range(NEXP)], 0)
    dwn = g("L0_ffn_down_weights")
    W["wdwn"] = np.concatenate([_tile_lhsT(dwn[e], K=FF) for e in range(NEXP)], 0)
    W["wsgu"] = _tile_lhsT(g("L0_ffn_shared_gate_up"))
    W["wsd"] = _tile_lhsT(g("L0_ffn_shared_down"), K=FS)
    W["wa_t"] = np.ascontiguousarray(
        g("L0_attn_Wa").T.reshape(NKC, 128, 16).transpose(1, 0, 2).reshape(128, NKC * 16))
    W["wb_t"] = np.ascontiguousarray(
        g("L0_attn_Wb").T.reshape(NKC, 128, 16).transpose(1, 0, 2).reshape(128, NKC * 16))
    W["wgate"] = np.ascontiguousarray(
        g("L0_ffn_gate_weight").T.reshape(NKC, 128, 8).transpose(1, 0, 2).reshape(128, NKC * 8))
    W["bqkv_t"] = np.ascontiguousarray(g("L0_attn_bqkv").reshape(32, 128).T)
    W["cb_t"] = np.ascontiguousarray(g("L0_attn_conv_b").reshape(32, 128).T)
    W["cw_t"] = np.ascontiguousarray(
        g("L0_attn_conv_w").reshape(32, 128, 4).transpose(1, 0, 2).reshape(128, 128))
    W["bz_row"] = g("L0_attn_bz")[None, :]
    W["ba_c"] = g("L0_attn_ba")[:, None]
    W["bb_c"] = g("L0_attn_bb")[:, None]
    W["dtb"] = g("L0_attn_dt_bias")[:, None]
    W["alog"] = g("L0_attn_A_log")[:, None]
    W["nw_row"] = g("L0_attn_norm_w")[None, :]
    W["bout_t"] = np.ascontiguousarray(g("L0_attn_bout").reshape(16, 128).T)
    W["ones_row"] = np.ones((1, 128), f32)
    W["ones_col"] = np.ones((128, 1), f32)
    W["ones8"] = np.ones((8, 128), f32)
    W["id8"] = np.eye(8, dtype=f32)
    W["id16"] = np.eye(16, dtype=f32)
    W["id32"] = np.eye(32, dtype=f32)
    W["id128"] = np.eye(128, dtype=f32)
    W["blk8"] = np.repeat(np.eye(8, dtype=f32), 128, axis=1)
    W["mask_su"] = np.triu(np.ones((128, 128), f32), 1)
    W["mask_sl"] = np.tril(np.ones((128, 128), f32), -1)
    W["mask_iu"] = np.triu(np.ones((128, 128), f32), 0)
    return W


def _prep_core(x, bidx, t0):
    f32 = np.float32
    xb = np.asarray(x[bidx], dtype=f32)
    xpre = np.zeros((NPT * PCOLS, HID), f32)
    premask = np.zeros((1, NPT * PCOLS), f32)
    start0 = t0 - PREF
    for pt in range(NPT):
        ts = start0 + pt * PTILE
        for j in range(PCOLS):
            tok = ts - 3 + (j - 1) + 1 if False else ts + (j - 4)
            if j == 0:
                continue
            tok = ts + (j - 4)
            if 0 <= tok < 4096:
                xpre[pt * PCOLS + j] = xb[tok]
                premask[0, pt * PCOLS + j] = 1.0
    xspan = np.zeros((NQ * QCOLS, HID), f32)
    for qq in range(NQ):
        ts = t0 + qq * QTOK
        for j in range(1, QCOLS):
            tok = ts + (j - 4)
            if 0 <= tok < 4096:
                xspan[qq * QCOLS + j] = xb[tok]
    tailm = np.full((128, 1), 0.0 if t0 == 0 else 1.0, f32)
    return {
        "xpre": np.ascontiguousarray(xpre.T),
        "premask": premask,
        "xspan": np.ascontiguousarray(xspan.T),
        "tailm": tailm,
    }


_NC_CACHE = {}


def get_nc(debug=False):
    if debug not in _NC_CACHE:
        _NC_CACHE[debug] = build_full(debug)
    return _NC_CACHE[debug]


def kernel(**inputs):
    x = np.asarray(inputs["L0_attn_hidden_states"], dtype=np.float32)
    W = _prep_weights(inputs)
    nc = get_nc(False)
    in_maps = []
    for c in range(NCORES):
        bidx, t0 = c // 4, (c % 4) * SPAN
        m = dict(W)
        m.update(_prep_core(x, bidx, t0))
        in_maps.append(m)
    res = run_bass_kernel_spmd(nc, in_maps, core_ids=list(range(NCORES)))
    outf = np.empty((2, 4096, HID), np.float32)
    for c in range(NCORES):
        bidx, t0 = c // 4, (c % 4) * SPAN
        outf[bidx, t0:t0 + SPAN] = res.results[c]["out"].T
    return outf


# version 10
# speedup vs baseline: 1.1489x; 1.0380x over previous
import os
from collections import defaultdict

import numpy as np
import concourse.bass as bass
import concourse.mybir as mybir
from concourse.bass_utils import run_bass_kernel_spmd

F32 = mybir.dt.float32
AF = mybir.ActivationFunctionType
ALU = mybir.AluOpType
AX = mybir.AxisListType

HID = 2048
HK = 8
HV = 16
DK = 128
DV = 128
KCONV = 4
KEY_DIM = HK * DK
VAL_DIM = HV * DV
CONV_DIM = 2 * KEY_DIM + VAL_DIM
RMS_EPS = 1e-6
L2_EPS = 1e-6
LN_EPS = 1e-5
NEXP = 8
FF = 1408
FS = 2816
NKC = HID // 128

SPAN = 1024
PREF = 3072
PTILE = 512
NPT = PREF // PTILE
PCOLS = PTILE + 4
QTOK = 256
NQ = SPAN // QTOK
QCOLS = QTOK + 4
CH = 128

NCORES = 8



class Buf:
    __slots__ = ("writer", "readers", "name")
    def __init__(self, name=""):
        self.writer = None
        self.readers = []
        self.name = name


class PH:
    __slots__ = ("eng", "tok")
    def __init__(self, eng):
        self.eng = eng
        self.tok = None


class Sched:
    ENGINES = ("tensor", "vector", "scalar", "gpsimd", "sync")
    KSEM = 4

    def __init__(self):
        self.ops = {e: [] for e in self.ENGINES}
        self.counts = defaultdict(int)
        self.opidx = defaultdict(int)
        self.waiters = defaultdict(list)
        self.pending_ph = {e: [] for e in self.ENGINES}
        self._rr = 0

    def _force_token(self, eng):
        return self.op(eng, lambda e: e.nop(), reads=(), writes=(), dma=False,
                       extra=(), quiet=False, _force=True)

    def _resolve(self, d, eng):
        if not isinstance(d, PH):
            return d
        if d.tok is not None:
            return d.tok
        if d.eng == eng:
            return None
        return self._force_token(d.eng)

    def op(self, eng, fn, reads=(), writes=(), dma=False, extra=(), quiet=False,
           _force=False):
        deps = []
        for d in extra:
            if d is None:
                continue
            d = self._resolve(d, eng)
            if d is not None:
                deps.append(d)
        for b in reads:
            w = self._resolve(b.writer, eng) if b.writer is not None else None
            if w is not None:
                deps.append(w)
        for b in writes:
            w = self._resolve(b.writer, eng) if b.writer is not None else None
            if w is not None:
                deps.append(w)
            for r in list(b.readers):
                r = self._resolve(r, eng)
                if r is not None:
                    deps.append(r)
        if quiet and not writes and not dma:
            ph = PH(eng)
            self.ops[eng].append((fn, deps, None, dma))
            for (s, v) in deps:
                self.waiters[s].append((v, ph))
            for b in reads:
                b.readers.append(ph)
            self.pending_ph[eng].append(ph)
            return ph
        kind = "d" if dma else "c"
        idx = self.opidx[(eng, kind)]
        self.opidx[(eng, kind)] = idx + 1
        sem = f"{eng}_{kind}{idx % self.KSEM}"
        inc = 16 if dma else 1
        start_val = self.counts[sem]
        wl = self.waiters[sem]
        if wl:
            keep = []
            for (v, wtok) in wl:
                if v == start_val:
                    if isinstance(wtok, PH):
                        if wtok.tok is not None:
                            deps.append(wtok.tok)
                        elif wtok.eng != eng:
                            raise AssertionError("unresolved waiter placeholder")
                    else:
                        deps.append(wtok)
                elif v > start_val:
                    keep.append((v, wtok))
            self.waiters[sem] = keep
        self.counts[sem] = start_val + inc
        tok = (sem, start_val + inc)
        self.ops[eng].append((fn, deps, tok, dma))
        for (s, v) in deps:
            self.waiters[s].append((v, tok))
        for ph in self.pending_ph[eng]:
            ph.tok = tok
        self.pending_ph[eng].clear()
        for b in reads:
            b.readers.append(tok)
        for b in writes:
            b.writer = tok
            b.readers = []
        return tok

    def ev(self):
        self._rr += 1
        return "vector" if self._rr % 2 else "scalar"

    def emit(self, nc):
        sems = {}
        for e in self.ENGINES:
            for kind in ("c", "d"):
                for i in range(self.KSEM):
                    name = f"{e}_{kind}{i}"
                    sems[name] = nc.semaphore(f"S_{name}").__enter__()
        blk = nc.Block()
        block = blk.__enter__()

        def mk(eng_name):
            own = eng_name + "_c" if eng_name == "tensor" else "\0"
            def body(eng):
                watermark = defaultdict(int)
                for fn, deps, tok, dma in self.ops[eng_name]:
                    need = {}
                    for (sem, val) in deps:
                        if sem.startswith(own):
                            continue
                        if val > watermark[sem] and val > need.get(sem, 0):
                            need[sem] = val
                    for sem, val in need.items():
                        eng.wait_ge(sems[sem], val)
                        watermark[sem] = val
                    inst = fn(eng)
                    if tok is not None:
                        inst.then_inc(sems[tok[0]], 16 if dma else 1)
            return body

        block.sync(mk("sync"))
        block.tensor(mk("tensor"))
        block.vector(mk("vector"))
        block.scalar(mk("scalar"))
        block.gpsimd(mk("gpsimd"))
        blk.__exit__(None, None, None)



class Bld:

    def __init__(self, arena_cols=51500):
        self.nc = bass.Bass("TRN2", target_bir_lowering=False)
        self.S = Sched()
        self.AR = self.nc.alloc_sbuf_tensor("AR", [128, arena_cols], F32).ap()
        self.PS = [self.nc.alloc_psum_tensor(f"PB{i}", [128, 512], F32).ap() for i in range(8)]
        self.psb = [Buf(f"ps{i}") for i in range(8)]
        self._cur = 0
        self._arena_cols = arena_cols
        self.bufs = {}
        self.marks = []

    def alloc(self, n, name=""):
        o = self._cur
        self._cur += n
        assert self._cur <= self._arena_cols, f"arena overflow at {name}: {self._cur}"
        return o

    def push(self):
        self.marks.append(self._cur)

    def pop(self):
        self._cur = self.marks.pop()

    def V(self, off, parts, cols):
        return self.AR[0:parts, off:off + cols]

    def VR(self, off, p0, p1, cols):
        return self.AR[p0:p1, off:off + cols]

    def B(self, key):
        if key not in self.bufs:
            self.bufs[key] = Buf(str(key))
        return self.bufs[key]

    def dma(self, dst, src, reads=(), writes=(), eng="sync"):
        return self.S.op(eng, lambda e: e.dma_start(out=dst, in_=src),
                         reads=reads, writes=writes, dma=True)

    def mm(self, bank, n, lhsT, rhs, start, stop, reads=(), c0=0, parts=128):
        ps = self.PS[bank][0:parts, c0:c0 + n]
        extra = ()
        if start:
            extra = tuple(self.psb[bank].readers)
            if self.psb[bank].writer is not None:
                extra += (self.psb[bank].writer,)
        return self.S.op("tensor",
                         lambda e: e.matmul(ps, lhsT, rhs, start=start, stop=stop),
                         reads=reads,
                         writes=(self.psb[bank],) if stop else (), extra=extra,
                         quiet=False)

    def trans(self, bank, out_parts, out_cols, in_, ident, reads=(), c0=0):
        ps = self.PS[bank][0:out_parts, c0:c0 + out_cols]
        extra = tuple(self.psb[bank].readers)
        if self.psb[bank].writer is not None:
            extra += (self.psb[bank].writer,)
        return self.S.op("tensor", lambda e: e.transpose(ps, in_, ident),
                         reads=reads, writes=(self.psb[bank],), extra=extra)

    def vo(self, fn, reads=(), writes=(), eng="vector"):
        return self.S.op(eng, fn, reads=reads, writes=writes)

    def cp(self, dst, src, reads=(), writes=()):
        eng = self.S.ev()
        if eng == "scalar":
            return self.S.op(eng, lambda e: e.copy(dst, src), reads=reads, writes=writes)
        return self.S.op(eng, lambda e: e.tensor_copy(dst, src), reads=reads, writes=writes)

    def ev2(self):
        return "vector"

    def alloc_state(self):
        self.O_S = self.alloc(HV * DK, "S")
        self.b_S = [self.B(("S", h)) for h in range(HV)]
        for h in range(HV):
            self.vo(lambda e, h=h: e.memset(self.V(self.O_S + h * DK, 128, DK), 0.0),
                    writes=(self.b_S[h],))

    def alloc_scan_ws(self):
        a = self.alloc
        self.O_GBTM = a(32, "gbtm")
        self.O_CGTM = a(16, "cgtm")
        self.O_LAM0 = a(16, "lam0")
        self.O_CCOL = a(16, "ccol")
        self.O_CGF = [a(128, "cgf0"), a(128, "cgf1")]
        self.O_BFM = [a(128, "bfm0"), a(128, "bfm1")]
        self.O_BD = a(1024, "bd")
        self.O_E = a(1024, "etile")
        self.O_EP = a(1024, "eptile")
        self.O_BB = a(1024, "bbeta")
        self.O_KK = a(512, "kk4")
        self.O_QK = a(512, "qk4")
        self.O_G = a(1024, "gtile")
        self.O_E127 = a(8, "e127")
        self.O_LAMC = a(8, "lamc")
        self.O_KTM = a(1024, "ktm")
        self.O_VTM = a(512, "vtm")
        self.O_PA = a(512, "pa")
        self.O_PB = a(512, "pb")
        self.O_X = a(1024, "xsolve")
        self.O_UB = a(512, "ubar")
        self.O_WFM = a(512, "wfm")
        self.O_CORE = a(128, "core")
        self.O_ZS = a(128, "zsil")

    def alloc_scan_consts(self):
        a = self.alloc
        self.O_ONESR = a(128, "onesr")
        self.O_ONESC = a(1, "onesc")
        self.O_ONES8 = a(128, "ones8")
        self.O_ID8 = a(8, "id8")
        self.O_ID16 = a(16, "id16")
        self.O_ID32 = a(32, "id32")
        self.O_ID128 = a(128, "id128")
        self.O_MSU = a(128, "msu")
        self.O_MSL = a(128, "msl")
        self.O_MIU = a(128, "miu")
        self.O_BLK8 = a(1024, "blk8")
        self.O_NWBC = a(128, "nwbc")

    def load_scan_consts(self, din):
        V, b = self.V, self.B("const")
        for dst, name, shape in [
            (V(self.O_ONESR, 1, 128), "ones_row", [1, 128]),
            (V(self.O_ONESC, 128, 1), "ones_col", [128, 1]),
            (V(self.O_ONES8, 8, 128), "ones8", [8, 128]),
            (V(self.O_ID8, 8, 8), "id8", [8, 8]),
            (V(self.O_ID16, 16, 16), "id16", [16, 16]),
            (V(self.O_ID32, 32, 32), "id32", [32, 32]),
            (V(self.O_ID128, 128, 128), "id128", [128, 128]),
            (V(self.O_MSU, 128, 128), "mask_su", [128, 128]),
            (V(self.O_MSL, 128, 128), "mask_sl", [128, 128]),
            (V(self.O_MIU, 128, 128), "mask_iu", [128, 128]),
            (V(self.O_BLK8, 8, 1024), "blk8", [8, 1024]),
        ]:
            self.dma(dst, din(name, shape), writes=(b,))

    def emit_chunk(self, kq_slices, v_slices, o_g, o_bt, gcol, b_p, b_gb, with_out,
                   core_sink=None, probe=None, o_alt=None):
        S, V, VR, B, PS = self.S, self.V, self.VR, self.B, self.PS
        psb = self.psb
        bc = B("const")

        bgbt = B("gbtm")
        self.trans(4, 128, 16, VR(o_g + gcol, 0, 16, CH), V(self.O_ID16, 16, 16),
                   reads=(b_gb,))
        self.vo(lambda e: e.tensor_copy(V(self.O_GBTM, 128, 16), PS[4][:, 0:16]),
                reads=(psb[4],), writes=(bgbt,))
        self.trans(5, 128, 16, VR(o_bt + gcol, 0, 16, CH), V(self.O_ID16, 16, 16),
                   reads=(b_gb,))
        self.vo(lambda e: e.tensor_copy(VR(self.O_GBTM + 16, 0, 128, 16), PS[5][:, 0:16]),
                reads=(psb[5],), writes=(bgbt,))
        mm = self.mm
        mm(5, 16, V(self.O_MIU, 128, 128), V(self.O_GBTM, 128, 16), True, True,
           reads=(bc, bgbt))
        bcg = B("cgtm")
        self.vo(lambda e: e.tensor_copy(V(self.O_CGTM, 128, 16), PS[5][:, 0:16]),
                reads=(psb[5],), writes=(bcg,))
        if probe: probe("cgtm", V(self.O_CGTM, 128, 16), (bcg,))
        blam = B("lam0")
        S.op("scalar", lambda e: e.activation(V(self.O_LAM0, 128, 16), V(self.O_CGTM, 128, 16), AF.Exp),
             reads=(bcg,), writes=(blam,))
        bccol = B("ccol")
        self.vo(lambda e: e.tensor_tensor(V(self.O_CCOL, 128, 16), V(self.O_LAM0, 128, 16),
                                          VR(self.O_GBTM + 16, 0, 128, 16), ALU.mult),
                reads=(blam, bgbt), writes=(bccol,))
        bcgf = [B("cgf0"), B("cgf1")]
        bbfm = [B("bfm0"), B("bfm1")]
        for half in range(2):
            self.trans(4, 8, 128, V(self.O_CGTM + 8 * half, 128, 8),
                       V(self.O_ID128, 128, 128), reads=(bcg,), c0=128 * half)
            self.vo(lambda e, half=half: e.tensor_copy(
                V(self.O_CGF[half], 8, 128), PS[4][0:8, 128 * half:128 * half + 128]),
                reads=(psb[4],), writes=(bcgf[half],))
        for half in range(2):
            self.trans(5, 8, 128, VR(self.O_GBTM + 16 + 8 * half, 0, 128, 8),
                       V(self.O_ID128, 128, 128), reads=(bgbt,), c0=128 * half)
            self.vo(lambda e, half=half: e.tensor_scalar(
                V(self.O_BFM[half], 8, 128), PS[5][0:8, 128 * half:128 * half + 128],
                -1.0, None, ALU.mult),
                reads=(psb[5],), writes=(bbfm[half],))

        bktm = B("ktm")
        for kh in range(HK):
            k_ap, _ = kq_slices(kh)
            bank = 4 if kh % 2 == 0 else 5
            self.trans(bank, 128, 128, k_ap, V(self.O_ID128, 128, 128), reads=(b_p,))
            self.cp(V(self.O_KTM + kh * 128, 128, 128), PS[bank][:, 0:128],
                    reads=(psb[bank],), writes=(bktm,))

        for bt in range(2):
            bD = B("etile")
            bDP = B("eptile")
            bBB = B("bbeta")
            bKK = B("kk4")
            bQK = B("qk4")
            bG = B("gtile")
            bE127 = B("e127")
            bLC = B("lamc")

            for i in range(4):
                kh = 4 * bt + i
                k_ap, _ = kq_slices(kh)
                mm(0, 128, k_ap, k_ap, True, True, reads=(b_p,), c0=i * 128)
            tkk = self.vo(lambda e: e.tensor_copy(V(self.O_KK, 128, 512), PS[0][:, 0:512]),
                          reads=(psb[0],), writes=(bKK,))
            if with_out:
                for i in range(4):
                    kh = 4 * bt + i
                    k_ap, q_ap = kq_slices(kh)
                    mm(1, 128, k_ap, q_ap, True, True, reads=(b_p,), c0=i * 128)
                self.vo(lambda e: e.tensor_copy(V(self.O_QK, 128, 512), PS[1][:, 0:512]),
                        reads=(psb[1],), writes=(bQK,))

            bBD = B("bd")
            cgf = V(self.O_CGF[bt], 8, 128)
            cgf_b = cgf.rearrange("p (o f) -> p o f", o=1).broadcast_to([8, 8, 128])
            blkv = V(self.O_BLK8, 8, 1024).rearrange("p (o f) -> p o f", o=8)
            bdv = V(self.O_BD, 8, 1024).rearrange("p (o f) -> p o f", o=8)
            self.vo(lambda e, bdv=bdv, cgf_b=cgf_b, blkv=blkv: e.tensor_tensor(bdv, cgf_b, blkv, ALU.mult),
                    reads=(bcgf[bt], bc), writes=(bBD,))
            if probe: probe(f"BD{bt}", self.V(self.O_BD, 8, 1024), (bBD,))
            for half in range(2):
                mm(2, 512, cgf, VR(self.O_BLK8 + half * 512, 0, 8, 512), True, True,
                   reads=(bcgf[bt], bc), c0=0)
                self.vo(lambda e, half=half: e.tensor_copy(
                    V(self.O_EP + half * 512, 128, 512), PS[2][:, 0:512]),
                    reads=(psb[2],), writes=(bDP,))
            for half in range(2):
                mm(3, 512, V(self.O_ONES8, 8, 128), VR(self.O_BD + half * 512, 0, 8, 512),
                   True, True, reads=(bBD, bc), c0=0)
                self.vo(lambda e, half=half: e.tensor_copy(
                    V(self.O_E + half * 512, 128, 512), PS[3][:, 0:512]),
                    reads=(psb[3],), writes=(bD,))
            if probe: probe(f"CGS{bt}", V(self.O_EP, 128, 1024), (bDP,))
            if probe: probe(f"BCG{bt}", V(self.O_E, 128, 1024), (bD,))
            bfm = V(self.O_BFM[bt], 8, 128)
            bfm_b = bfm.rearrange("p (o f) -> p o f", o=1).broadcast_to([8, 8, 128])
            self.vo(lambda e, bdv=bdv, bfm_b=bfm_b, blkv=blkv: e.tensor_tensor(bdv, bfm_b, blkv, ALU.mult),
                    reads=(bbfm[bt], bc), writes=(bBD,))
            for half in range(2):
                mm(2, 512, V(self.O_ONES8, 8, 128), VR(self.O_BD + half * 512, 0, 8, 512),
                   True, True, reads=(bBD, bc), c0=0)
                self.vo(lambda e, half=half: e.tensor_copy(
                    V(self.O_BB + half * 512, 128, 512), PS[2][:, 0:512]),
                    reads=(psb[2],), writes=(bBB,))

            if probe: probe(f"CGF{bt}", V(self.O_CGF[bt], 8, 128), (bcgf[bt],))
            ev_ = V(self.O_E, 128, 1024)
            epv = V(self.O_EP, 128, 1024)
            self.vo(lambda e: e.tensor_tensor(ev_, ev_, epv, ALU.subtract),
                    reads=(bDP,), writes=(bD,))
            self.vo(lambda e: e.tensor_scalar(epv, ev_, -1.0, 0.0, ALU.mult, ALU.min),
                    reads=(bD,), writes=(bDP,))
            S.op("scalar", lambda e: e.activation(epv, epv, AF.Exp), reads=(), writes=(bDP,))
            self.vo(lambda e: e.tensor_scalar(ev_, ev_, 1.0, 0.0, ALU.mult, ALU.min),
                    reads=(), writes=(bD,))
            S.op("scalar", lambda e: e.activation(ev_, ev_, AF.Exp), reads=(), writes=(bD,))
            if probe: probe(f"E{bt}", ev_, (bD,))
            if probe: probe(f"EP{bt}", epv, (bDP,))

            e127v = V(self.O_E127, 128, 8)
            esel = V(self.O_E, 128, 1024).rearrange("p (h t) -> p h t", h=8)[:, :, 127:128]
            self.vo(lambda e, e127v=e127v, esel=esel: e.tensor_copy(e127v.rearrange("p (h o) -> p h o", o=1), esel),
                    reads=(bD,), writes=(bE127,))
            self.vo(lambda e, bt=bt: e.tensor_tensor(
                V(self.O_LAMC, 128, 8), e127v, VR(self.O_LAM0 + 8 * bt, 0, 128, 8), ALU.mult),
                reads=(bE127, blam), writes=(bLC,))

            kkpair = V(self.O_KK, 128, 512).rearrange("p (h o t) -> p h o t", h=4, o=1) \
                .broadcast_to([128, 4, 2, 128])
            msuv = V(self.O_MSU, 128, 128).rearrange("p (o t) -> p o t", o=1).broadcast_to([128, 8, 128])
            mslv = V(self.O_MSL, 128, 128).rearrange("p (o t) -> p o t", o=1).broadcast_to([128, 8, 128])
            miuv = V(self.O_MIU, 128, 128).rearrange("p (o t) -> p o t", o=1).broadcast_to([128, 8, 128])
            ev4 = V(self.O_E, 128, 1024).rearrange("p (h r t) -> p h r t", h=4, r=2)
            epv4 = V(self.O_EP, 128, 1024).rearrange("p (h r t) -> p h r t", h=4, r=2)
            ev8 = V(self.O_E, 128, 1024).rearrange("p (h t) -> p h t", h=8)
            epv8 = V(self.O_EP, 128, 1024).rearrange("p (h t) -> p h t", h=8)
            gv8 = V(self.O_G, 128, 1024).rearrange("p (h t) -> p h t", h=8)
            gv4 = V(self.O_G, 128, 1024).rearrange("p (h r t) -> p h r t", h=4, r=2)

            if with_out:
                qkpair = V(self.O_QK, 128, 512).rearrange("p (h o t) -> p h o t", h=4, o=1) \
                    .broadcast_to([128, 4, 2, 128])
                self.vo(lambda e: e.tensor_tensor(gv8, ev8, miuv, ALU.mult),
                        reads=(bD, bc), writes=(bG,))
                self.vo(lambda e: e.tensor_tensor(gv4, gv4, qkpair, ALU.mult),
                        reads=(bQK,), writes=(bG,))
            self.vo(lambda e: e.tensor_tensor(ev8, ev8, msuv, ALU.mult),
                    reads=(bc,), writes=(bD,))
            self.vo(lambda e: e.tensor_tensor(ev4, ev4, kkpair, ALU.mult),
                    reads=(bKK,), writes=(bD,))
            self.vo(lambda e: e.tensor_tensor(V(self.O_E, 128, 1024), V(self.O_E, 128, 1024),
                                              V(self.O_BB, 128, 1024), ALU.mult),
                    reads=(bBB,), writes=(bD,))
            if probe: probe(f"M{bt}", V(self.O_E, 128, 1024), (bD,))
            self.vo(lambda e: e.tensor_tensor(epv8, epv8, mslv, ALU.mult),
                    reads=(bc,), writes=(bDP,))
            self.vo(lambda e: e.tensor_tensor(epv4, epv4, kkpair, ALU.mult),
                    reads=(bKK,), writes=(bDP,))
            for i in range(8):
                h = 8 * bt + i
                asl = V(self.O_EP + i * 128, 128, 128)
                bcol = VR(self.O_GBTM + 16 + h, 0, 128, 1)
                eng = "vector"
                self.vo(lambda e, asl=asl, bcol=bcol:
                    e.tensor_scalar(asl, asl, bcol, -1.0, ALU.mult, ALU.mult),
                    reads=(bgbt,), writes=(bDP,), eng=eng)
            if probe: probe(f"A{bt}", V(self.O_EP, 128, 1024), (bDP,))

            wsX = [self.O_X, o_alt]
            wsPA = [self.O_PA, o_alt + 1024]
            wsPB = [self.O_PB, o_alt + 1536]
            wsUB = [self.O_UB, o_alt + 2048]
            wsWF = [self.O_WFM, o_alt + 2560]
            wsVT = [self.O_VTM, o_alt + 3072]
            bX_ = [B("xsolve"), B(("xsolve", 1))]
            bPA_ = [B("pa"), B(("pa", 1))]
            bPB_ = [B("pb"), B(("pb", 1))]
            bUB_ = [B("ubar"), B(("ubar", 1))]
            bWF_ = [B("wfm"), B(("wfm", 1))]
            bVT_ = [B("vtm"), B(("vtm", 1))]
            TRB = [(4, 5), (2, 3)]
            APB = [(2, 3), (6, 7)]
            SQB = [(0, 1), (4, 5)]
            WSB = [4, 6]
            STB = [7, 3]

            def heads_of(g):
                return [8 * bt + 4 * g + i for i in range(4)]

            for g in range(2):
                for i, h in enumerate(heads_of(g)):
                    bank = TRB[g][i % 2]
                    self.trans(bank, 128, 128, v_slices(h), V(self.O_ID128, 128, 128),
                               reads=(b_p,))
                    self.cp(V(wsVT[g] + i * 128, 128, 128), PS[bank][:, 0:128],
                            reads=(psb[bank],), writes=(bVT_[g],))
            for g in range(2):
                for i, h in enumerate(heads_of(g)):
                    kh = h // 2
                    ccol = VR(self.O_CCOL + h, 0, 128, 1)
                    btcol = VR(self.O_GBTM + 16 + h, 0, 128, 1)
                    self.vo(lambda e, i=i, kh=kh, ccol=ccol, g=g: e.tensor_scalar(
                        V(wsX[g] + i * 256, 128, 128), V(self.O_KTM + kh * 128, 128, 128),
                        ccol, None, ALU.mult),
                        reads=(bktm, bccol), writes=(bX_[g],), eng=self.ev2())
                    self.vo(lambda e, i=i, btcol=btcol, g=g: e.tensor_scalar(
                        V(wsX[g] + i * 256 + 128, 128, 128), V(wsVT[g] + i * 128, 128, 128),
                        btcol, None, ALU.mult),
                        reads=(bVT_[g], bgbt), writes=(bX_[g],), eng=self.ev2())
            if probe:
                probe(f"R{bt}0", V(wsX[0], 128, 1024), (bX_[0],))
                probe(f"R{bt}1", V(wsX[1], 128, 1024), (bX_[1],))

            def pslice(g, j, which, i):
                if j == 0:
                    of = self.O_E if which == 0 else self.O_EP
                    return V(of + (4 * g + i) * 128, 128, 128)
                of = wsPA[g] if which == 0 else wsPB[g]
                return V(of + i * 128, 128, 128)

            def pbufs(g, j):
                if j == 0:
                    return (bD, bDP)
                return (bPA_[g], bPB_[g])

            for j in range(7):
                for g in range(2):
                    pa_b, pb_b = pbufs(g, j)
                    for i in range(4):
                        bank = APB[g][0] if i < 2 else APB[g][1]
                        mm(bank, 256, pslice(g, j, 0, i), V(wsX[g] + i * 256, 128, 256),
                           True, True, reads=(pa_b, bX_[g]), c0=(i % 2) * 256)
                for g in range(2):
                    for half in range(2):
                        bank = APB[g][half]
                        xv = V(wsX[g] + half * 512, 128, 512)
                        self.vo(lambda e, xv=xv, bank=bank:
                                e.tensor_tensor(xv, xv, PS[bank][:, 0:512], ALU.add),
                                reads=(psb[bank],), writes=(bX_[g],))
                if j < 6:
                    for g in range(2):
                        pa_b, pb_b = pbufs(g, j)
                        for i in range(4):
                            mm(SQB[g][0], 128, pslice(g, j, 1, i), pslice(g, j, 0, i),
                               True, True, reads=(pa_b, pb_b), c0=i * 128)
                        for i in range(4):
                            mm(SQB[g][1], 128, pslice(g, j, 0, i), pslice(g, j, 1, i),
                               True, True, reads=(pa_b, pb_b), c0=i * 128)
                    for g in range(2):
                        self.cp(V(wsPA[g], 128, 512), PS[SQB[g][0]][:, 0:512],
                                reads=(psb[SQB[g][0]],), writes=(bPA_[g],))
                        self.cp(V(wsPB[g], 128, 512), PS[SQB[g][1]][:, 0:512],
                                reads=(psb[SQB[g][1]],), writes=(bPB_[g],))
            if probe:
                probe(f"X{bt}0", V(wsX[0], 128, 1024), (bX_[0],))
                probe(f"X{bt}1", V(wsX[1], 128, 1024), (bX_[1],))
            for g in range(2):
                for i, h in enumerate(heads_of(g)):
                    bank = TRB[g][i % 2]
                    self.trans(bank, 128, 128, V(wsX[g] + i * 256, 128, 128),
                               V(self.O_ID128, 128, 128), reads=(bX_[g],))
                    self.cp(V(wsWF[g] + i * 128, 128, 128), PS[bank][:, 0:128],
                            reads=(psb[bank],), writes=(bWF_[g],))
            for g in range(2):
                for i, h in enumerate(heads_of(g)):
                    mm(WSB[g], 128, V(wsWF[g] + i * 128, 128, 128),
                       V(self.O_S + h * DK, 128, 128), True, True,
                       reads=(bWF_[g], self.b_S[h]), c0=(i % 4) * 128)
            for g in range(2):
                for i, h in enumerate(heads_of(g)):
                    self.vo(lambda e, i=i, g=g: e.tensor_tensor(
                        V(wsUB[g] + i * 128, 128, 128), V(wsX[g] + i * 256 + 128, 128, 128),
                        PS[WSB[g]][:, i * 128:(i + 1) * 128], ALU.subtract),
                        reads=(psb[WSB[g]], bX_[g]), writes=(bUB_[g],))
            if probe:
                probe(f"UB{bt}0", V(wsUB[0], 128, 512), (bUB_[0],))
                probe(f"UB{bt}1", V(wsUB[1], 128, 512), (bUB_[1],))
            if with_out:
                for g in range(2):
                    for i, h in enumerate(heads_of(g)):
                        kh = h // 2
                        _, q_ap = kq_slices(kh)
                        mm(5, 128, q_ap, V(self.O_S + h * DK, 128, 128), True, True,
                           reads=(b_p, self.b_S[h]), c0=0)
                        gsl_h = V(self.O_G + (4 * g + i) * 128, 128, 128)
                        mm(6, 128, gsl_h, V(wsUB[g] + i * 128, 128, 128), True, True,
                           reads=(bG, bUB_[g]), c0=0)
                        lam0col = VR(self.O_LAM0 + h, 0, 128, 1)
                        bCORE = B("core")
                        cv = V(self.O_CORE, 128, 128)
                        self.vo(lambda e, cv=cv, lam0col=lam0col: e.tensor_scalar(
                            cv, PS[5][:, 0:128], lam0col, None, ALU.mult),
                            reads=(psb[5], blam), writes=(bCORE,))
                        self.vo(lambda e, cv=cv: e.tensor_tensor(cv, cv, PS[6][:, 0:128], ALU.add),
                                reads=(psb[6],), writes=(bCORE,))
                        core_sink(h, cv, (bCORE,))
            for g in range(2):
                for i, h in enumerate(heads_of(g)):
                    kh = h // 2
                    e127c = VR(self.O_E127 + (h - 8 * bt), 0, 128, 1)
                    self.vo(lambda e, i=i, kh=kh, e127c=e127c, g=g: e.tensor_scalar(
                        V(wsX[g] + i * 256, 128, 128), V(self.O_KTM + kh * 128, 128, 128),
                        e127c, None, ALU.mult),
                        reads=(bktm, bE127), writes=(bX_[g],), eng=self.ev2())
            for g in range(2):
                for i, h in enumerate(heads_of(g)):
                    mm(STB[g], 128, V(wsX[g] + i * 256, 128, 128),
                       V(wsUB[g] + i * 128, 128, 128),
                       True, True, reads=(bX_[g], bUB_[g]), c0=(i % 4) * 128)
            for g in range(2):
                for i, h in enumerate(heads_of(g)):
                    lamcc = VR(self.O_LAMC + (h - 8 * bt), 0, 128, 1)
                    sv = V(self.O_S + h * DK, 128, 128)
                    self.vo(lambda e, sv=sv, lamcc=lamcc, i=i, g=g: e.scalar_tensor_tensor(
                        sv, sv, lamcc, PS[STB[g]][:, i * 128:(i + 1) * 128], ALU.mult, ALU.add),
                        reads=(psb[STB[g]], bLC), writes=(self.b_S[h],))


FULL_BUILD_PLACEHOLDER = True




def build_full(debug=False, phases=("prefix", "span", "ffn")):
    b = Bld(53200)
    nc = b.nc
    S, V, VR, PS, psb = b.S, b.V, b.VR, b.PS, b.psb
    B = b.B
    BF16 = mybir.dt.bfloat16

    def bfv(off, cols_f32):
        return b.AR[0:128, off:off + cols_f32].bitcast(BF16)

    def din(name, shape):
        return nc.dram_tensor(name, shape, F32, kind="ExternalInput").ap()

    xpre = din("xpre", [HID, NPT * PCOLS])
    premask = din("premask", [1, NPT * PCOLS])
    xspan = din("xspan", [HID, NQ * QCOLS])
    tailm = din("tailm", [128, 1])
    wqkv = din("wqkv", [CONV_DIM, HID])
    wz = din("wz", [HID, HID])
    wout_w = din("wout_w", [VAL_DIM, HID])
    wgup = din("wgup", [NEXP * 22 * 128, HID])
    wdwn = din("wdwn", [NEXP * 16 * 128, FF])
    wsgu = din("wsgu", [44 * 128, HID])
    wsd = din("wsd", [16 * 128, FS])
    wa_t = din("wa_t", [128, NKC * 16])
    wb_t = din("wb_t", [128, NKC * 16])
    wgate = din("wgate", [128, NKC * NEXP])
    bqkv_t = din("bqkv_t", [128, 32])
    cb_t = din("cb_t", [128, 32])
    cw_t = din("cw_t", [128, 128])
    bz_row = din("bz_row", [1, VAL_DIM])
    ba_c = din("ba_c", [16, 1])
    bb_c = din("bb_c", [16, 1])
    dtb = din("dtb", [16, 1])
    alog = din("alog", [16, 1])
    nw_row = din("nw_row", [1, DV])
    bout_t = din("bout_t", [128, 16])

    out = nc.dram_tensor("out", [HID, SPAN], F32, kind="ExternalOutput").ap()
    wqkv_s = nc.dram_tensor("wqkv_s", [CONV_DIM, 2 * HID], mybir.dt.bfloat16, kind="Internal").ap()
    wz_s = nc.dram_tensor("wz_s", [HID, 2 * HID], mybir.dt.bfloat16, kind="Internal").ap()
    wout_s = nc.dram_tensor("wout_s", [VAL_DIM, 2 * HID], mybir.dt.bfloat16, kind="Internal").ap()
    wgup_s = nc.dram_tensor("wgup_s", [NEXP * 22 * 128, 2 * HID], mybir.dt.bfloat16, kind="Internal").ap()
    wdwn_s = nc.dram_tensor("wdwn_s", [NEXP * 16 * 128, 2 * FF], mybir.dt.bfloat16, kind="Internal").ap()
    wsgu_s = nc.dram_tensor("wsgu_s", [44 * 128, 2 * HID], mybir.dt.bfloat16, kind="Internal").ap()
    wsd_s = nc.dram_tensor("wsd_s", [16 * 128, 2 * FS], mybir.dt.bfloat16, kind="Internal").ap()
    xnh_scr = nc.dram_tensor("xnh_scr", [HID, SPAN], BF16, kind="Internal").ap()
    xnl_scr = nc.dram_tensor("xnl_scr", [HID, SPAN], BF16, kind="Internal").ap()
    ax_scr = nc.dram_tensor("ax_scr", [HID, SPAN], F32, kind="Internal").ap()
    out_re = out.rearrange("(k p) t -> p k t", p=128)
    xnh_re = xnh_scr.rearrange("(k p) t -> p k t", p=128)
    xnl_re = xnl_scr.rearrange("(k p) t -> p k t", p=128)
    ax_re = ax_scr.rearrange("(k p) t -> p k t", p=128)
    xpre_re = xpre.rearrange("(k p) t -> p k t", p=128)
    xspan_re = xspan.rearrange("(k p) t -> p k t", p=128)
    if debug:
        d_spre = nc.dram_tensor("d_spre", [128, HV * DK], F32, kind="ExternalOutput").ap()
        d_ax = nc.dram_tensor("d_ax", [HID, SPAN], F32, kind="ExternalOutput").ap()

    b.alloc_scan_consts()
    O_WAH = b.alloc(128, "wah"); O_WAL = b.alloc(128, "wal")
    O_WBH = b.alloc(128, "wbh"); O_WBL = b.alloc(128, "wbl")
    O_WGH = b.alloc(64, "wgh"); O_WGL = b.alloc(64, "wgl")
    O_WSTAGE0 = b.alloc(256, "wstage_ab")
    O_BQKV = b.alloc(32, "bqkv")
    O_CBt = b.alloc(32, "cbt")
    O_CW = b.alloc(128, "cw")
    O_BOUT = b.alloc(16, "bout")
    O_BA = b.alloc(1, "ba")
    O_BB2 = b.alloc(1, "bb")
    O_DTB = b.alloc(1, "dtb")
    O_NEXPA = b.alloc(1, "nexpa")
    O_TAILM = b.alloc(1, "tailm")
    O_RC = b.alloc(1, "rcol")
    b.alloc_state()
    O_SHARED = b.alloc(3104, "shared")
    b.alloc_scan_ws()
    BIG0 = b._cur
    BIGSZ = 53200 - BIG0
    assert BIGSZ >= 27000, BIGSZ

    bc = B("const")
    b_shared = B("shared")
    b_xnscr = B("xn_scr")
    b_axscr = B("ax_scr")

    b.load_scan_consts(din)
    for dst, src in [
        (V(O_WSTAGE0, 128, 256), wa_t),
        (V(O_BQKV, 128, 32), bqkv_t),
        (V(O_CBt, 128, 32), cb_t), (V(O_CW, 128, 128), cw_t),
        (V(O_BOUT, 128, 16), bout_t), (V(O_BA, 16, 1), ba_c),
        (V(O_BB2, 16, 1), bb_c), (V(O_DTB, 16, 1), dtb),
        (V(O_NEXPA, 16, 1), alog), (V(O_TAILM, 128, 1), tailm),
    ]:
        b.dma(dst, src, writes=(bc,))

    def split_fl(o_src, o_hi, o_lo, cols, br, bw, src_parts=128):
        sv = V(o_src, src_parts, cols)
        hv_ = VR(o_hi, 0, src_parts, cols // 2).bitcast(BF16)
        lv = VR(o_lo, 0, src_parts, cols // 2).bitcast(BF16)
        S.op("scalar", lambda e: e.copy(hv_, sv), reads=(br,), writes=(bw,))
        b.vo(lambda e: e.tensor_tensor(lv, sv, hv_, ALU.subtract), reads=(br,), writes=(bw,))

    split_fl(O_WSTAGE0, O_WAH, O_WAL, 256, bc, bc)
    b.dma(V(O_WSTAGE0, 128, 256), wb_t, writes=(bc,))
    split_fl(O_WSTAGE0, O_WBH, O_WBL, 256, bc, bc)
    b.dma(VR(O_SHARED, 0, 128, 128), wgate, writes=(b_shared,))
    split_fl(O_SHARED, O_WGH, O_WGL, 128, b_shared, bc)

    S.op("scalar", lambda e: e.activation(V(O_NEXPA, 16, 1), V(O_NEXPA, 16, 1), AF.Exp),
         reads=(bc,), writes=(bc,))
    b.vo(lambda e: e.tensor_scalar(V(O_NEXPA, 16, 1), V(O_NEXPA, 16, 1), -1.0, None, ALU.mult),
         reads=(bc,), writes=(bc,))
    b.dma(V(b.O_NWBC, 1, 128), nw_row, writes=(B("nwtmp"),))
    b.mm(0, 128, V(b.O_ONESR, 1, 128), V(b.O_NWBC, 1, 128), True, True,
         reads=(bc, B("nwtmp")))
    b.vo(lambda e: e.tensor_copy(V(b.O_NWBC, 128, 128), PS[0][:, 0:128]),
         reads=(psb[0],), writes=(bc,))


    def emit_wload(o_slot, scr, r0, K, b_slot):
        b.dma(VR(o_slot, 0, 128, K).bitcast(mybir.dt.bfloat16),
              scr[r0:r0 + 128, 0:2 * K],
              reads=(B(("wscr", scr.tensor.name)),), writes=(b_slot,))

    def wall():
        for e in S.ENGINES:
            if S.pending_ph[e]:
                S.op(e, lambda en: en.nop(), reads=(), writes=(B(("wallnop", e)),))
        snap = tuple((s, v) for s, v in S.counts.items() if v > 0)
        wt = S.op("sync", lambda e: e.nop(), extra=snap)
        for bf in b.bufs.values():
            bf.writer = wt
            bf.readers = []
        for pb in psb:
            pb.writer = wt
            pb.readers = []

    def mm3(bank, n, whi, wlo, xhi, xlo, start, stop, reads=(), c0=0, parts=128):
        b.mm(bank, n, whi, xhi, start, False, reads=reads, c0=c0, parts=parts)
        b.mm(bank, n, whi, xlo, False, False, reads=reads, c0=c0, parts=parts)
        b.mm(bank, n, wlo, xhi, False, stop, reads=reads, c0=c0, parts=parts)

    def xsl(o_x, ncols, kc, part, t0, tn):
        base = o_x + kc * ncols + part * (ncols // 2)
        return b.AR[0:128, base:base + ncols // 2].bitcast(BF16)[:, t0:t0 + tn]

    def emit_ln(o_x, ncols, c0, n, o_sq, o_row, o_bc, bxk):
        assert n <= 512 and ncols % 2 == 0
        for kc in range(NKC):
            xa = V(o_x + kc * ncols + c0, 128, n)
            b.mm(6, n, V(b.O_ONESC, 128, 1), xa, kc == 0, kc == NKC - 1,
                 reads=(bc, bxk[kc]), parts=1)
        for kc in range(NKC):
            xa = V(o_x + kc * ncols + c0, 128, n)
            sq = V(o_sq + (kc % 2) * 512, 128, n)
            bsq = B(("lnsq", kc % 2))
            S.op("scalar", lambda e, sq=sq, xa=xa: e.activation(sq, xa, AF.Square),
                 reads=(bxk[kc],), writes=(bsq,))
            b.mm(7, n, V(b.O_ONESC, 128, 1), sq, kc == 0, kc == NKC - 1,
                 reads=(bc, bsq), parts=1)
        r_m = V(o_row, 1, n)
        r_v = V(o_row + 512, 1, n)
        r_t = V(o_row + 1024, 1, n)
        bm, bv, bt_ = B("lnm"), B("lnv"), B("lnt")
        b.vo(lambda e: e.tensor_scalar(r_m, PS[6][0:1, 0:n], 1.0 / HID, None, ALU.mult),
             reads=(psb[6],), writes=(bm,))
        b.vo(lambda e: e.tensor_scalar(r_v, PS[7][0:1, 0:n], 1.0 / HID, LN_EPS, ALU.mult, ALU.add),
             reads=(psb[7],), writes=(bv,))
        b.vo(lambda e: e.tensor_tensor(r_t, r_m, r_m, ALU.mult), reads=(bm,), writes=(bt_,))
        b.vo(lambda e: e.tensor_tensor(r_v, r_v, r_t, ALU.subtract), reads=(bt_,), writes=(bv,))
        b.vo(lambda e: e.reciprocal(r_v, r_v), reads=(), writes=(bv,))
        S.op("scalar", lambda e: e.activation(r_v, r_v, AF.Sqrt), reads=(), writes=(bv,))
        b.mm(6, n, V(b.O_ONESR, 1, 128), r_v, True, True, reads=(bc, bv))
        bc1 = B("lnc1")
        b.vo(lambda e: e.tensor_copy(V(o_bc, 128, n), PS[6][:, 0:n]),
             reads=(psb[6],), writes=(bc1,))
        b.mm(7, n, V(b.O_ONESR, 1, 128), r_m, True, True, reads=(bc, bm))
        bc0 = B("lnc0")
        b.vo(lambda e: e.tensor_copy(V(o_bc + 512, 128, n), PS[7][:, 0:n]),
             reads=(psb[7],), writes=(bc0,))
        for kc in range(NKC):
            xa = V(o_x + kc * ncols + c0, 128, n)
            t2 = V(o_sq + (kc % 2) * 512, 128, n)
            bsq = B(("lnsq", kc % 2))
            b.vo(lambda e, xa=xa: e.tensor_tensor(xa, xa, V(o_bc + 512, 128, n), ALU.subtract),
                 reads=(bc0,), writes=(bxk[kc],))
            b.vo(lambda e, xa=xa, t2=t2: e.tensor_tensor(t2, xa, V(o_bc, 128, n), ALU.mult),
                 reads=(bc1, bxk[kc]), writes=(bsq,))
            hi = bfv(o_x + kc * ncols, ncols // 2)[:, c0:c0 + n]
            lo = bfv(o_x + kc * ncols + ncols // 2, ncols // 2)[:, c0:c0 + n]
            S.op("scalar", lambda e, hi=hi, t2=t2: e.copy(hi, t2), reads=(bsq,), writes=(bxk[kc],))
            b.vo(lambda e, lo=lo, t2=t2, hi=hi: e.tensor_tensor(lo, t2, hi, ALU.subtract),
                 reads=(bsq,), writes=(bxk[kc],))

    def emit_wsplit(o_stage, o_slot, cols, wdram, r0, b_st, b_sl):
        b.dma(V(o_stage, 128, cols), wdram[r0:r0 + 128, 0:cols], writes=(b_st,))
        split_fl(o_stage, o_slot, o_slot + cols // 2, cols, b_st, b_sl)

    def emit_proj(o_dst, ncols_dst, bfts, gfts, o_xn, ncols_xn, c0, n, wsrc_scr,
                  o_wsl, b_wsl, bxk, b_p, o_pm=None, b_pm=None, pm_c0=0):
        assert n <= 512
        for i, (bft, gft) in enumerate(zip(bfts, gfts)):
            sl = i % 2
            emit_wload(o_wsl + sl * 2048, wsrc_scr, gft * 128, 2048, b_wsl[sl])
            bank = 6 if i % 2 == 0 else 7
            for kc in range(NKC):
                whi = bfv(o_wsl + sl * 2048 + kc * 64, 64)
                wlo = bfv(o_wsl + sl * 2048 + 1024 + kc * 64, 64)
                xhi = xsl(o_xn, ncols_xn, kc, 0, c0, n)
                xlo = xsl(o_xn, ncols_xn, kc, 1, c0, n)
                mm3(bank, n, whi, wlo, xhi, xlo, kc == 0, kc == NKC - 1,
                    reads=(b_wsl[sl], bxk[kc]))
            da = V(o_dst + bft * ncols_dst + c0, 128, n)
            bcol = V(O_BQKV + gft, 128, 1)
            if o_pm is not None:
                pmv = V(o_pm + pm_c0, 128, n)
                b.vo(lambda e, da=da, bank=bank, bcol=bcol, pmv=pmv:
                     e.scalar_tensor_tensor(da, PS[bank][:, 0:n], bcol, pmv, ALU.add, ALU.mult),
                     reads=(psb[bank], bc, b_pm), writes=(b_p,))
            else:
                eng = b.S.ev()
                if eng == "scalar":
                    b.vo(lambda e, da=da, bank=bank, bcol=bcol:
                         e.activation(da, PS[bank][:, 0:n], AF.Identity, bias=bcol),
                         reads=(psb[bank], bc), writes=(b_p,), eng="scalar")
                else:
                    b.vo(lambda e, da=da, bank=bank, bcol=bcol:
                         e.tensor_scalar(da, PS[bank][:, 0:n], bcol, None, ALU.add),
                         reads=(psb[bank], bc), writes=(b_p,))

    def emit_conv_silu(o_p, ncols, n, nft, gft0, o_cv, b_p):
        for ft in range(nft):
            gft = gft0 + ft
            acc = V(o_cv, 128, n)
            bacc = B("cvacc")
            for j in range(KCONV):
                src = V(o_p + ft * ncols + 1 + j, 128, n)
                cwc = V(O_CW + gft * 4 + j, 128, 1)
                if j == 0:
                    b.vo(lambda e, acc=acc, src=src, cwc=cwc: e.tensor_scalar(acc, src, cwc, None, ALU.mult),
                         reads=(b_p, bc), writes=(bacc,))
                else:
                    b.vo(lambda e, acc=acc, src=src, cwc=cwc:
                         e.scalar_tensor_tensor(acc, src, cwc, acc, ALU.mult, ALU.add),
                         reads=(b_p, bc), writes=(bacc,))
            dstv = V(o_p + ft * ncols + 4, 128, n)
            cbcol = V(O_CBt + gft, 128, 1)
            S.op("scalar", lambda e, dstv=dstv, acc=acc, cbcol=cbcol:
                 e.activation(dstv, acc, AF.Silu, bias=cbcol),
                 reads=(bacc, bc), writes=(b_p,))

    def emit_l2(o_p, ncols, n, bfts, is_q, o_sq, o_row, b_p):
        assert n <= 512
        for ft in bfts:
            base = V(o_p + ft * ncols + 4, 128, n)
            sq = V(o_sq, 128, n)
            bsq = B("l2sq")
            S.op("scalar", lambda e, sq=sq, base=base: e.activation(sq, base, AF.Square),
                 reads=(b_p,), writes=(bsq,))
            b.mm(6, n, V(b.O_ONESC, 128, 1), sq, True, True, reads=(bc, bsq), parts=1)
            rrow = V(o_row, 1, n)
            brow = B("l2row")
            b.vo(lambda e, rrow=rrow: e.tensor_scalar(rrow, PS[6][0:1, 0:n], L2_EPS, None, ALU.add),
                 reads=(psb[6],), writes=(brow,))
            b.vo(lambda e, rrow=rrow: e.reciprocal(rrow, rrow), reads=(), writes=(brow,))
            sc = (1.0 / DK) if is_q else 1.0
            S.op("scalar", lambda e, rrow=rrow, sc=sc: e.activation(rrow, rrow, AF.Sqrt, scale=sc),
                 reads=(), writes=(brow,))
            b.mm(7, n, V(b.O_ONESR, 1, 128), rrow, True, True, reads=(bc, brow))
            b.vo(lambda e, base=base: e.tensor_tensor(base, base, PS[7][:, 0:n], ALU.mult),
                 reads=(psb[7],), writes=(b_p,))

    def emit_gb(o_g, o_bt, o_gt, o_xn, ncols_xn, c0, n, bxk, b_gb, o_pm=None,
                b_pm=None, pm_c0=0):
        assert n <= 512
        for z_i, (o_dst, o_wh, o_wl, o_bias) in enumerate(
                [(o_g, O_WAH, O_WAL, O_BA), (o_bt, O_WBH, O_WBL, O_BB2)]):
            bank = 6 + z_i
            for kc in range(NKC):
                whi = bfv(o_wh + kc * 8, 8)
                wlo = bfv(o_wl + kc * 8, 8)
                xhi = xsl(o_xn, ncols_xn, kc, 0, c0, n)
                xlo = xsl(o_xn, ncols_xn, kc, 1, c0, n)
                mm3(bank, n, whi, wlo, xhi, xlo, kc == 0, kc == NKC - 1,
                    reads=(bc, bxk[kc]), parts=16)
            da = VR(o_dst, 0, 16, n)
            b.vo(lambda e, da=da, bank=bank, o_bias=o_bias:
                 e.tensor_scalar(da, PS[bank][0:16, 0:n], V(o_bias, 16, 1), None, ALU.add),
                 reads=(psb[bank], bc), writes=(b_gb,))
        ga = VR(o_g, 0, 16, n)
        bet = VR(o_bt, 0, 16, n)
        t1 = VR(o_gt, 0, 16, n)
        b1 = B("gbt1")
        S.op("scalar", lambda e: e.activation(bet, bet, AF.Sigmoid), reads=(), writes=(b_gb,))
        S.op("scalar", lambda e: e.activation(t1, ga, AF.Abs, bias=V(O_DTB, 16, 1)),
             reads=(b_gb, bc), writes=(b1,))
        S.op("scalar", lambda e: e.activation(t1, t1, AF.Exp, scale=-1.0), reads=(), writes=(b1,))
        S.op("scalar", lambda e: e.activation(t1, t1, AF.Ln, bias=1.0), reads=(), writes=(b1,))
        S.op("scalar", lambda e: e.activation(ga, ga, AF.Relu, bias=V(O_DTB, 16, 1)),
             reads=(bc,), writes=(b_gb,))
        b.vo(lambda e: e.tensor_tensor(ga, ga, t1, ALU.add), reads=(b1,), writes=(b_gb,))
        b.vo(lambda e: e.tensor_scalar(ga, ga, V(O_NEXPA, 16, 1), None, ALU.mult),
             reads=(bc,), writes=(b_gb,))
        if o_pm is not None:
            pmg = VR(o_pm + pm_c0, 0, 16, n)
            b.vo(lambda e: e.tensor_tensor(ga, ga, pmg, ALU.mult), reads=(b_pm,), writes=(b_gb,))
            b.vo(lambda e: e.tensor_tensor(bet, bet, pmg, ALU.mult), reads=(b_pm,), writes=(b_gb,))

    b._cur = BIG0
    o_xq = b.alloc(16 * PCOLS, "p_xq")
    o_sq = b.alloc(1024, "p_sq")
    o_row = b.alloc(1536, "p_row")
    o_bcb = b.alloc(1024, "p_bc")
    o_pkv = b.alloc(24 * PCOLS, "p_pkv")
    o_g = b.alloc(PTILE, "p_g")
    o_bt = b.alloc(PTILE, "p_bt")
    o_gt = b.alloc(PTILE, "p_gt")
    o_cv = b.alloc(584, "p_cv")
    o_pm = b.alloc(512, "p_pm")
    o_wsl = b.alloc(4096, "p_wsl")
    b_wsl = [B("wsl0"), B("wsl1")]
    b.dma(VR(O_SHARED, 0, 1, NPT * PCOLS), premask, writes=(b_shared,))

    bxk_p = [B(("px", kc)) for kc in range(NKC)]
    b.dma(V(o_xq, 128, 16 * PCOLS).rearrange("p (k t) -> p k t", k=16),
          xpre_re[:, :, 0:PCOLS], writes=tuple(bxk_p))
    emit_ln(o_xq, PCOLS, 4, 512, o_sq, o_row, o_bcb, bxk_p)

    o_pstage = o_pkv
    o_pslot = o_pkv + 8192
    b_pst = [B(f"pst{i}") for i in range(4)]
    b_psl = [B(f"psl{i}") for i in range(4)]
    psl_i = 0
    for scr, wdram, nft, K in [
        (wqkv_s, wqkv, CONV_DIM // 128, HID),
        (wz_s, wz, NKC, HID),
        (wout_s, wout_w, VAL_DIM // 128, HID),
        (wgup_s, wgup, NEXP * 22, HID),
        (wdwn_s, wdwn, NEXP * 16, FF),
        (wsgu_s, wsgu, 44, HID),
        (wsd_s, wsd, 16, FS),
    ]:
        npc = (K + 2047) // 2048
        w = K // npc
        scr_re = scr.rearrange("r (two k) -> r two k", two=2)
        for ft in range(nft):
            for pc in range(npc):
                c0 = pc * w
                sl = psl_i % 4
                psl_i += 1
                b.dma(V(o_pstage + sl * 2048, 128, w),
                      wdram[ft * 128:(ft + 1) * 128, c0:c0 + w], writes=(b_pst[sl],))
                split_fl(o_pstage + sl * 2048, o_pslot + sl * 2048,
                         o_pslot + sl * 2048 + w // 2, w, b_pst[sl], b_psl[sl])
                b.dma(scr_re[ft * 128:(ft + 1) * 128, :, c0:c0 + w],
                      VR(o_pslot + sl * 2048, 0, 128, w).bitcast(mybir.dt.bfloat16)
                      .rearrange("p (two k) -> p two k", two=2),
                      reads=(b_psl[sl],), writes=(B(("wscr", scr.tensor.name)),),
                      eng="gpsimd")
    wall()

    for pt in range(NPT):
        b_p = B("pp")
        b_gb = B("pgb")
        b_pm = B("ppm")
        if pt > 0:
            b.dma(V(o_xq, 128, 16 * PCOLS).rearrange("p (k t) -> p k t", k=16),
                  xpre_re[:, :, pt * PCOLS:(pt + 1) * PCOLS], writes=tuple(bxk_p))
        b.mm(6, 512, V(b.O_ONESR, 1, 128), VR(O_SHARED + pt * PCOLS + 4, 0, 1, 512),
             True, True, reads=(bc, b_shared))
        b.vo(lambda e: e.tensor_copy(V(o_pm, 128, 512), PS[6][:, 0:512]),
             reads=(psb[6],), writes=(b_pm,))
        if pt > 0:
            emit_ln(o_xq, PCOLS, 4, 512, o_sq, o_row, o_bcb, bxk_p)
        emit_proj(o_pkv, PCOLS, list(range(24)), list(range(8, 32)),
                  o_xq, PCOLS, 4, 512, wqkv_s, o_wsl, b_wsl, bxk_p, b_p,
                  o_pm=o_pm, b_pm=b_pm, pm_c0=0)
        hsl = V(o_pkv, 128, 24 * PCOLS).rearrange("p (f t) -> p f t", f=24)
        stash = V(o_cv + 512, 128, 24 * 3).rearrange("p (f t) -> p f t", f=24)
        if pt == 0:
            b.vo(lambda e: e.memset(hsl[:, :, 1:4], 0.0), reads=(), writes=(b_p,))
        else:
            b.vo(lambda e: e.tensor_copy(hsl[:, :, 1:4], stash),
                 reads=(B("cvhalo"),), writes=(b_p,))
        b.vo(lambda e: e.tensor_copy(stash, hsl[:, :, PCOLS - 3:PCOLS]),
             reads=(b_p,), writes=(B("cvhalo"),))
        emit_conv_silu(o_pkv, PCOLS, PTILE, 24, 8, o_cv, b_p)
        emit_l2(o_pkv, PCOLS, PTILE, list(range(8)), False, o_sq, o_row, b_p)
        emit_gb(o_g, o_bt, o_gt, o_xq, PCOLS, 4, PTILE, bxk_p, b_gb, o_pm=o_pm,
                b_pm=b_pm, pm_c0=0)

        def kqp(kh):
            return (V(o_pkv + kh * PCOLS + 4 + kqp.c * CH, 128, 128), None)

        def vsp(h):
            return V(o_pkv + (8 + h) * PCOLS + 4 + vsp.c * CH, 128, 128)

        ws1b = tuple(B(k) for k in (("xsolve", 1), ("pa", 1), ("pb", 1),
                                    ("ubar", 1), ("wfm", 1), ("vtm", 1)))
        S.op("sync", lambda e: e.nop(), writes=ws1b + (b_wsl[0], b_wsl[1]))
        for ci in range(PTILE // CH):
            kqp.c = ci
            vsp.c = ci
            b.emit_chunk(kqp, vsp, o_g, o_bt, ci * CH, b_p, b_gb, False,
                         o_alt=o_wsl)
        S.op("sync", lambda e: e.nop(), writes=ws1b + (b_wsl[0], b_wsl[1]))

    if debug:
        for h in range(HV):
            b.dma(d_spre[:, h * DK:(h + 1) * DK], V(b.O_S + h * DK, 128, DK),
                  reads=(b.b_S[h],))

    if "span" not in phases:
        fence = [(s, v) for s, v in S.counts.items() if "_d" in s and v > 0]
        S.op("sync", lambda e: e.nop(), extra=tuple(fence))
        S.emit(nc)
        return nc
    wall()
    b._cur = BIG0
    o_xq2 = b.alloc(16 * QCOLS, "s_xq")
    o_p2 = b.alloc(32 * QCOLS, "s_p")
    o_z = b.alloc(2 * VAL_DIM, "s_z")
    o_g2 = b.alloc(QTOK, "s_g")
    o_bt2 = b.alloc(QTOK, "s_bt")
    o_gt2 = b.alloc(QTOK, "s_gt")
    o_sq2 = b.alloc(1024, "s_sq")
    o_row2 = b.alloc(1536, "s_row")
    o_bc2 = b.alloc(1024, "s_bc")
    o_cv2 = b.alloc(584, "s_cv")
    o_gh = b.alloc(16 * 128, "s_gatedh")
    o_gl = b.alloc(16 * 128, "s_gatedl")
    o_wsl2 = b.alloc(4096, "s_wsl")
    b_wsl = [B("wsl0"), B("wsl1")]
    o_ax = o_xq2
    o_xres = o_z

    b.dma(VR(O_SHARED, 0, 1, VAL_DIM), bz_row, writes=(b_shared,))
    for fb in range(4):
        b.mm(6, 512, V(b.O_ONESR, 1, 128), VR(O_SHARED + fb * 512, 0, 1, 512),
             True, True, reads=(bc, b_shared))
        b.vo(lambda e, fb=fb: e.tensor_copy(V(O_SHARED + 1024 + fb * 512, 128, 512), PS[6][:, 0:512]),
             reads=(psb[6],), writes=(b_shared,))
    O_BZBC = O_SHARED + 1024

    bxk_s = [B(("sx", kc)) for kc in range(NKC)]
    for q in range(NQ):
        b_p = B("sp")
        b_gb = B("sgb")
        b_z = B("sz")
        b_gt = B("sgt")
        b_xr = B("sz")
        b.dma(V(o_xq2, 128, 16 * QCOLS).rearrange("p (k t) -> p k t", k=16),
              xspan_re[:, :, q * QCOLS:(q + 1) * QCOLS], writes=tuple(bxk_s))
        emit_ln(o_xq2, QCOLS, 0, QCOLS, o_sq2, o_row2, o_bc2, bxk_s)
        b.dma(xnh_re[:, :, q * QTOK:(q + 1) * QTOK],
              V(o_xq2, 128, 16 * QCOLS).rearrange("p (k t) -> p k t", k=16)
              [:, :, 0:QCOLS // 2].bitcast(BF16)[:, :, 4:QCOLS],
              reads=tuple(bxk_s), writes=(b_xnscr,))
        b.dma(xnl_re[:, :, q * QTOK:(q + 1) * QTOK],
              V(o_xq2, 128, 16 * QCOLS).rearrange("p (k t) -> p k t", k=16)
              [:, :, QCOLS // 2:QCOLS].bitcast(BF16)[:, :, 4:QCOLS],
              reads=tuple(bxk_s), writes=(b_xnscr,))
        emit_proj(o_p2, QCOLS, list(range(32)), list(range(32)),
                  o_xq2, QCOLS, 1, 259, wqkv_s, o_wsl2, b_wsl, bxk_s, b_p)
        if q == 0:
            b.vo(lambda e: e.tensor_scalar(
                V(o_p2, 128, 32 * QCOLS).rearrange("p (f t) -> p f t", f=32)[:, :, 1:4],
                V(o_p2, 128, 32 * QCOLS).rearrange("p (f t) -> p f t", f=32)[:, :, 1:4],
                V(O_TAILM, 128, 1), None, ALU.mult),
                reads=(bc,), writes=(b_p,))
        emit_conv_silu(o_p2, QCOLS, QTOK, 32, 0, o_cv2, b_p)
        emit_l2(o_p2, QCOLS, QTOK, list(range(8)), True, o_sq2, o_row2, b_p)
        emit_l2(o_p2, QCOLS, QTOK, list(range(8, 16)), False, o_sq2, o_row2, b_p)
        emit_gb(o_g2, o_bt2, o_gt2, o_xq2, QCOLS, 4, QTOK, bxk_s, b_gb)
        for kc in range(NKC):
            sl = kc % 2
            emit_wload(o_wsl2 + sl * 2048, wz_s, kc * 128, 2048, b_wsl[sl])
            for tc in range(2):
                lh = xsl(o_xq2, QCOLS, kc, 0, 4 + tc * 128, 128)
                ll = xsl(o_xq2, QCOLS, kc, 1, 4 + tc * 128, 128)
                for fb in range(4):
                    whi = bfv(o_wsl2 + sl * 2048 + fb * 256, 256)
                    wlo = bfv(o_wsl2 + sl * 2048 + 1024 + fb * 256, 256)
                    mm3(tc * 4 + fb, 512, lh, ll, whi, wlo, kc == 0, kc == NKC - 1,
                        reads=(b_wsl[sl], bxk_s[kc]))
        for tc in range(2):
            for fb in range(4):
                bank = tc * 4 + fb
                da = V(o_z + tc * VAL_DIM + fb * 512, 128, 512)
                b.vo(lambda e, da=da, bank=bank, fb=fb: e.tensor_tensor(
                    da, PS[bank][:, 0:512], V(O_BZBC + fb * 512, 128, 512), ALU.add),
                    reads=(psb[bank], b_shared), writes=(b_z,))

        def kqs2(kh):
            return (V(o_p2 + (8 + kh) * QCOLS + 4 + kqs2.c * CH, 128, 128),
                    V(o_p2 + kh * QCOLS + 4 + kqs2.c * CH, 128, 128))

        def vss(h):
            return V(o_p2 + (16 + h) * QCOLS + 4 + vss.c * CH, 128, 128)

        ws1b = tuple(B(k) for k in (("xsolve", 1), ("pa", 1), ("pb", 1),
                                    ("ubar", 1), ("wfm", 1), ("vtm", 1)))
        S.op("sync", lambda e: e.nop(), writes=ws1b + (b_wsl[0], b_wsl[1]))
        for ci in range(QTOK // CH):
            kqs2.c = ci
            vss.c = ci

            def sink(h, cv, reads, ci=ci, b_z=b_z, b_gt=b_gt):
                rc = V(O_RC, 128, 1)
                brc = B("rcol")
                csq = V(b.O_ZS, 128, 128)
                bzs = B("zsil")
                S.op("scalar", lambda e, cv=cv: e.activation(csq, cv, AF.Square),
                     reads=tuple(reads), writes=(bzs,))
                b.vo(lambda e: e.tensor_reduce(rc, csq, AX.X, ALU.add),
                     reads=(bzs,), writes=(brc,))
                b.vo(lambda e: e.tensor_scalar(rc, rc, 1.0 / DV, RMS_EPS, ALU.mult, ALU.add),
                     reads=(), writes=(brc,))
                b.vo(lambda e: e.reciprocal(rc, rc), reads=(), writes=(brc,))
                S.op("scalar", lambda e: e.activation(rc, rc, AF.Sqrt), reads=(), writes=(brc,))
                b.vo(lambda e, cv=cv: e.tensor_scalar(cv, cv, rc, None, ALU.mult),
                     reads=(brc,), writes=tuple(reads))
                b.vo(lambda e, cv=cv: e.tensor_tensor(cv, cv, V(b.O_NWBC, 128, 128), ALU.mult),
                     reads=(bc,), writes=tuple(reads))
                zs = V(b.O_ZS, 128, 128)
                zsl = V(o_z + ci * VAL_DIM + h * 128, 128, 128)
                S.op("scalar", lambda e, zsl=zsl: e.activation(zs, zsl, AF.Silu),
                     reads=(b_z,), writes=(bzs,))
                b.vo(lambda e, cv=cv: e.tensor_tensor(cv, cv, zs, ALU.mult),
                     reads=(bzs,), writes=tuple(reads))
                tb = h % 2
                b.trans(tb, 128, 128, cv, V(b.O_ID128, 128, 128), reads=tuple(reads))
                hi = bfv(o_gh + h * 128, 128)[:, ci * 128:ci * 128 + 128]
                lo = bfv(o_gl + h * 128, 128)[:, ci * 128:ci * 128 + 128]
                S.op("scalar", lambda e, hi=hi, tb=tb: e.copy(hi, PS[tb][:, 0:128]),
                     reads=(psb[tb],), writes=(b_gt,))
                b.vo(lambda e, lo=lo, hi=hi, tb=tb: e.tensor_tensor(lo, PS[tb][:, 0:128], hi, ALU.subtract),
                     reads=(psb[tb],), writes=(b_gt,))

            b.emit_chunk(kqs2, vss, o_g2, o_bt2, ci * CH, b_p, b_gb, True, sink,
                         o_alt=o_wsl2)

        S.op("sync", lambda e: e.nop(), writes=ws1b + (b_wsl[0], b_wsl[1]))
        b.dma(V(o_xres, 128, 16 * QTOK).rearrange("p (k t) -> p k t", k=16),
              xspan_re[:, :, q * QCOLS + 4:(q + 1) * QCOLS], writes=(b_xr,))
        for fo in range(16):
            sl = fo % 2
            emit_wload(o_wsl2 + sl * 2048, wout_s, fo * 128, 2048, b_wsl[sl])
            bank = 6 if fo % 2 == 0 else 7
            for kc in range(NKC):
                whi = bfv(o_wsl2 + sl * 2048 + kc * 64, 64)
                wlo = bfv(o_wsl2 + sl * 2048 + 1024 + kc * 64, 64)
                ghi = bfv(o_gh + kc * 128, 128)
                glo = bfv(o_gl + kc * 128, 128)
                mm3(bank, QTOK, whi, wlo, ghi, glo, kc == 0, kc == NKC - 1,
                    reads=(b_wsl[sl], b_gt))
            da = V(o_ax + fo * QCOLS, 128, QTOK)
            b.vo(lambda e, da=da, bank=bank, fo=fo: e.scalar_tensor_tensor(
                da, PS[bank][:, 0:QTOK], V(O_BOUT + fo, 128, 1),
                V(o_xres + fo * QTOK, 128, QTOK), ALU.add, ALU.add),
                reads=(psb[bank], bc, b_xr), writes=(bxk_s[fo],))
        b.dma(ax_re[:, :, q * QTOK:(q + 1) * QTOK],
              V(o_ax, 128, 16 * QCOLS).rearrange("p (k t) -> p k t", k=16)[:, :, 0:QTOK],
              reads=tuple(bxk_s), writes=(b_axscr,))

    if debug:
        b.dma(d_ax, ax_scr, reads=(b_axscr,))

    if "ffn" not in phases:
        fence = [(s, v) for s, v in S.counts.items() if "_d" in s and v > 0]
        S.op("sync", lambda e: e.nop(), extra=tuple(fence))
        S.emit(nc)
        return nc
    wall()
    b._cur = BIG0
    o_fxn = b.alloc(16 * 512, "f_xn")
    o_acc = b.alloc(16 * 512, "f_acc")
    o_act = b.alloc(11 * 512, "f_act")
    o_fsc = b.alloc(512, "f_scr")
    o_cmb = b.alloc(512, "f_cmb")
    o_cb = b.alloc(512, "f_cb")
    o_gs = b.alloc(8, "f_gs")
    o_gs2 = b.alloc(8, "f_gs2")
    o_gcl = b.alloc(4, "f_gcol")
    o_bd8 = b.alloc(512, "f_bd8")
    o_wsl3 = b.alloc(8192, "f_wsl")
    b_wsl = [B("wsl0"), B("wsl1"), B("wsl2"), B("wsl3")]

    def fxn_sl(kc, part, t0, tn):
        base = o_fxn + kc * 512 + part * 256
        return b.AR[0:128, base:base + 256].bitcast(BF16)[:, t0:t0 + tn]

    for half in range(2):
        b_x = B("fx")
        b_a = B("fa")
        b_cmb = B("fcmb")
        b_st = B("wstage")
        b.dma(V(o_fxn, 128, 16 * 512).rearrange("p (k t) -> p k t", k=16)
              [:, :, 0:256].bitcast(BF16),
              xnh_re[:, :, half * 512:(half + 1) * 512], reads=(b_xnscr,), writes=(b_x,))
        b.dma(V(o_fxn, 128, 16 * 512).rearrange("p (k t) -> p k t", k=16)
              [:, :, 256:512].bitcast(BF16),
              xnl_re[:, :, half * 512:(half + 1) * 512], reads=(b_xnscr,), writes=(b_x,))
        b.dma(V(o_acc, 128, 16 * 512).rearrange("p (k t) -> p k t", k=16),
              ax_re[:, :, half * 512:(half + 1) * 512], reads=(b_axscr,), writes=(b_a,))
        for tc in range(4):
            for kc in range(NKC):
                lh = fxn_sl(kc, 0, tc * 128, 128)
                ll = fxn_sl(kc, 1, tc * 128, 128)
                wgh = bfv(O_WGH + kc * 4, 4)
                wgl = bfv(O_WGL + kc * 4, 4)
                mm3(6, 8, lh, ll, wgh, wgl, kc == 0, kc == NKC - 1, reads=(bc, b_x))
            sc_ = V(o_gs, 128, 8)
            mc = V(o_gcl, 128, 1)
            m2c = V(o_gcl + 1, 128, 1)
            rs_ = V(o_gcl + 2, 128, 1)
            bsc, bmc = B("fsc"), B("fmc")
            b.vo(lambda e: e.tensor_reduce(mc, PS[6][:, 0:8], AX.X, ALU.max),
                 reads=(psb[6],), writes=(bmc,))
            b.vo(lambda e: e.tensor_scalar(mc, mc, -1.0, None, ALU.mult), reads=(), writes=(bmc,))
            S.op("scalar", lambda e: e.activation(sc_, PS[6][:, 0:8], AF.Exp, bias=mc),
                 reads=(psb[6], bmc), writes=(bsc,))
            b.vo(lambda e: e.tensor_reduce(rs_, sc_, AX.X, ALU.add), reads=(bsc,), writes=(bmc,))
            b.vo(lambda e: e.reciprocal(rs_, rs_), reads=(), writes=(bmc,))
            b.vo(lambda e: e.tensor_scalar(sc_, sc_, rs_, None, ALU.mult),
                 reads=(bmc,), writes=(bsc,))
            b.vo(lambda e: e.tensor_reduce(mc, sc_, AX.X, ALU.max), reads=(bsc,), writes=(bmc,))
            ge1 = V(o_gs2, 128, 8)
            bge = B("fge")
            b.vo(lambda e: e.tensor_scalar(ge1, sc_, mc, None, ALU.is_ge),
                 reads=(bsc, bmc), writes=(bge,))
            b.vo(lambda e: e.scalar_tensor_tensor(ge1, ge1, -1e9, sc_, ALU.mult, ALU.add),
                 reads=(bsc,), writes=(bge,))
            b.vo(lambda e: e.tensor_reduce(m2c, ge1, AX.X, ALU.max), reads=(bge,), writes=(bmc,))
            b.vo(lambda e: e.tensor_scalar(ge1, sc_, m2c, None, ALU.is_ge),
                 reads=(bsc, bmc), writes=(bge,))
            b.vo(lambda e: e.tensor_tensor(ge1, ge1, sc_, ALU.mult), reads=(bsc,), writes=(bge,))
            b.trans(5, 8, 128, ge1, V(b.O_ID128, 128, 128), reads=(bge,))
            b.vo(lambda e, tc=tc: e.tensor_copy(VR(o_cmb + tc * 128, 0, 8, 128), PS[5][0:8, 0:128]),
                 reads=(psb[5],), writes=(b_cmb,))
        for ei in range(NEXP):
            bcb = B("fcb")
            bd8 = VR(o_bd8, 0, 8, 512)
            b.vo(lambda e, ei=ei: e.tensor_scalar(bd8, VR(o_cmb, 0, 8, 512),
                                                  V(b.O_ID8 + ei, 8, 1), None, ALU.mult),
                 reads=(b_cmb, bc), writes=(B("fbd8"),))
            b.mm(6, 512, V(b.O_ONES8, 8, 128), bd8, True, True, reads=(B("fbd8"), bc))
            b.vo(lambda e: e.tensor_copy(V(o_cb, 128, 512), PS[6][:, 0:512]),
                 reads=(psb[6],), writes=(bcb,))
            bact = B("fact")
            for fp in range(11):
                r0 = (ei * 22 + fp) * 128
                r1 = (ei * 22 + fp + 11) * 128
                s0 = o_wsl3 + (fp % 2) * 4096
                w0, w1 = b_wsl[(fp % 2) * 2], b_wsl[(fp % 2) * 2 + 1]
                bk0, bk1 = (0, 1) if fp % 2 == 0 else (4, 5)
                emit_wload(s0, wgup_s, r0, 2048, w0)
                emit_wload(s0 + 2048, wgup_s, r1, 2048, w1)
                for kc in range(NKC):
                    xh = fxn_sl(kc, 0, 0, 512)
                    xlo = fxn_sl(kc, 1, 0, 512)
                    mm3(bk0, 512, bfv(s0 + kc * 64, 64), bfv(s0 + 1024 + kc * 64, 64),
                        xh, xlo, kc == 0, kc == NKC - 1, reads=(w0, b_x))
                for kc in range(NKC):
                    xh = fxn_sl(kc, 0, 0, 512)
                    xlo = fxn_sl(kc, 1, 0, 512)
                    mm3(bk1, 512, bfv(s0 + 2048 + kc * 64, 64), bfv(s0 + 3072 + kc * 64, 64),
                        xh, xlo, kc == 0, kc == NKC - 1, reads=(w1, b_x))
                fs_ = V(o_fsc, 128, 512)
                bfs = B("ffs")
                S.op("scalar", lambda e, fs_=fs_, bk0=bk0: e.activation(fs_, PS[bk0][:, 0:512], AF.Silu),
                     reads=(psb[bk0],), writes=(bfs,))
                b.vo(lambda e, fs_=fs_, bk1=bk1: e.tensor_tensor(fs_, fs_, PS[bk1][:, 0:512], ALU.mult),
                     reads=(psb[bk1],), writes=(bfs,))
                b.vo(lambda e, fs_=fs_: e.tensor_tensor(fs_, fs_, V(o_cb, 128, 512), ALU.mult),
                     reads=(bcb,), writes=(bfs,))
                ahi = bfv(o_act + fp * 512, 256)
                alo_ = bfv(o_act + fp * 512 + 256, 256)
                S.op("scalar", lambda e, ahi=ahi, fs_=fs_: e.copy(ahi, fs_),
                     reads=(bfs,), writes=(bact,))
                b.vo(lambda e, alo_=alo_, fs_=fs_, ahi=ahi: e.tensor_tensor(alo_, fs_, ahi, ALU.subtract),
                     reads=(bfs,), writes=(bact,))
            for fo in range(16):
                sl = fo % 4
                r0 = (ei * 16 + fo) * 128
                emit_wload(o_wsl3 + sl * 2048, wdwn_s, r0, FF, b_wsl[sl])
                bank = (2, 3, 6, 7)[fo % 4]
                for kc in range(11):
                    whi = bfv(o_wsl3 + sl * 2048 + kc * 64, 64)
                    wlo = bfv(o_wsl3 + sl * 2048 + 704 + kc * 64, 64)
                    ahi = bfv(o_act + kc * 512, 256)
                    alo_ = bfv(o_act + kc * 512 + 256, 256)
                    mm3(bank, 512, whi, wlo, ahi, alo_, kc == 0, kc == 10,
                        reads=(b_wsl[sl], bact))
                av = V(o_acc + fo * 512, 128, 512)
                b.vo(lambda e, av=av, bank=bank: e.tensor_tensor(av, av, PS[bank][:, 0:512], ALU.add),
                     reads=(psb[bank],), writes=(b_a,))
        for st in range(2):
            bact = B("fact")
            for fp in range(11):
                ft = st * 11 + fp
                s0 = o_wsl3 + (ft % 2) * 4096
                w0, w1 = b_wsl[(ft % 2) * 2], b_wsl[(ft % 2) * 2 + 1]
                bk0, bk1 = (0, 1) if ft % 2 == 0 else (4, 5)
                emit_wload(s0, wsgu_s, ft * 128, 2048, w0)
                emit_wload(s0 + 2048, wsgu_s, (ft + 22) * 128, 2048, w1)
                for kc in range(NKC):
                    xh = fxn_sl(kc, 0, 0, 512)
                    xlo = fxn_sl(kc, 1, 0, 512)
                    mm3(bk0, 512, bfv(s0 + kc * 64, 64), bfv(s0 + 1024 + kc * 64, 64),
                        xh, xlo, kc == 0, kc == NKC - 1, reads=(w0, b_x))
                for kc in range(NKC):
                    xh = fxn_sl(kc, 0, 0, 512)
                    xlo = fxn_sl(kc, 1, 0, 512)
                    mm3(bk1, 512, bfv(s0 + 2048 + kc * 64, 64), bfv(s0 + 3072 + kc * 64, 64),
                        xh, xlo, kc == 0, kc == NKC - 1, reads=(w1, b_x))
                fs_ = V(o_fsc, 128, 512)
                bfs = B("ffs")
                S.op("scalar", lambda e, fs_=fs_, bk0=bk0: e.activation(fs_, PS[bk0][:, 0:512], AF.Silu),
                     reads=(psb[bk0],), writes=(bfs,))
                b.vo(lambda e, fs_=fs_, bk1=bk1: e.tensor_tensor(fs_, fs_, PS[bk1][:, 0:512], ALU.mult),
                     reads=(psb[bk1],), writes=(bfs,))
                ahi = bfv(o_act + fp * 512, 256)
                alo_ = bfv(o_act + fp * 512 + 256, 256)
                S.op("scalar", lambda e, ahi=ahi, fs_=fs_: e.copy(ahi, fs_),
                     reads=(bfs,), writes=(bact,))
                b.vo(lambda e, alo_=alo_, fs_=fs_, ahi=ahi: e.tensor_tensor(alo_, fs_, ahi, ALU.subtract),
                     reads=(bfs,), writes=(bact,))
            for fo in range(16):
                sl = fo % 4
                b.dma(VR(o_wsl3 + sl * 2048, 0, 128, FF).bitcast(mybir.dt.bfloat16)
                      .rearrange("p (h t) -> p h t", h=2),
                      wsd_s[fo * 128:(fo + 1) * 128, :].rearrange("p (h s) -> p h s", h=2)
                      [:, :, st * FF:(st + 1) * FF],
                      reads=(B(("wscr", "wsd_s")),), writes=(b_wsl[sl],))
                bank = (2, 3, 6, 7)[fo % 4]
                for kc in range(11):
                    whi = bfv(o_wsl3 + sl * 2048 + kc * 64, 64)
                    wlo = bfv(o_wsl3 + sl * 2048 + 704 + kc * 64, 64)
                    ahi = bfv(o_act + kc * 512, 256)
                    alo_ = bfv(o_act + kc * 512 + 256, 256)
                    mm3(bank, 512, whi, wlo, ahi, alo_, kc == 0, kc == 10,
                        reads=(b_wsl[sl], bact))
                av = V(o_acc + fo * 512, 128, 512)
                b.vo(lambda e, av=av, bank=bank: e.tensor_tensor(av, av, PS[bank][:, 0:512], ALU.add),
                     reads=(psb[bank],), writes=(b_a,))
        b.dma(out_re[:, :, half * 512:(half + 1) * 512],
              V(o_acc, 128, 16 * 512).rearrange("p (k t) -> p k t", k=16),
              reads=(b_a,), writes=(B("fout"),))

    fence = []
    for s, v in S.counts.items():
        if "_d" in s and v > 0:
            fence.append((s, v))
    S.op("sync", lambda e: e.nop(), extra=tuple(fence))
    S.emit(nc)
    return nc



def _tile_lhsT(W, K=HID):
    F = W.shape[0]
    nft = F // 128
    nkc = K // 128
    H = W.reshape(nft, 128, nkc, 128).transpose(0, 3, 2, 1).reshape(nft * 128, K)
    return np.ascontiguousarray(H)


def _prep_weights(inp):
    f32 = np.float32
    g = lambda k: np.asarray(inp[k], dtype=f32)
    W = {}
    W["wqkv"] = _tile_lhsT(g("L0_attn_Wqkv"))
    W["wz"] = np.ascontiguousarray(g("L0_attn_Wz").T.reshape(NKC * 128, HID))
    W["wout_w"] = _tile_lhsT(g("L0_attn_Wout"), K=VAL_DIM)
    gup = g("L0_ffn_gate_up_weights")
    W["wgup"] = np.concatenate([_tile_lhsT(gup[e]) for e in range(NEXP)], 0)
    dwn = g("L0_ffn_down_weights")
    W["wdwn"] = np.concatenate([_tile_lhsT(dwn[e], K=FF) for e in range(NEXP)], 0)
    W["wsgu"] = _tile_lhsT(g("L0_ffn_shared_gate_up"))
    W["wsd"] = _tile_lhsT(g("L0_ffn_shared_down"), K=FS)
    W["wa_t"] = np.ascontiguousarray(
        g("L0_attn_Wa").T.reshape(NKC, 128, 16).transpose(1, 0, 2).reshape(128, NKC * 16))
    W["wb_t"] = np.ascontiguousarray(
        g("L0_attn_Wb").T.reshape(NKC, 128, 16).transpose(1, 0, 2).reshape(128, NKC * 16))
    W["wgate"] = np.ascontiguousarray(
        g("L0_ffn_gate_weight").T.reshape(NKC, 128, 8).transpose(1, 0, 2).reshape(128, NKC * 8))
    W["bqkv_t"] = np.ascontiguousarray(g("L0_attn_bqkv").reshape(32, 128).T)
    W["cb_t"] = np.ascontiguousarray(g("L0_attn_conv_b").reshape(32, 128).T)
    W["cw_t"] = np.ascontiguousarray(
        g("L0_attn_conv_w").reshape(32, 128, 4).transpose(1, 0, 2).reshape(128, 128))
    W["bz_row"] = g("L0_attn_bz")[None, :]
    W["ba_c"] = g("L0_attn_ba")[:, None]
    W["bb_c"] = g("L0_attn_bb")[:, None]
    W["dtb"] = g("L0_attn_dt_bias")[:, None]
    W["alog"] = g("L0_attn_A_log")[:, None]
    W["nw_row"] = g("L0_attn_norm_w")[None, :]
    W["bout_t"] = np.ascontiguousarray(g("L0_attn_bout").reshape(16, 128).T)
    W["ones_row"] = np.ones((1, 128), f32)
    W["ones_col"] = np.ones((128, 1), f32)
    W["ones8"] = np.ones((8, 128), f32)
    W["id8"] = np.eye(8, dtype=f32)
    W["id16"] = np.eye(16, dtype=f32)
    W["id32"] = np.eye(32, dtype=f32)
    W["id128"] = np.eye(128, dtype=f32)
    W["blk8"] = np.repeat(np.eye(8, dtype=f32), 128, axis=1)
    W["mask_su"] = np.triu(np.ones((128, 128), f32), 1)
    W["mask_sl"] = np.tril(np.ones((128, 128), f32), -1)
    W["mask_iu"] = np.triu(np.ones((128, 128), f32), 0)
    return W


def _prep_core(x, bidx, t0):
    f32 = np.float32
    xb = np.asarray(x[bidx], dtype=f32)
    xpre = np.zeros((NPT * PCOLS, HID), f32)
    premask = np.zeros((1, NPT * PCOLS), f32)
    start0 = t0 - PREF
    for pt in range(NPT):
        ts = start0 + pt * PTILE
        for j in range(PCOLS):
            tok = ts - 3 + (j - 1) + 1 if False else ts + (j - 4)
            if j == 0:
                continue
            tok = ts + (j - 4)
            if 0 <= tok < 4096:
                xpre[pt * PCOLS + j] = xb[tok]
                premask[0, pt * PCOLS + j] = 1.0
    xspan = np.zeros((NQ * QCOLS, HID), f32)
    for qq in range(NQ):
        ts = t0 + qq * QTOK
        for j in range(1, QCOLS):
            tok = ts + (j - 4)
            if 0 <= tok < 4096:
                xspan[qq * QCOLS + j] = xb[tok]
    tailm = np.full((128, 1), 0.0 if t0 == 0 else 1.0, f32)
    return {
        "xpre": np.ascontiguousarray(xpre.T),
        "premask": premask,
        "xspan": np.ascontiguousarray(xspan.T),
        "tailm": tailm,
    }


_NC_CACHE = {}


def get_nc(debug=False):
    if debug not in _NC_CACHE:
        _NC_CACHE[debug] = build_full(debug)
    return _NC_CACHE[debug]


def kernel(**inputs):
    x = np.asarray(inputs["L0_attn_hidden_states"], dtype=np.float32)
    W = _prep_weights(inputs)
    nc = get_nc(False)
    in_maps = []
    for c in range(NCORES):
        bidx, t0 = c // 4, (c % 4) * SPAN
        m = dict(W)
        m.update(_prep_core(x, bidx, t0))
        in_maps.append(m)
    res = run_bass_kernel_spmd(nc, in_maps, core_ids=list(range(NCORES)))
    outf = np.empty((2, 4096, HID), np.float32)
    for c in range(NCORES):
        bidx, t0 = c // 4, (c % 4) * SPAN
        outf[bidx, t0:t0 + SPAN] = res.results[c]["out"].T
    return outf
